# version 24
# baseline (speedup 1.0000x reference)
import numpy as np
import contextlib
import concourse.bass as bass
import concourse.mybir as mybir
from concourse.bass_utils import run_bass_kernel_spmd

F32 = mybir.dt.float32
BF16 = mybir.dt.bfloat16
I32 = mybir.dt.int32
U8 = mybir.dt.uint8
AF = mybir.ActivationFunctionType
ALU = mybir.AluOpType
AX = mybir.AxisListType.X

ENGINES = ("tensor", "scalar", "vector", "gpsimd", "sync")


class _Op:
    __slots__ = ("eng", "fn", "deps", "dma", "ord", "awaited", "idx")


class Sched:

    def __init__(self):
        self.ops = []
        self.state = {}

    def op(self, eng, fn, reads=(), writes=(), deps=(), dma=False):
        o = _Op()
        o.eng, o.fn, o.dma = eng, fn, dma
        o.awaited, o.ord = False, None
        o.idx = len(self.ops)
        dep = set()
        for d in deps:
            if d is not None:
                dep.add(d)
        for ks in reads:
            for k in ks:
                st = self.state.get(k)
                if st and st[0] is not None:
                    dep.add(st[0])
        for ks in writes:
            for k in ks:
                st = self.state.get(k)
                if st:
                    if st[0] is not None:
                        dep.add(st[0])
                    dep.update(st[1])
        o.deps = list(dep)
        self.ops.append(o)
        for ks in reads:
            for k in ks:
                self.state.setdefault(k, [None, []])[1].append(o)
        for ks in writes:
            for k in ks:
                self.state[k] = [o, []]
        return o

    def emit(self, block, csem, dsem, nring=8):
        for o in self.ops:
            for d in o.deps:
                d.awaited = True
        ccnt = {e: 0 for e in ENGINES}
        dcnt = {e: 0 for e in ENGINES}
        for o in self.ops:
            if o.dma:
                i = dcnt[o.eng]
                dcnt[o.eng] += 1
                o.ord = (i % nring, 16 * (i // nring + 1), i)
                o.awaited = True
            elif o.awaited:
                ccnt[o.eng] += 1
                o.ord = ccnt[o.eng]

        def mk(eng):
            my_ops = [o for o in self.ops if o.eng == eng]

            def body(E):
                waited = {}
                for o in my_ops:
                    best = {}
                    for d in o.deps:
                        if d.dma:
                            k = (d.eng, "d", d.ord[0])
                            if k not in best or best[k].idx < d.idx:
                                best[k] = d
                        else:
                            k = (d.eng, "c")
                            if k not in best or best[k].idx < d.idx:
                                best[k] = d
                    for k, d in best.items():
                        if d.dma:
                            sem, val = dsem[d.eng][d.ord[0]], d.ord[1]
                        else:
                            sem, val = csem[d.eng], d.ord
                        if waited.get(k, 0) < val:
                            E.wait_ge(sem, val)
                            waited[k] = val
                    if o.dma:
                        slot, val, i = o.ord
                        if i >= nring:
                            k = (eng, "d", slot)
                            if waited.get(k, 0) < val - 16:
                                E.wait_ge(dsem[eng][slot], val - 16)
                                waited[k] = val - 16
                    inst = o.fn(E)
                    if o.dma:
                        inst.then_inc(dsem[eng][o.ord[0]], 16)
                    elif o.awaited:
                        inst.then_inc(csem[eng], 1)
            return body

        for eng in ENGINES:
            getattr(block, eng)(mk(eng))


L = {}


def _lay(name, nbytes, align=64):
    off = _lay.cur
    off = (off + align - 1) // align * align
    L[name] = off
    _lay.cur = off + nbytes
    return off


_lay.cur = 0
_lay("ident_bf", 256)
_lay("ident_f", 512)
_lay("ones_f", 512)
_lay("ones_bf", 256)
_lay("bias_all", 128)
_lay("brow", 384)
_lay("negr_bf", 64)
_lay("maxsq", 64)
_lay("maxk", 64)
_lay("gs_sc", 256)
_lay("us_sc", 256)
_lay("ds_sc", 64)
_lay("wcst", 32)
_lay("scnat", 512)
_lay("mn_bf", 16)
_lay("qmask", 128)
_lay("YS", 4096)
_lay("S1", 4096)
_lay("S2", 4096)
_lay("S3", 4096)
_lay("S4", 4096)
_lay("rowbf", 2048)
_lay("QA", 2048)
_lay("QB", 2048)
_lay("P2", 32768)
_lay("P3", 16384 + 32768)
_lay("P4", 65536)
_lay("P5", 32768)
AR_BYTES = (_lay.cur + 63) // 64 * 64
assert AR_BYTES <= 212800, AR_BYTES

D_KEY_UNIT = 2048


def _keys(off, nbytes):
    return tuple(("A", u) for u in range(off // D_KEY_UNIT,
                                         (off + nbytes - 1) // D_KEY_UNIT + 1))


class TV:

    def __init__(self, ap, keys):
        self.ap = ap
        self.k = keys

    def __getitem__(self, sl):
        return self.ap[sl]


def build_nc(dump=None):
    nc = bass.Bass("TRN2", target_bir_lowering=False)

    XT = nc.dram_tensor("XT", [2048, 2048], F32, kind="ExternalInput").ap()
    WQC = nc.dram_tensor("WQC", [16, 128, 2048], F32, kind="ExternalInput").ap()
    WKC = nc.dram_tensor("WKC", [4, 128, 2048], F32, kind="ExternalInput").ap()
    WVC = nc.dram_tensor("WVC", [4, 128, 2048], F32, kind="ExternalInput").ap()
    MEM = nc.dram_tensor("MEM", [512, 512], F32, kind="ExternalInput").ap()
    MNORM = nc.dram_tensor("MNORM", [512, 1], F32, kind="ExternalInput").ap()
    MW = nc.dram_tensor("MW", [1, 1], F32, kind="ExternalInput").ap()
    GCB = nc.dram_tensor("GCB", [32, 128, 4096], U8, kind="ExternalInput").ap()
    UCB = nc.dram_tensor("UCB", [32, 128, 4096], U8, kind="ExternalInput").ap()
    DCB = nc.dram_tensor("DCB", [8, 8, 128, 2048], U8, kind="ExternalInput").ap()
    GS = nc.dram_tensor("GS", [64, 128], F32, kind="ExternalInput").ap()
    US = nc.dram_tensor("US", [64, 128], F32, kind="ExternalInput").ap()
    DS = nc.dram_tensor("DS", [16, 128], F32, kind="ExternalInput").ap()
    OUT = nc.dram_tensor("OUT", [2048, 1024], F32, kind="ExternalOutput").ap()
    DUMP = None
    if dump is not None:
        DUMP = nc.dram_tensor("DUMP", [128, AR_BYTES // 4], F32,
                              kind="ExternalOutput").ap()

    with contextlib.ExitStack() as ctx:
        AR = ctx.enter_context(nc.sbuf_tensor("AR", [128, AR_BYTES // 4], F32))
        ARB = AR.bitcast(BF16)
        AR_F = AR.ap()
        AR_B = ARB.ap()
        PS = [ctx.enter_context(nc.psum_tensor("PS%d" % i, [128, 1024], F32))
              for i in range(4)]
        PSB = [p.bitcast(BF16) for p in PS]
        PSK = [(("PS", i),) for i in range(4)]

        csem = {e: ctx.enter_context(nc.semaphore("c_" + e)) for e in ENGINES}
        dsem = {e: [ctx.enter_context(nc.semaphore("d_%s%d" % (e, i)))
                    for i in range(8)]
                for e in ("gpsimd", "sync", "scalar")}

        def vf(name, cols, off_extra=0):
            off = L[name] + off_extra
            return TV(AR_F[:, off // 4: off // 4 + cols], _keys(off, cols * 4))

        def vb(name, cols, off_extra=0):
            off = L[name] + off_extra
            return TV(AR_B[:, off // 2: off // 2 + cols], _keys(off, cols * 2))

        ident_bf = vb("ident_bf", 128)
        ident_f = vf("ident_f", 128)
        ones_f = vf("ones_f", 128)
        ones_bf = vb("ones_bf", 128)
        bias_all = vf("bias_all", 32)
        brow = vf("brow", 96)
        negr_bf = vb("negr_bf", 32)
        maxsq = vf("maxsq", 16)
        maxk = vf("maxk", 16)
        gs_sc = vf("gs_sc", 64)
        us_sc = vf("us_sc", 64)
        ds_sc = vf("ds_sc", 16)
        wcst = vf("wcst", 8)
        scnat = vf("scnat", 128)
        mn_bf = vb("mn_bf", 8)
        qmask = vb("qmask", 33)
        YS = vf("YS", 1024)
        y1b = vb("YS", 1024)
        sqb = vb("YS", 1024, off_extra=2048)
        S1 = vf("S1", 1024)
        S2 = vf("S2", 1024)
        S3 = vf("S3", 1024)
        S4 = vf("S4", 1024)
        KSQ = TV(AR_F[:, L["S1"] // 4: L["S1"] // 4 + 2048],
                 _keys(L["S1"], 8192))
        rowbf = vb("rowbf", 1024)

        QT = [vb("P2", 1024, off_extra=i * 2048) for i in range(16)]
        x1T = QT
        QAb = vb("QA", 1024)
        QBb = vb("QB", 1024)
        khat = [vb("P3", 2048, off_extra=kc * 4096) for kc in range(4)]
        vhat = [vb("P3", 2048, off_extra=16384 + h * 4096) for h in range(8)]
        xbT = [vb("P4", 2048, off_extra=hc * 4096) for hc in range(16)]
        y1 = [vf("P4", 1024, off_extra=hc * 4096) for hc in range(16)]
        zz = y1
        NWS = 3
        wslot_f = [vf("P5", 2048, off_extra=s * 8192) for s in range(NWS)]
        wslot = [vb("P5", 2048, off_extra=s * 8192) for s in range(NWS)]
        vt = [vb("S1", 2048), vb("S2", 2048)]
        mem_bf = vb("P5", 2048, off_extra=24576)
        expT = [vb("P5", 1024, off_extra=s * 2048) for s in range(8)]
        sig = [vb("P5", 1024, off_extra=16384 + s * 2048) for s in range(4)]
        hact = [vb("P5", 8192, off_extra=s * 16384) for s in range(2)]
        AR_I = AR.bitcast(I32).ap()

        def vi(name, cols, off_extra=0):
            off = L[name] + off_extra
            return TV(AR_I[:, off // 4: off // 4 + cols], _keys(off, cols * 4))

        AR_U = AR.bitcast(U8).ap()

        def vu(name, cols, off_extra=0):
            off = L[name] + off_extra
            return TV(AR_U[:, off: off + cols], _keys(off, cols))

        NGS = 3
        NDS = 3
        gstg_u = [vu("P3", 4096, off_extra=s * 8192 + 4096) for s in range(NGS)]
        gstg = [vb("P3", 4096, off_extra=s * 8192) for s in range(NGS)]
        ustg_u = [vu("P3", 4096, off_extra=24576 + s * 8192 + 4096) for s in range(NGS)]
        ustg = [vb("P3", 4096, off_extra=24576 + s * 8192) for s in range(NGS)]
        dstg_u = [vu(nm, 2048, off_extra=2048) for nm in ("S1", "S2", "YS")]
        dstg = [vb(nm, 2048) for nm in ("S1", "S2", "YS")]

        S = Sched()

        if dump is not None:
            S.op("gpsimd", lambda E: E.memset(AR_F[:, :], 0.0),
                 writes=[tuple(("A", u) for u in range(0, 120))])
        S.op("gpsimd", lambda E: E.memset(ones_f.ap, 1.0), writes=[ones_f.k])
        S.op("gpsimd", lambda E: E.memset(ones_bf.ap, 1.0), writes=[ones_bf.k])
        S.op("gpsimd", lambda E: E.memset(wcst[0:128, 4:5], -128.0),
             writes=[wcst.k])
        S.op("gpsimd", lambda E: E.memset(QAb[64:128, :], 0.0), writes=[QAb.k])
        S.op("gpsimd", lambda E: E.memset(QBb[0:64, :], 0.0), writes=[QBb.k])
        S.op("gpsimd", lambda E: E.memset(qmask.ap, 0.0), writes=[qmask.k])
        S.op("gpsimd", lambda E: E.memset(qmask[0:64, 0:1], 1.0),
             writes=[qmask.k])
        S.op("gpsimd", lambda E: E.memset(qmask[64:128, 32:33], 1.0),
             writes=[qmask.k])
        S.op("gpsimd", lambda E: E.affine_select(
            ident_bf.ap, ones_bf.ap, [[-1, 128]], ALU.is_equal, 0.0,
            base=0, channel_multiplier=1), reads=[ones_bf.k], writes=[ident_bf.k])
        S.op("gpsimd", lambda E: E.affine_select(
            ident_f.ap, ones_f.ap, [[-1, 128]], ALU.is_equal, 0.0,
            base=0, channel_multiplier=1), reads=[ones_f.k], writes=[ident_f.k])

        S.op("sync", lambda E: E.dma_start(out=wcst[0:1, 0:1], in_=MW[:, :]),
             writes=[wcst.k], dma=True)
        S.op("sync", lambda E: E.dma_start(out=wcst[64:65, 0:1], in_=MW[:, :]),
             writes=[wcst.k], dma=True)
        for b in (0, 64):
            S.op("scalar", lambda E, b=b: E.activation(
                wcst[b:b + 1, 1:2], wcst[b:b + 1, 0:1], AF.Sigmoid),
                reads=[wcst.k], writes=[wcst.k])
            S.op("vector", lambda E, b=b: E.tensor_scalar(
                wcst[b:b + 1, 2:3], wcst[b:b + 1, 1:2], -1.0, 1.0,
                ALU.mult, ALU.add), reads=[wcst.k], writes=[wcst.k])
            S.op("scalar", lambda E, b=b: E.activation(
                wcst[b:b + 1, 3:4], wcst[b:b + 1, 2:3], AF.Ln),
                reads=[wcst.k], writes=[wcst.k])

        for (nat_rows, DR, dst) in ((64, GS, gs_sc), (64, US, us_sc),
                                    (16, DS, ds_sc)):
            S.op("sync", lambda E, DR=DR, r=nat_rows: E.dma_start(
                out=scnat[0:r, 0:128], in_=DR[:, :]), writes=[scnat.k], dma=True)
            S.op("tensor", lambda E, r=nat_rows: E.transpose(
                PS[0].ap()[0:128, 0:r], scnat[0:r, 0:128], ident_f[0:r, 0:r]),
                reads=[scnat.k, ident_f.k], writes=[PSK[0]])
            S.op("vector", lambda E, dst=dst, r=nat_rows: E.tensor_copy(
                dst[0:128, 0:r], PS[0].ap()[0:128, 0:r]),
                reads=[PSK[0]], writes=[dst.k])

        S.op("gpsimd", lambda E: E.dma_start(
            out=mem_bf.ap, in_=MEM.rearrange("(c p) j -> p c j", p=128)),
            writes=[mem_bf.k], dma=True)
        S.op("sync", lambda E: E.dma_start(
            out=scnat[0:4, 0:128], in_=MNORM.rearrange("(c p) o -> c (p o)", p=128)),
            writes=[scnat.k], dma=True)
        S.op("tensor", lambda E: E.transpose(
            PS[0].ap()[0:128, 0:4], scnat[0:4, 0:128], ident_f[0:4, 0:4]),
            reads=[scnat.k, ident_f.k], writes=[PSK[0]])
        S.op("vector", lambda E: E.tensor_copy(mn_bf[0:128, 0:4],
                                               PS[0].ap()[0:128, 0:4]),
             reads=[PSK[0]], writes=[mn_bf.k])

        xstg = [TV(AR_F[:, L["S1"] // 4: L["S1"] // 4 + 2048],
                   _keys(L["S1"], 8192)),
                TV(AR_F[:, L["S3"] // 4: L["S3"] // 4 + 2048],
                   _keys(L["S3"], 8192)),
                TV(AR_F[:, L["P2"] // 4: L["P2"] // 4 + 2048],
                   _keys(L["P2"], 8192)),
                TV(AR_F[:, L["P2"] // 4 + 2048: L["P2"] // 4 + 4096],
                   _keys(L["P2"] + 8192, 8192))]
        _wl = [0]

        def wload_dma(DRC, idx):
            slot = _wl[0] % NWS
            _wl[0] += 1
            op = S.op("scalar", lambda E, slot=slot: E.dma_start(
                out=wslot_f[slot].ap, in_=DRC[idx]),
                writes=[wslot_f[slot].k], dma=True)
            return slot, op

        def wload_cast(slot):
            S.op("scalar", lambda E, slot=slot: E.activation(
                wslot[slot].ap, wslot_f[slot].ap, AF.Copy),
                 reads=[wslot_f[slot].k], writes=[wslot_f[slot].k])

        def wload(DRC, idx):
            slot, op = wload_dma(DRC, idx)
            wload_cast(slot)
            return slot, op

        wpref = [wload_dma(WKC, kc) for kc in range(2)]
        for hc in range(16):
            xsl = hc % 4
            ring = "sync" if xsl < 2 else "gpsimd"
            ceng = "scalar" if hc % 2 == 0 else "vector"
            S.op(ring, lambda E, hc=hc, xsl=xsl: E.dma_start(
                out=xstg[xsl].ap, in_=XT[hc * 128:(hc + 1) * 128, :]),
                writes=[xstg[xsl].k], dma=True)
            if ceng == "scalar":
                S.op("scalar", lambda E, hc=hc, xsl=xsl: E.activation(
                    xbT[hc].ap, xstg[xsl].ap, AF.Copy),
                     reads=[xstg[xsl].k], writes=[xbT[hc].k])
            else:
                S.op("vector", lambda E, hc=hc, xsl=xsl: E.tensor_copy(
                    xbT[hc].ap, xstg[xsl].ap),
                     reads=[xstg[xsl].k], writes=[xbT[hc].k])

        def proj_group(slot, ps_lo, ps_hi, tq_n, psk_writes):
            for hc in range(16):
                def fn(E, hc=hc):
                    inst = None
                    for tq in range(tq_n):
                        p = PS[ps_lo] if tq < 2 else PS[ps_hi]
                        col = (tq % 2) * 512
                        inst = E.matmul(
                            p.ap()[:, col:col + 512],
                            wslot[slot][:, hc * 128:(hc + 1) * 128],
                            xbT[hc][:, tq * 512:(tq + 1) * 512],
                            start=(hc == 0), stop=(hc == 15))
                    return inst
                S.op("tensor", fn, reads=[wslot[slot].k, xbT[hc].k],
                     writes=psk_writes)

        for kc in range(4):
            if kc < 2:
                sl, _ = wpref[kc]
                wload_cast(sl)
            else:
                sl, _ = wload(WKC, kc)
            pl = 2 * (kc % 2)
            proj_group(sl, pl, pl + 1, 4, [PSK[pl], PSK[pl + 1]])
            for half in range(2):
                S.op("vector", lambda E, kc=kc, half=half, pl=pl: E.tensor_copy(
                    khat[kc][:, half * 1024:(half + 1) * 1024],
                    PS[pl + half].ap()[:, :]),
                    reads=[PSK[pl + half]], writes=[khat[kc].k])
        for kc in range(4):
            sl, _ = wload(WVC, kc)
            vsl = kc % 2
            pl = 2 * (kc % 2)
            proj_group(sl, pl, pl + 1, 4, [PSK[pl], PSK[pl + 1]])
            for half in range(2):
                S.op("vector", lambda E, half=half, vsl=vsl, pl=pl: E.tensor_copy(
                    vt[vsl][:, half * 1024:(half + 1) * 1024],
                    PS[pl + half].ap()[:, :]),
                    reads=[PSK[pl + half]], writes=[vt[vsl].k])
            for hsub in range(2):
                h = 2 * kc + hsub
                b = hsub * 64
                pt = 2 + (kc % 2)
                def trf(E, b=b, sl=vsl, pt=pt):
                    inst = None
                    for tch in range(16):
                        inst = E.transpose(
                            PSB[pt].ap()[0:128, tch * 64:(tch + 1) * 64],
                            vt[sl][b:b + 64, tch * 128:(tch + 1) * 128],
                            ident_bf[b:b + 64, b:b + 64])
                    return inst
                S.op("tensor", trf, reads=[vt[vsl].k, ident_bf.k],
                     writes=[PSK[pt]])
                S.op("vector", lambda E, h=h, pt=pt: E.tensor_copy(
                    vhat[h].ap.rearrange("p (c w) -> p c w", w=128)[:, :, 0:64],
                    PSB[pt].ap()[0:128, 0:1024].rearrange("p (c w) -> p c w", w=64)),
                    reads=[PSK[pt]], writes=[vhat[h].k])
                S.op("gpsimd", lambda E, h=h: E.memset(
                    vhat[h].ap.rearrange("p (c w) -> p c w", w=128)[:, :, 64:65],
                    1.0), writes=[vhat[h].k])
                S.op("gpsimd", lambda E, h=h: E.memset(
                    vhat[h].ap.rearrange("p (c w) -> p c w", w=128)[:, :, 65:128],
                    0.0), writes=[vhat[h].k])
        sqA = vb("S3", 1024)
        sqB = vb("YS", 1024)
        KSQB = vb("S1", 2048)

        def qbias(oc):
            sq = sqA if oc % 2 == 0 else sqB
            pb = 1 + 2 * (oc % 2)
            S.op("vector", lambda E, oc=oc, sq=sq: E.tensor_tensor(
                sq.ap, QT[oc].ap, QT[oc].ap, ALU.mult),
                reads=[QT[oc].k], writes=[sq.k])
            def qrowfn(E, pb=pb, sq=sq):
                inst = None
                for th in range(2):
                    inst = E.matmul(
                        PS[pb].ap()[0:33, th * 512:(th + 1) * 512],
                        qmask[0:128, 0:33],
                        sq[0:128, th * 512:(th + 1) * 512],
                        start=True, stop=True)
                return inst
            S.op("tensor", qrowfn, reads=[qmask.k, sq.k], writes=[PSK[pb]])
            S.op("vector", lambda E, oc=oc, pb=pb: E.reduce_max(
                brow[0:33, oc:oc + 1], PS[pb].ap()[0:33, 0:1024], axis=AX),
                reads=[PSK[pb]], writes=[brow.k])

        def kbias(kc):
            S.op("vector", lambda E, kc=kc: E.tensor_tensor(
                KSQB.ap, khat[kc].ap, khat[kc].ap, ALU.mult),
                reads=[khat[kc].k], writes=[KSQB.k])
            for tha in range(2):
                pb = 1 + 2 * tha
                def krowfn(E, pb=pb, tha=tha):
                    inst = None
                    for tq in range(2):
                        inst = E.matmul(
                            PS[pb].ap()[0:33, tq * 512:(tq + 1) * 512],
                            qmask[0:128, 0:33],
                            KSQB.ap[0:128, (2 * tha + tq) * 512:
                                    (2 * tha + tq + 1) * 512],
                            start=True, stop=True)
                    return inst
                S.op("tensor", krowfn, reads=[qmask.k, KSQB.k],
                     writes=[PSK[pb]])
                S.op("vector", lambda E, kc=kc, pb=pb, tha=tha: E.reduce_max(
                    brow[0:33, 16 + 4 * tha + kc:17 + 4 * tha + kc],
                    PS[pb].ap()[0:33, 0:1024], axis=AX),
                    reads=[PSK[pb]], writes=[brow.k])
            S.op("vector", lambda E, kc=kc: E.tensor_tensor(
                brow[0:33, 24 + kc:25 + kc], brow[0:33, 16 + kc:17 + kc],
                brow[0:33, 20 + kc:21 + kc], ALU.max),
                reads=[brow.k], writes=[brow.k])

        for oc in range(16):
            sl, _ = wload(WQC, oc)
            pp = 2 * (oc % 2)
            proj_group(sl, pp, pp + 1, 2, [PSK[pp]])
            S.op("vector", lambda E, oc=oc, pp=pp: E.tensor_copy(
                QT[oc].ap, PS[pp].ap()[:, :]),
                reads=[PSK[pp]], writes=[QT[oc].k])
            qbias(oc)
            if oc < 4:
                kbias(oc)

        for hc in range(16):
            S.op("sync", lambda E, hc=hc: E.dma_start(
                out=y1[hc].ap, in_=XT[hc * 128:(hc + 1) * 128, 0:1024]),
                writes=[y1[hc].k], dma=True)
        def prodfn(E):
            inst = None
            for oc in range(16):
                kc = oc % 4
                inst = E.tensor_tensor(brow[0:33, 28 + oc:29 + oc],
                                       brow[0:33, oc:oc + 1],
                                       brow[0:33, 24 + kc:25 + kc], ALU.mult)
            return inst
        S.op("vector", prodfn, reads=[brow.k], writes=[brow.k])
        S.op("vector", lambda E: E.memset(negr_bf[0:33, 0:32], 0.0),
             writes=[negr_bf.k])
        for par, sub in ((0, 0), (64, 1)):
            pr = 32 if par else 0
            S.op("scalar", lambda E, pr=pr: E.activation(
                S4[pr:pr + 1, 0:16], brow[pr:pr + 1, 28:44], AF.Ln),
                 reads=[brow.k], writes=[S4.k])
            S.op("scalar", lambda E, pr=pr: E.activation(
                S4[pr:pr + 1, 16:32], S4[pr:pr + 1, 0:16], AF.Exp, scale=0.5),
                 reads=[S4.k], writes=[S4.k])
            S.op("vector", lambda E, pr=pr: E.tensor_scalar(
                S4[pr:pr + 1, 32:48], S4[pr:pr + 1, 16:32], -0.125, None,
                ALU.mult), reads=[S4.k], writes=[S4.k])
            S.op("vector", lambda E, pr=pr, sub=sub: E.tensor_copy(
                negr_bf.ap.rearrange("p (a b) -> p a b", b=2)[pr:pr + 1, :, sub:sub + 1],
                S4[pr:pr + 1, 32:48]),
                 reads=[S4.k], writes=[negr_bf.k])
        def bcastfn(E):
            E.matmul(PS[3].ap()[0:128, 0:32], ones_bf[0:1, 0:128],
                     negr_bf[0:1, 0:32], start=True, stop=False)
            return E.matmul(PS[3].ap()[0:128, 0:32], ones_bf[32:33, 0:128],
                            negr_bf[32:33, 0:32], start=False, stop=True)
        S.op("tensor", bcastfn, reads=[ones_bf.k, negr_bf.k], writes=[PSK[3]])
        S.op("vector", lambda E: E.tensor_copy(bias_all.ap,
                                               PS[3].ap()[0:128, 0:32]),
             reads=[PSK[3]], writes=[bias_all.k])

        if dump == "qkv":
            _emit_dump(S, AR_F, DUMP)

        for g in range(4):
            for c in range(4):
                oc = 4 * g + c
                sA, sB = (S1, S2) if c % 2 == 0 else (S3, YS)
                S.op("vector", lambda E, oc=oc, sA=sA: E.tensor_scalar_min(
                    sA.ap, QT[oc].ap, 0.0), reads=[QT[oc].k], writes=[sA.k])
                S.op("scalar", lambda E, sA=sA, sB=sB: E.activation(
                    sB.ap, sA.ap, AF.Exp), reads=[sA.k], writes=[sB.k])
                S.op("vector", lambda E, oc=oc, sA=sA: E.tensor_scalar_max(
                    sA.ap, QT[oc].ap, 0.0), reads=[QT[oc].k], writes=[sA.k])
                S.op("vector", lambda E, c=c, sA=sA, sB=sB: E.scalar_tensor_tensor(
                    sig[c].ap, sB.ap, -1.0, sA.ap, ALU.add, ALU.add),
                     reads=[sA.k, sB.k], writes=[sig[c].k])
            def nrmfn(E):
                inst = None
                for th in range(2):
                    for c in range(4):
                        inst = E.matmul(
                            PS[3].ap()[0:1, th * 512:(th + 1) * 512],
                            mn_bf[0:128, c:c + 1],
                            sig[c][:, th * 512:(th + 1) * 512],
                            start=(c == 0), stop=(c == 3))
                return inst
            S.op("tensor", nrmfn, reads=[mn_bf.k] + [t.k for t in sig],
                 writes=[PSK[3]])
            S.op("vector", lambda E: E.reciprocal(S4[0:1, 0:1024],
                                                  PS[3].ap()[0:1, :]),
                 reads=[PSK[3]], writes=[S4.k])
            S.op("vector", lambda E: E.tensor_scalar(
                rowbf[0:1, 0:1024], S4[0:1, 0:1024], wcst[0:1, 1:2], None,
                ALU.mult), reads=[S4.k, wcst.k], writes=[rowbf.k])
            def psi2fn(E):
                inst = None
                for th in range(2):
                    inst = E.matmul(PS[3].ap()[0:128, th * 512:(th + 1) * 512],
                                    ones_bf[0:1, 0:128],
                                    rowbf[0:1, th * 512:(th + 1) * 512],
                                    start=True, stop=True)
                return inst
            S.op("tensor", psi2fn, reads=[ones_bf.k, rowbf.k], writes=[PSK[3]])
            S.op("vector", lambda E: E.tensor_copy(S3.ap, PS[3].ap()[:, :]),
                 reads=[PSK[3]], writes=[S3.k])
            for jc in range(4):
                pj = jc % 2
                def amfn(E, pj=pj, jc=jc):
                    inst = None
                    for th in range(2):
                        for c in range(4):
                            inst = E.matmul(
                                PS[pj].ap()[:, th * 512:(th + 1) * 512],
                                mem_bf[:, c * 512 + jc * 128: c * 512 + (jc + 1) * 128],
                                sig[c][:, th * 512:(th + 1) * 512],
                                start=(c == 0), stop=(c == 3))
                    return inst
                S.op("tensor", amfn, reads=[mem_bf.k] + [t.k for t in sig],
                     writes=[PSK[pj]])
                tmp = S1 if jc % 2 == 0 else S2
                S.op("vector", lambda E, pj=pj, tmp=tmp: E.tensor_tensor(
                    tmp.ap, PS[pj].ap()[:, :], S3.ap, ALU.mult),
                     reads=[PSK[pj], S3.k], writes=[tmp.k])
                hc = 4 * g + jc
                S.op("vector", lambda E, hc=hc, tmp=tmp: E.tensor_tensor(
                    y1[hc].ap, y1[hc].ap, tmp.ap, ALU.add),
                     reads=[tmp.k, y1[hc].k], writes=[y1[hc].k])

        if dump == "amem":
            _emit_dump(S, AR_F, DUMP)

        def emit_qk(p, tch):
            q, h = p // 8, p % 8
            oc, b, kc = q * 4 + h // 2, (h % 2) * 64, h // 2
            G = p * 16 + tch
            pc = 2 * (G % 2)
            sl = G % 8
            qh = QAb if b == 0 else QBb
            def qkfn(E, kc=kc, tch=tch, pc=pc, qh=qh):
                inst = None
                for sh in range(2):
                    inst = E.matmul(
                        PS[pc].ap()[:, sh * 512:(sh + 1) * 512],
                        khat[kc][0:128, tch * 128:(tch + 1) * 128],
                        qh[0:128, sh * 512:(sh + 1) * 512],
                        start=True, stop=True)
                return inst
            S.op("tensor", qkfn, reads=[khat[kc].k, qh.k], writes=[PSK[pc]])
            S.op("scalar", lambda E, sl=sl, p=p, pc=pc: E.activation(
                expT[sl].ap, PS[pc].ap()[:, :], AF.Exp,
                scale=0.125, bias=bias_all[:, p:p + 1]),
                 reads=[PSK[pc], bias_all.k], writes=[expT[sl].k])

        def emit_pv(p, tch):
            h = p % 8
            pa = 1 + 2 * (p % 2)
            sl = (p * 16 + tch) % 8
            def pvfn(E, h=h, tch=tch, sl=sl, pa=pa):
                inst = None
                for sh in range(2):
                    inst = E.matmul(
                        PS[pa].ap()[:, sh * 512:(sh + 1) * 512],
                        vhat[h][:, tch * 128: tch * 128 + 128],
                        expT[sl][:, sh * 512:(sh + 1) * 512],
                        start=(tch == 0), stop=(tch == 15))
                return inst
            S.op("tensor", pvfn, reads=[vhat[h].k, expT[sl].k],
                 writes=[PSK[pa]])

        def emit_tail(p, step):
            q, h = p // 8, p % 8
            oc, b = q * 4 + h // 2, (h % 2) * 64
            pa = 1 + 2 * (p % 2)
            if step == 0:
                S.op("scalar", lambda E, pa=pa: E.activation(
                    S4[64:65, 0:1024], PS[pa].ap()[64:65, :], AF.Ln),
                     reads=[PSK[pa]], writes=[S4.k])
                S.op("scalar", lambda E: E.activation(
                    rowbf[64:65, 0:1024], S4[64:65, 0:1024], AF.Exp,
                    scale=-1.0, bias=wcst[64:65, 3:4]),
                     reads=[S4.k, wcst.k], writes=[rowbf.k])
            elif step == 1:
                def psifn(E, pa=pa):
                    inst = None
                    for sh in range(2):
                        inst = E.matmul(
                            PS[pa].ap()[64:128, sh * 512:(sh + 1) * 512],
                            ones_bf[64:65, 0:64],
                            rowbf[64:65, sh * 512:(sh + 1) * 512],
                            start=True, stop=True, tile_position=(64, 64))
                    return inst
                S.op("tensor", psifn, reads=[ones_bf.k, rowbf.k],
                     writes=[PSK[pa]])
            elif step == 2:
                S.op("vector", lambda E, pa=pa: E.tensor_copy(
                    S2[0:64, 0:1024], PS[pa].ap()[64:128, :]),
                     reads=[PSK[pa]], writes=[S2.k])
            elif step == 3:
                if b == 0:
                    S.op("vector", lambda E, pa=pa: E.tensor_tensor(
                        S1[0:64, 0:1024], PS[pa].ap()[0:64, :],
                        S2[0:64, 0:1024], ALU.mult),
                         reads=[PSK[pa], S2.k], writes=[S1.k])
                else:
                    S.op("vector", lambda E, pa=pa: E.tensor_tensor(
                        rowbf[0:64, 0:1024], PS[pa].ap()[0:64, :],
                        S2[0:64, 0:1024], ALU.mult),
                         reads=[PSK[pa], S2.k], writes=[rowbf.k])
                    def shfn(E, pa=pa):
                        inst = None
                        for sh in range(2):
                            inst = E.matmul(
                                PS[pa].ap()[64:128, sh * 512:(sh + 1) * 512],
                                ident_bf[0:64, 0:64],
                                rowbf[0:64, sh * 512:(sh + 1) * 512],
                                start=True, stop=True, tile_position=(0, 64))
                        return inst
                    S.op("tensor", shfn, reads=[ident_bf.k, rowbf.k],
                         writes=[PSK[pa]])
            elif step == 4:
                if b == 0:
                    S.op("vector", lambda E, oc=oc: E.tensor_tensor(
                        y1[oc][0:64, :], y1[oc][0:64, :], S1[0:64, 0:1024],
                        ALU.add), reads=[S1.k, y1[oc].k], writes=[y1[oc].k])
                else:
                    S.op("vector", lambda E, oc=oc, pa=pa: E.tensor_tensor(
                        y1[oc][64:128, :], y1[oc][64:128, :],
                        PS[pa].ap()[64:128, :], ALU.add),
                         reads=[PSK[pa], y1[oc].k], writes=[y1[oc].k])

        PVLAG = 3
        TAIL_SLOTS = {5: 0, 7: 1, 9: 2, 11: 3, 13: 4}
        for p in range(32):
            if p % 2 == 0:
                oc0 = (p // 8) * 4 + (p % 8) // 2
                S.op("vector", lambda E, oc0=oc0: E.tensor_copy(
                    QAb[0:64, :], QT[oc0][0:64, :]),
                     reads=[QT[oc0].k], writes=[QAb.k])
                S.op("vector", lambda E, oc0=oc0: E.tensor_copy(
                    QBb[64:128, :], QT[oc0][64:128, :]),
                     reads=[QT[oc0].k], writes=[QBb.k])
            for tch in range(16):
                emit_qk(p, tch)
                G = p * 16 + tch
                if G >= PVLAG:
                    emit_pv((G - PVLAG) // 16, (G - PVLAG) % 16)
                if p >= 1 and tch in TAIL_SLOTS:
                    emit_tail(p - 1, TAIL_SLOTS[tch])
        for G in range(512 - PVLAG, 512):
            emit_pv(G // 16, G % 16)
        for step in range(5):
            emit_tail(31, step)

        if dump == "attn":
            _emit_dump(S, AR_F, DUMP)

        y1b2 = vb("S3", 1024)
        sqb2 = vb("S3", 1024, off_extra=2048)

        def layernorm(src, dst_fn, tag):
            for hc in range(16):
                yb = y1b if hc % 2 == 0 else y1b2
                sb_ = sqb if hc % 2 == 0 else sqb2
                S.op("vector", lambda E, hc=hc, yb=yb: E.tensor_copy(
                    yb.ap, src[hc].ap), reads=[src[hc].k], writes=[yb.k])
                S.op("vector", lambda E, yb=yb, sb_=sb_: E.tensor_tensor(
                    sb_.ap, yb.ap, yb.ap, ALU.mult),
                     reads=[yb.k], writes=[sb_.k])
                def s12fn(E, hc=hc, yb=yb, sb_=sb_):
                    inst = None
                    for th in range(2):
                        inst = E.matmul(
                            PS[0].ap()[0:1, th * 512:(th + 1) * 512],
                            ones_bf[0:128, 0:1],
                            yb[0:128, th * 512:(th + 1) * 512],
                            start=(hc == 0), stop=(hc == 15))
                        inst = E.matmul(
                            PS[3].ap()[0:1, th * 512:(th + 1) * 512],
                            ones_bf[0:128, 0:1],
                            sb_[0:128, th * 512:(th + 1) * 512],
                            start=(hc == 0), stop=(hc == 15))
                    return inst
                S.op("tensor", s12fn, reads=[yb.k, sb_.k, ones_bf.k],
                     writes=[PSK[0], PSK[3]])
            S.op("vector", lambda E: E.tensor_scalar(
                S4[0:1, 0:1024], PS[0].ap()[0:1, :], 1.0 / 2048.0, None,
                ALU.mult), reads=[PSK[0]], writes=[S4.k])
            S.op("vector", lambda E: E.tensor_tensor(
                S2[0:1, 0:1024], S4[0:1, 0:1024], S4[0:1, 0:1024], ALU.mult),
                 reads=[S4.k], writes=[S2.k])
            S.op("vector", lambda E: E.scalar_tensor_tensor(
                S3[0:1, 0:1024], PS[3].ap()[0:1, :], 1.0 / 2048.0,
                S2[0:1, 0:1024], ALU.mult, ALU.subtract),
                 reads=[PSK[3], S2.k], writes=[S3.k])
            S.op("vector", lambda E: E.tensor_scalar(
                S3[0:1, 0:1024], S3[0:1, 0:1024], 1e-5, None, ALU.add),
                 reads=[S3.k], writes=[S3.k])
            S.op("scalar", lambda E: E.activation(
                S2[0:1, 0:1024], S3[0:1, 0:1024], AF.Ln),
                 reads=[S3.k], writes=[S2.k])
            S.op("scalar", lambda E: E.activation(
                S3[0:1, 0:1024], S2[0:1, 0:1024], AF.Exp, scale=-0.5),
                 reads=[S2.k], writes=[S3.k])
            S.op("vector", lambda E: E.scalar_tensor_tensor(
                S4[0:1, 0:1024], S4[0:1, 0:1024], -1.0, S3[0:1, 0:1024],
                ALU.mult, ALU.mult), reads=[S3.k, S4.k], writes=[S4.k])
            def bcast(row, psi):
                def fn(E):
                    inst = None
                    for th in range(2):
                        inst = E.matmul(
                            PS[psi].ap()[0:128, th * 512:(th + 1) * 512],
                            ones_f[0:1, 0:128],
                            row[0:1, th * 512:(th + 1) * 512],
                            start=True, stop=True)
                    return inst
                return fn
            S.op("tensor", bcast(S3.ap, 1), reads=[ones_f.k, S3.k],
                 writes=[PSK[1]])
            S.op("tensor", bcast(S4.ap, 2), reads=[ones_f.k, S4.k],
                 writes=[PSK[2]])
            S.op("vector", lambda E: E.tensor_copy(S1.ap, PS[1].ap()[:, :]),
                 reads=[PSK[1]], writes=[S1.k])
            S.op("vector", lambda E: E.tensor_copy(S2.ap, PS[2].ap()[:, :]),
                 reads=[PSK[2]], writes=[S2.k])
            for hc in range(16):
                dst_fn(hc)

        QAf = TV(AR_F[:, L["QA"] // 4: L["QA"] // 4 + 1024], _keys(L["QA"], 4096))

        def ln1_dst(hc):
            tmp = S3 if hc % 2 == 0 else QAf
            S.op("vector", lambda E, hc=hc, tmp=tmp: E.tensor_tensor(
                tmp.ap, y1[hc].ap, S1.ap, ALU.mult),
                 reads=[y1[hc].k, S1.k], writes=[tmp.k])
            S.op("vector", lambda E, hc=hc, tmp=tmp: E.tensor_tensor(
                x1T[hc].ap, tmp.ap, S2.ap, ALU.add),
                 reads=[tmp.k, S2.k], writes=[x1T[hc].k])
        layernorm(y1, ln1_dst, "ln1")

        if dump == "ln1":
            _emit_dump(S, AR_F, DUMP)

        def stage(DRC, idx, slots_u, slots_b, nsl, ring, ceng="scalar"):
            sl = idx % nsl
            S.op(ring, lambda E, sl=sl, DRC=DRC: E.dma_start(
                out=slots_u[sl].ap, in_=DRC), writes=[slots_b[sl].k], dma=True)
            if ceng == "scalar":
                S.op("scalar", lambda E, sl=sl: E.activation(
                    slots_b[sl].ap, slots_u[sl].ap, AF.Copy, bias=-128.0),
                     reads=[slots_b[sl].k], writes=[slots_b[sl].k])
            else:
                S.op(ceng, lambda E, sl=sl: E.tensor_scalar(
                    slots_b[sl].ap, slots_u[sl].ap, -128.0, None, ALU.add),
                     reads=[slots_b[sl].k], writes=[slots_b[sl].k])
            return sl

        def down_block(g, cb):
            hs = g % 2
            dsl = stage(DCB[cb, g], 8 * g + cb, dstg_u, dstg, NDS, "gpsimd")
            for sub in range(2):
                co = 2 * cb + sub
                pd = 2 + (co % 2)
                def dfn(E, dsl=dsl, hs=hs, pd=pd, sub=sub):
                    inst = None
                    for th in range(2):
                        for il in range(8):
                            inst = E.matmul(
                                PS[pd].ap()[:, th * 512:(th + 1) * 512],
                                dstg[dsl][:, il * 256 + sub * 128:
                                          il * 256 + sub * 128 + 128],
                                hact[hs][:, il * 1024 + th * 512:
                                         il * 1024 + (th + 1) * 512],
                                start=(il == 0), stop=(il == 7))
                    return inst
                S.op("tensor", dfn, reads=[dstg[dsl].k, hact[hs].k],
                     writes=[PSK[pd]])
                if g == 0:
                    S.op("vector", lambda E, co=co, pd=pd: E.tensor_scalar(
                        zz[co].ap, PS[pd].ap()[:, :], ds_sc[:, co:co + 1],
                        None, ALU.mult),
                         reads=[PSK[pd], ds_sc.k], writes=[zz[co].k])
                else:
                    S.op("vector", lambda E, co=co, pd=pd: E.scalar_tensor_tensor(
                        zz[co].ap, PS[pd].ap()[:, :], ds_sc[:, co:co + 1],
                        zz[co].ap, ALU.mult, ALU.add),
                         reads=[PSK[pd], ds_sc.k, zz[co].k], writes=[zz[co].k])

        for g in range(8):
            hs = g % 2
            for ib in range(4):
                ssl = stage(GCB[4 * g + ib], 4 * g + ib, gstg_u, gstg, NGS,
                            "sync")
                usl = stage(UCB[4 * g + ib], 4 * g + ib, ustg_u, ustg, NGS,
                            "scalar")
                for sub in range(2):
                    il = 2 * ib + sub
                    ic = 8 * g + il
                    def gufn(E, ssl=ssl, usl=usl, sub=sub, w=0):
                        stg = gstg[ssl] if w == 0 else ustg[usl]
                        pi = 0 if w == 0 else 1
                        inst = None
                        for hc in range(16):
                            for th in range(2):
                                inst = E.matmul(
                                    PS[pi].ap()[:, th * 512:(th + 1) * 512],
                                    stg[:, hc * 256 + sub * 128:
                                        hc * 256 + sub * 128 + 128],
                                    x1T[hc][:, th * 512:(th + 1) * 512],
                                    start=(hc == 0), stop=(hc == 15))
                        return inst
                    S.op("tensor", lambda E, ssl=ssl, usl=usl, sub=sub: gufn(
                        E, ssl, usl, sub, 0),
                         reads=[gstg[ssl].k] + [x.k for x in x1T],
                         writes=[PSK[0]])
                    S.op("tensor", lambda E, ssl=ssl, usl=usl, sub=sub: gufn(
                        E, ssl, usl, sub, 1),
                         reads=[ustg[usl].k] + [x.k for x in x1T],
                         writes=[PSK[1]])
                    S.op("scalar", lambda E, ic=ic: E.activation(
                        S3.ap, PS[0].ap()[:, :], AF.Silu,
                        scale=gs_sc[:, ic:ic + 1]),
                         reads=[PSK[0], gs_sc.k], writes=[S3.k])
                    S.op("vector", lambda E, ic=ic, hs=hs, il=il: E.scalar_tensor_tensor(
                        hact[hs][:, il * 1024:(il + 1) * 1024],
                        PS[1].ap()[:, :], us_sc[:, ic:ic + 1], S3.ap,
                        ALU.mult, ALU.mult),
                         reads=[PSK[1], us_sc.k, S3.k], writes=[hact[hs].k])
                if g > 0:
                    down_block(g - 1, 2 * ib)
                    down_block(g - 1, 2 * ib + 1)
        for cb in range(8):
            down_block(7, cb)

        if dump == "ffn":
            _emit_dump(S, AR_F, DUMP)

        st_ops = []

        def ln2_dst(hc):
            outv = YS if hc % 2 == 0 else S4
            tmp = S3 if hc % 2 == 0 else QAf
            S.op("vector", lambda E, hc=hc, tmp=tmp: E.tensor_tensor(
                tmp.ap, zz[hc].ap, S1.ap, ALU.mult),
                 reads=[zz[hc].k, S1.k], writes=[tmp.k])
            S.op("vector", lambda E, hc=hc, outv=outv, tmp=tmp: E.tensor_tensor(
                outv.ap, tmp.ap, S2.ap, ALU.add),
                 reads=[tmp.k, S2.k], writes=[outv.k])
            st_ops.append(S.op("sync", lambda E, hc=hc, outv=outv: E.dma_start(
                out=OUT[hc * 128:(hc + 1) * 128, :], in_=outv.ap),
                reads=[outv.k], dma=True))
        layernorm(zz, ln2_dst, "ln2")

        S.op("sync", lambda E: E.nop(), deps=st_ops)

        block = ctx.enter_context(nc.Block())
        S.emit(block, csem, dsem)
    return nc


def _emit_dump(S, AR_F, DUMP):
    for hc in range(16):
        w = AR_F.shape[1] // 16 if hc < 15 else AR_F.shape[1] - 15 * (AR_F.shape[1] // 16)
        c0 = hc * (AR_F.shape[1] // 16)
        S.op("sync", lambda E, c0=c0, w=w: E.dma_start(
            out=DUMP[:, c0:c0 + w], in_=AR_F[:, c0:c0 + w]),
            reads=[tuple(("A", u) for u in range(0, 120))], dma=True)


def _prep(inp):
    x = np.asarray(inp["L0_attn_x"], np.float32)
    Wq = np.asarray(inp["L0_attn_Wq"], np.float32)
    Wk = np.asarray(inp["L0_attn_Wk"], np.float32)
    Wv = np.asarray(inp["L0_attn_Wv"], np.float32)
    mem = np.ascontiguousarray(np.asarray(inp["L0_attn_memory"], np.float32))
    mnorm = np.ascontiguousarray(
        np.asarray(inp["L0_attn_memory_norm"], np.float32).reshape(512, 1))
    mw = np.asarray(inp["L0_attn_memory_weight"], np.float32).reshape(1, 1)
    gw = np.asarray(inp["L0_ffn_gate_w"], np.int32)
    uw = np.asarray(inp["L0_ffn_up_w"], np.int32)
    dw = np.asarray(inp["L0_ffn_down_w"], np.int32)
    gw8 = np.ascontiguousarray(gw.view(np.uint8).reshape(8192, 2048, 4)[:, :, 0])
    uw8 = np.ascontiguousarray(uw.view(np.uint8).reshape(8192, 2048, 4)[:, :, 0])
    dw8 = np.ascontiguousarray(dw.view(np.uint8).reshape(2048, 8192, 4)[:, :, 0])
    gs = np.asarray(inp["L0_ffn_gate_scale"], np.float32)
    us = np.asarray(inp["L0_ffn_up_scale"], np.float32)
    ds = np.asarray(inp["L0_ffn_down_scale"], np.float32)

    common = dict(
        WQC=np.ascontiguousarray(
            Wq.reshape(16, 128, 16, 128).transpose(0, 3, 2, 1)
            .reshape(16, 128, 2048)),
        WKC=np.ascontiguousarray(
            Wk.reshape(4, 128, 16, 128).transpose(0, 3, 2, 1)
            .reshape(4, 128, 2048)),
        WVC=np.ascontiguousarray(
            Wv.reshape(4, 128, 16, 128).transpose(0, 3, 2, 1)
            .reshape(4, 128, 2048)),
        MEM=mem, MNORM=mnorm, MW=np.ascontiguousarray(mw),
        GCB=np.ascontiguousarray(
            gw8.reshape(32, 256, 16, 128).transpose(0, 3, 2, 1)
            .reshape(32, 128, 4096)),
        UCB=np.ascontiguousarray(
            uw8.reshape(32, 256, 16, 128).transpose(0, 3, 2, 1)
            .reshape(32, 128, 4096)),
        DCB=np.ascontiguousarray(
            dw8.reshape(8, 256, 8, 8, 128).transpose(0, 2, 4, 3, 1)
            .reshape(8, 8, 128, 2048)),
        GS=np.ascontiguousarray(gs.reshape(64, 128)),
        US=np.ascontiguousarray(us.reshape(64, 128)),
        DS=np.ascontiguousarray(ds.reshape(16, 128)),
    )
    in_maps = []
    xTs = {}
    for c in range(8):
        beta, half = c // 2, c % 2
        if beta not in xTs:
            xTs[beta] = np.ascontiguousarray(x[beta].T)
        xT = xTs[beta]
        if half == 0:
            XT = xT
        else:
            XT = np.ascontiguousarray(
                np.concatenate([xT[:, 1024:], xT[:, :1024]], axis=1))
        m = dict(common)
        m["XT"] = XT
        in_maps.append(m)
    return in_maps


def kernel(**inputs):
    in_maps = _prep(inputs)
    nc = build_nc()
    res = run_bass_kernel_spmd(nc, in_maps, core_ids=list(range(8)))
    out = np.empty((4, 2048, 2048), np.float32)
    for c in range(8):
        beta, half = c // 2, c % 2
        out[beta, half * 1024:(half + 1) * 1024, :] = res.results[c]["OUT"].T
    return out


# version 25
# speedup vs baseline: 1.0070x; 1.0070x over previous
import numpy as np
import contextlib
import concourse.bass as bass
import concourse.mybir as mybir
from concourse.bass_utils import run_bass_kernel_spmd

F32 = mybir.dt.float32
BF16 = mybir.dt.bfloat16
I32 = mybir.dt.int32
U8 = mybir.dt.uint8
AF = mybir.ActivationFunctionType
ALU = mybir.AluOpType
AX = mybir.AxisListType.X

ENGINES = ("tensor", "scalar", "vector", "gpsimd", "sync")


class _Op:
    __slots__ = ("eng", "fn", "deps", "dma", "ord", "awaited", "idx")


class Sched:

    def __init__(self):
        self.ops = []
        self.state = {}

    def op(self, eng, fn, reads=(), writes=(), deps=(), dma=False):
        o = _Op()
        o.eng, o.fn, o.dma = eng, fn, dma
        o.awaited, o.ord = False, None
        o.idx = len(self.ops)
        dep = set()
        for d in deps:
            if d is not None:
                dep.add(d)
        for ks in reads:
            for k in ks:
                st = self.state.get(k)
                if st and st[0] is not None:
                    dep.add(st[0])
        for ks in writes:
            for k in ks:
                st = self.state.get(k)
                if st:
                    if st[0] is not None:
                        dep.add(st[0])
                    dep.update(st[1])
        o.deps = list(dep)
        self.ops.append(o)
        for ks in reads:
            for k in ks:
                self.state.setdefault(k, [None, []])[1].append(o)
        for ks in writes:
            for k in ks:
                self.state[k] = [o, []]
        return o

    def emit(self, block, csem, dsem, nring=8):
        for o in self.ops:
            for d in o.deps:
                d.awaited = True
        ccnt = {e: 0 for e in ENGINES}
        dcnt = {e: 0 for e in ENGINES}
        for o in self.ops:
            if o.dma:
                i = dcnt[o.eng]
                dcnt[o.eng] += 1
                o.ord = (i % nring, 16 * (i // nring + 1), i)
                o.awaited = True
            elif o.awaited:
                ccnt[o.eng] += 1
                o.ord = ccnt[o.eng]

        def mk(eng):
            my_ops = [o for o in self.ops if o.eng == eng]

            def body(E):
                waited = {}
                for o in my_ops:
                    best = {}
                    for d in o.deps:
                        if d.dma:
                            k = (d.eng, "d", d.ord[0])
                            if k not in best or best[k].idx < d.idx:
                                best[k] = d
                        else:
                            k = (d.eng, "c")
                            if k not in best or best[k].idx < d.idx:
                                best[k] = d
                    for k, d in best.items():
                        if d.dma:
                            sem, val = dsem[d.eng][d.ord[0]], d.ord[1]
                        else:
                            sem, val = csem[d.eng], d.ord
                        if waited.get(k, 0) < val:
                            E.wait_ge(sem, val)
                            waited[k] = val
                    if o.dma:
                        slot, val, i = o.ord
                        if i >= nring:
                            k = (eng, "d", slot)
                            if waited.get(k, 0) < val - 16:
                                E.wait_ge(dsem[eng][slot], val - 16)
                                waited[k] = val - 16
                    inst = o.fn(E)
                    if o.dma:
                        inst.then_inc(dsem[eng][o.ord[0]], 16)
                    elif o.awaited:
                        inst.then_inc(csem[eng], 1)
            return body

        for eng in ENGINES:
            getattr(block, eng)(mk(eng))


L = {}


def _lay(name, nbytes, align=64):
    off = _lay.cur
    off = (off + align - 1) // align * align
    L[name] = off
    _lay.cur = off + nbytes
    return off


_lay.cur = 0
_lay("ident_bf", 256)
_lay("ident_f", 512)
_lay("ones_f", 512)
_lay("ones_bf", 256)
_lay("bias_all", 128)
_lay("brow", 384)
_lay("negr_bf", 64)
_lay("maxsq", 64)
_lay("maxk", 64)
_lay("gs_sc", 256)
_lay("us_sc", 256)
_lay("ds_sc", 64)
_lay("wcst", 32)
_lay("scnat", 512)
_lay("mn_bf", 16)
_lay("qmask", 128)
_lay("YS", 4096)
_lay("S1", 4096)
_lay("S2", 4096)
_lay("S3", 4096)
_lay("S4", 4096)
_lay("rowbf", 2048)
_lay("QA", 2048)
_lay("QB", 2048)
_lay("P2", 32768)
_lay("P3", 16384 + 32768)
_lay("P4", 65536)
_lay("P5", 32768)
AR_BYTES = (_lay.cur + 63) // 64 * 64
assert AR_BYTES <= 212800, AR_BYTES

D_KEY_UNIT = 2048


def _keys(off, nbytes):
    return tuple(("A", u) for u in range(off // D_KEY_UNIT,
                                         (off + nbytes - 1) // D_KEY_UNIT + 1))


class TV:

    def __init__(self, ap, keys):
        self.ap = ap
        self.k = keys

    def __getitem__(self, sl):
        return self.ap[sl]


def build_nc(dump=None):
    nc = bass.Bass("TRN2", target_bir_lowering=False)

    XT = nc.dram_tensor("XT", [2048, 2048], F32, kind="ExternalInput").ap()
    WQC = nc.dram_tensor("WQC", [16, 128, 2048], F32, kind="ExternalInput").ap()
    WKC = nc.dram_tensor("WKC", [4, 128, 2048], F32, kind="ExternalInput").ap()
    WVC = nc.dram_tensor("WVC", [4, 128, 2048], F32, kind="ExternalInput").ap()
    MEM = nc.dram_tensor("MEM", [512, 512], F32, kind="ExternalInput").ap()
    MNORM = nc.dram_tensor("MNORM", [512, 1], F32, kind="ExternalInput").ap()
    MW = nc.dram_tensor("MW", [1, 1], F32, kind="ExternalInput").ap()
    GCB = nc.dram_tensor("GCB", [32, 128, 4096], U8, kind="ExternalInput").ap()
    UCB = nc.dram_tensor("UCB", [32, 128, 4096], U8, kind="ExternalInput").ap()
    DCB = nc.dram_tensor("DCB", [8, 8, 128, 2048], U8, kind="ExternalInput").ap()
    GS = nc.dram_tensor("GS", [64, 128], F32, kind="ExternalInput").ap()
    US = nc.dram_tensor("US", [64, 128], F32, kind="ExternalInput").ap()
    DS = nc.dram_tensor("DS", [16, 128], F32, kind="ExternalInput").ap()
    OUT = nc.dram_tensor("OUT", [2048, 1024], F32, kind="ExternalOutput").ap()
    DUMP = None
    if dump is not None:
        DUMP = nc.dram_tensor("DUMP", [128, AR_BYTES // 4], F32,
                              kind="ExternalOutput").ap()

    with contextlib.ExitStack() as ctx:
        AR = ctx.enter_context(nc.sbuf_tensor("AR", [128, AR_BYTES // 4], F32))
        ARB = AR.bitcast(BF16)
        AR_F = AR.ap()
        AR_B = ARB.ap()
        PS = [ctx.enter_context(nc.psum_tensor("PS%d" % i, [128, 1024], F32))
              for i in range(4)]
        PSB = [p.bitcast(BF16) for p in PS]
        PSK = [(("PS", i),) for i in range(4)]

        csem = {e: ctx.enter_context(nc.semaphore("c_" + e)) for e in ENGINES}
        dsem = {e: [ctx.enter_context(nc.semaphore("d_%s%d" % (e, i)))
                    for i in range(8)]
                for e in ("gpsimd", "sync", "scalar")}

        def vf(name, cols, off_extra=0):
            off = L[name] + off_extra
            return TV(AR_F[:, off // 4: off // 4 + cols], _keys(off, cols * 4))

        def vb(name, cols, off_extra=0):
            off = L[name] + off_extra
            return TV(AR_B[:, off // 2: off // 2 + cols], _keys(off, cols * 2))

        ident_bf = vb("ident_bf", 128)
        ident_f = vf("ident_f", 128)
        ones_f = vf("ones_f", 128)
        ones_bf = vb("ones_bf", 128)
        bias_all = vf("bias_all", 32)
        brow = vf("brow", 96)
        negr_bf = vb("negr_bf", 32)
        maxsq = vf("maxsq", 16)
        maxk = vf("maxk", 16)
        gs_sc = vf("gs_sc", 64)
        us_sc = vf("us_sc", 64)
        ds_sc = vf("ds_sc", 16)
        wcst = vf("wcst", 8)
        scnat = vf("scnat", 128)
        mn_bf = vb("mn_bf", 8)
        qmask = vb("qmask", 33)
        YS = vf("YS", 1024)
        y1b = vb("YS", 1024)
        sqb = vb("YS", 1024, off_extra=2048)
        S1 = vf("S1", 1024)
        S2 = vf("S2", 1024)
        S3 = vf("S3", 1024)
        S4 = vf("S4", 1024)
        KSQ = TV(AR_F[:, L["S1"] // 4: L["S1"] // 4 + 2048],
                 _keys(L["S1"], 8192))
        rowbf = vb("rowbf", 1024)

        QT = [vb("P2", 1024, off_extra=i * 2048) for i in range(16)]
        x1T = QT
        QAb = vb("QA", 1024)
        QBb = vb("QB", 1024)
        khat = [vb("P3", 2048, off_extra=kc * 4096) for kc in range(4)]
        vhat = [vb("P3", 2048, off_extra=16384 + h * 4096) for h in range(8)]
        xbT = [vb("P4", 2048, off_extra=hc * 4096) for hc in range(16)]
        y1 = [vf("P4", 1024, off_extra=hc * 4096) for hc in range(16)]
        zz = y1
        NWS = 3
        wslot_f = [vf("P5", 2048, off_extra=s * 8192) for s in range(NWS)]
        wslot = [vb("P5", 2048, off_extra=s * 8192) for s in range(NWS)]
        vt = [vb("S1", 2048), vb("S2", 2048)]
        mem_bf = vb("P5", 2048, off_extra=24576)
        expT = [vb("P5", 1024, off_extra=s * 2048) for s in range(8)]
        sig = [vb("P5", 1024, off_extra=16384 + s * 2048) for s in range(4)]
        hact = [vb("P5", 8192, off_extra=s * 16384) for s in range(2)]
        AR_I = AR.bitcast(I32).ap()

        def vi(name, cols, off_extra=0):
            off = L[name] + off_extra
            return TV(AR_I[:, off // 4: off // 4 + cols], _keys(off, cols * 4))

        AR_U = AR.bitcast(U8).ap()

        def vu(name, cols, off_extra=0):
            off = L[name] + off_extra
            return TV(AR_U[:, off: off + cols], _keys(off, cols))

        NGS = 2
        NDS = 3
        gstg_u = [vu("P3", 4096, off_extra=s * 8192 + 4096) for s in range(NGS)]
        gstg = [vb("P3", 4096, off_extra=s * 8192) for s in range(NGS)]
        ustg_u = [vu("P3", 4096, off_extra=16384 + s * 8192 + 4096) for s in range(NGS)]
        ustg = [vb("P3", 4096, off_extra=16384 + s * 8192) for s in range(NGS)]
        dstg_u = [vu("P3", 2048, off_extra=32768 + s * 4096 + 2048) for s in range(NDS)]
        dstg = [vb("P3", 2048, off_extra=32768 + s * 4096) for s in range(NDS)]

        S = Sched()

        if dump is not None:
            S.op("gpsimd", lambda E: E.memset(AR_F[:, :], 0.0),
                 writes=[tuple(("A", u) for u in range(0, 120))])
        S.op("gpsimd", lambda E: E.memset(ones_f.ap, 1.0), writes=[ones_f.k])
        S.op("gpsimd", lambda E: E.memset(ones_bf.ap, 1.0), writes=[ones_bf.k])
        S.op("gpsimd", lambda E: E.memset(wcst[0:128, 4:5], -128.0),
             writes=[wcst.k])
        S.op("gpsimd", lambda E: E.memset(QAb[64:128, :], 0.0), writes=[QAb.k])
        S.op("gpsimd", lambda E: E.memset(QBb[0:64, :], 0.0), writes=[QBb.k])
        S.op("gpsimd", lambda E: E.memset(qmask.ap, 0.0), writes=[qmask.k])
        S.op("gpsimd", lambda E: E.memset(qmask[0:64, 0:1], 1.0),
             writes=[qmask.k])
        S.op("gpsimd", lambda E: E.memset(qmask[64:128, 32:33], 1.0),
             writes=[qmask.k])
        S.op("gpsimd", lambda E: E.affine_select(
            ident_bf.ap, ones_bf.ap, [[-1, 128]], ALU.is_equal, 0.0,
            base=0, channel_multiplier=1), reads=[ones_bf.k], writes=[ident_bf.k])
        S.op("gpsimd", lambda E: E.affine_select(
            ident_f.ap, ones_f.ap, [[-1, 128]], ALU.is_equal, 0.0,
            base=0, channel_multiplier=1), reads=[ones_f.k], writes=[ident_f.k])

        S.op("sync", lambda E: E.dma_start(out=wcst[0:1, 0:1], in_=MW[:, :]),
             writes=[wcst.k], dma=True)
        S.op("sync", lambda E: E.dma_start(out=wcst[64:65, 0:1], in_=MW[:, :]),
             writes=[wcst.k], dma=True)
        for b in (0, 64):
            S.op("scalar", lambda E, b=b: E.activation(
                wcst[b:b + 1, 1:2], wcst[b:b + 1, 0:1], AF.Sigmoid),
                reads=[wcst.k], writes=[wcst.k])
            S.op("vector", lambda E, b=b: E.tensor_scalar(
                wcst[b:b + 1, 2:3], wcst[b:b + 1, 1:2], -1.0, 1.0,
                ALU.mult, ALU.add), reads=[wcst.k], writes=[wcst.k])
            S.op("scalar", lambda E, b=b: E.activation(
                wcst[b:b + 1, 3:4], wcst[b:b + 1, 2:3], AF.Ln),
                reads=[wcst.k], writes=[wcst.k])

        for (nat_rows, DR, dst) in ((64, GS, gs_sc), (64, US, us_sc),
                                    (16, DS, ds_sc)):
            S.op("sync", lambda E, DR=DR, r=nat_rows: E.dma_start(
                out=scnat[0:r, 0:128], in_=DR[:, :]), writes=[scnat.k], dma=True)
            S.op("tensor", lambda E, r=nat_rows: E.transpose(
                PS[0].ap()[0:128, 0:r], scnat[0:r, 0:128], ident_f[0:r, 0:r]),
                reads=[scnat.k, ident_f.k], writes=[PSK[0]])
            S.op("vector", lambda E, dst=dst, r=nat_rows: E.tensor_copy(
                dst[0:128, 0:r], PS[0].ap()[0:128, 0:r]),
                reads=[PSK[0]], writes=[dst.k])

        S.op("gpsimd", lambda E: E.dma_start(
            out=mem_bf.ap, in_=MEM.rearrange("(c p) j -> p c j", p=128)),
            writes=[mem_bf.k], dma=True)
        S.op("sync", lambda E: E.dma_start(
            out=scnat[0:4, 0:128], in_=MNORM.rearrange("(c p) o -> c (p o)", p=128)),
            writes=[scnat.k], dma=True)
        S.op("tensor", lambda E: E.transpose(
            PS[0].ap()[0:128, 0:4], scnat[0:4, 0:128], ident_f[0:4, 0:4]),
            reads=[scnat.k, ident_f.k], writes=[PSK[0]])
        S.op("vector", lambda E: E.tensor_copy(mn_bf[0:128, 0:4],
                                               PS[0].ap()[0:128, 0:4]),
             reads=[PSK[0]], writes=[mn_bf.k])

        xstg = [TV(AR_F[:, L["S1"] // 4: L["S1"] // 4 + 2048],
                   _keys(L["S1"], 8192)),
                TV(AR_F[:, L["S3"] // 4: L["S3"] // 4 + 2048],
                   _keys(L["S3"], 8192))]
        for hc in range(16):
            xsl = hc % 2
            ring = "sync" if xsl == 0 else "gpsimd"
            S.op(ring, lambda E, hc=hc, xsl=xsl: E.dma_start(
                out=xstg[xsl].ap, in_=XT[hc * 128:(hc + 1) * 128, :]),
                writes=[xstg[xsl].k], dma=True)
            S.op("scalar", lambda E, hc=hc, xsl=xsl: E.activation(
                xbT[hc].ap, xstg[xsl].ap, AF.Copy),
                 reads=[xstg[xsl].k], writes=[xbT[hc].k])

        _wl = [0]

        def wload(DRC, idx):
            slot = _wl[0] % NWS
            _wl[0] += 1
            op = S.op("scalar", lambda E, slot=slot: E.dma_start(
                out=wslot_f[slot].ap, in_=DRC[idx]),
                writes=[wslot_f[slot].k], dma=True)
            S.op("scalar", lambda E, slot=slot: E.activation(
                wslot[slot].ap, wslot_f[slot].ap, AF.Copy),
                 reads=[wslot_f[slot].k], writes=[wslot_f[slot].k])
            return slot, op

        def proj_group(slot, ps_lo, ps_hi, tq_n, psk_writes):
            for hc in range(16):
                def fn(E, hc=hc):
                    inst = None
                    for tq in range(tq_n):
                        p = PS[ps_lo] if tq < 2 else PS[ps_hi]
                        col = (tq % 2) * 512
                        inst = E.matmul(
                            p.ap()[:, col:col + 512],
                            wslot[slot][:, hc * 128:(hc + 1) * 128],
                            xbT[hc][:, tq * 512:(tq + 1) * 512],
                            start=(hc == 0), stop=(hc == 15))
                    return inst
                S.op("tensor", fn, reads=[wslot[slot].k, xbT[hc].k],
                     writes=psk_writes)

        for kc in range(4):
            sl, _ = wload(WKC, kc)
            pl = 2 * (kc % 2)
            proj_group(sl, pl, pl + 1, 4, [PSK[pl], PSK[pl + 1]])
            for half in range(2):
                S.op("vector", lambda E, kc=kc, half=half, pl=pl: E.tensor_copy(
                    khat[kc][:, half * 1024:(half + 1) * 1024],
                    PS[pl + half].ap()[:, :]),
                    reads=[PSK[pl + half]], writes=[khat[kc].k])
        for kc in range(4):
            sl, _ = wload(WVC, kc)
            vsl = kc % 2
            pl = 2 * (kc % 2)
            proj_group(sl, pl, pl + 1, 4, [PSK[pl], PSK[pl + 1]])
            for half in range(2):
                S.op("vector", lambda E, half=half, vsl=vsl, pl=pl: E.tensor_copy(
                    vt[vsl][:, half * 1024:(half + 1) * 1024],
                    PS[pl + half].ap()[:, :]),
                    reads=[PSK[pl + half]], writes=[vt[vsl].k])
            for hsub in range(2):
                h = 2 * kc + hsub
                b = hsub * 64
                pt = 2 + (kc % 2)
                def trf(E, b=b, sl=vsl, pt=pt):
                    inst = None
                    for tch in range(16):
                        inst = E.transpose(
                            PSB[pt].ap()[0:128, tch * 64:(tch + 1) * 64],
                            vt[sl][b:b + 64, tch * 128:(tch + 1) * 128],
                            ident_bf[b:b + 64, b:b + 64])
                    return inst
                S.op("tensor", trf, reads=[vt[vsl].k, ident_bf.k],
                     writes=[PSK[pt]])
                S.op("vector", lambda E, h=h, pt=pt: E.tensor_copy(
                    vhat[h].ap.rearrange("p (c w) -> p c w", w=128)[:, :, 0:64],
                    PSB[pt].ap()[0:128, 0:1024].rearrange("p (c w) -> p c w", w=64)),
                    reads=[PSK[pt]], writes=[vhat[h].k])
                S.op("gpsimd", lambda E, h=h: E.memset(
                    vhat[h].ap.rearrange("p (c w) -> p c w", w=128)[:, :, 64:65],
                    1.0), writes=[vhat[h].k])
                S.op("gpsimd", lambda E, h=h: E.memset(
                    vhat[h].ap.rearrange("p (c w) -> p c w", w=128)[:, :, 65:128],
                    0.0), writes=[vhat[h].k])
        sqA = vb("S3", 1024)
        sqB = vb("YS", 1024)
        KSQB = vb("S1", 2048)

        def qbias(oc):
            sq = sqA if oc % 2 == 0 else sqB
            pb = 1 + 2 * (oc % 2)
            S.op("vector", lambda E, oc=oc, sq=sq: E.tensor_tensor(
                sq.ap, QT[oc].ap, QT[oc].ap, ALU.mult),
                reads=[QT[oc].k], writes=[sq.k])
            def qrowfn(E, pb=pb, sq=sq):
                inst = None
                for th in range(2):
                    inst = E.matmul(
                        PS[pb].ap()[0:33, th * 512:(th + 1) * 512],
                        qmask[0:128, 0:33],
                        sq[0:128, th * 512:(th + 1) * 512],
                        start=True, stop=True)
                return inst
            S.op("tensor", qrowfn, reads=[qmask.k, sq.k], writes=[PSK[pb]])
            S.op("vector", lambda E, oc=oc, pb=pb: E.reduce_max(
                brow[0:33, oc:oc + 1], PS[pb].ap()[0:33, 0:1024], axis=AX),
                reads=[PSK[pb]], writes=[brow.k])

        def kbias(kc):
            S.op("vector", lambda E, kc=kc: E.tensor_tensor(
                KSQB.ap, khat[kc].ap, khat[kc].ap, ALU.mult),
                reads=[khat[kc].k], writes=[KSQB.k])
            for tha in range(2):
                pb = 1 + 2 * tha
                def krowfn(E, pb=pb, tha=tha):
                    inst = None
                    for tq in range(2):
                        inst = E.matmul(
                            PS[pb].ap()[0:33, tq * 512:(tq + 1) * 512],
                            qmask[0:128, 0:33],
                            KSQB.ap[0:128, (2 * tha + tq) * 512:
                                    (2 * tha + tq + 1) * 512],
                            start=True, stop=True)
                    return inst
                S.op("tensor", krowfn, reads=[qmask.k, KSQB.k],
                     writes=[PSK[pb]])
                S.op("vector", lambda E, kc=kc, pb=pb, tha=tha: E.reduce_max(
                    brow[0:33, 16 + 4 * tha + kc:17 + 4 * tha + kc],
                    PS[pb].ap()[0:33, 0:1024], axis=AX),
                    reads=[PSK[pb]], writes=[brow.k])
            S.op("vector", lambda E, kc=kc: E.tensor_tensor(
                brow[0:33, 24 + kc:25 + kc], brow[0:33, 16 + kc:17 + kc],
                brow[0:33, 20 + kc:21 + kc], ALU.max),
                reads=[brow.k], writes=[brow.k])

        for oc in range(16):
            sl, _ = wload(WQC, oc)
            pp = 2 * (oc % 2)
            proj_group(sl, pp, pp + 1, 2, [PSK[pp]])
            S.op("vector", lambda E, oc=oc, pp=pp: E.tensor_copy(
                QT[oc].ap, PS[pp].ap()[:, :]),
                reads=[PSK[pp]], writes=[QT[oc].k])
            qbias(oc)
            if oc < 4:
                kbias(oc)

        for hc in range(16):
            S.op("sync", lambda E, hc=hc: E.dma_start(
                out=y1[hc].ap, in_=XT[hc * 128:(hc + 1) * 128, 0:1024]),
                writes=[y1[hc].k], dma=True)
        def prodfn(E):
            inst = None
            for oc in range(16):
                kc = oc % 4
                inst = E.tensor_tensor(brow[0:33, 28 + oc:29 + oc],
                                       brow[0:33, oc:oc + 1],
                                       brow[0:33, 24 + kc:25 + kc], ALU.mult)
            return inst
        S.op("vector", prodfn, reads=[brow.k], writes=[brow.k])
        S.op("vector", lambda E: E.memset(negr_bf[0:33, 0:32], 0.0),
             writes=[negr_bf.k])
        for par, sub in ((0, 0), (64, 1)):
            pr = 32 if par else 0
            S.op("scalar", lambda E, pr=pr: E.activation(
                S4[pr:pr + 1, 0:16], brow[pr:pr + 1, 28:44], AF.Ln),
                 reads=[brow.k], writes=[S4.k])
            S.op("scalar", lambda E, pr=pr: E.activation(
                S4[pr:pr + 1, 16:32], S4[pr:pr + 1, 0:16], AF.Exp, scale=0.5),
                 reads=[S4.k], writes=[S4.k])
            S.op("vector", lambda E, pr=pr: E.tensor_scalar(
                S4[pr:pr + 1, 32:48], S4[pr:pr + 1, 16:32], -0.125, None,
                ALU.mult), reads=[S4.k], writes=[S4.k])
            S.op("vector", lambda E, pr=pr, sub=sub: E.tensor_copy(
                negr_bf.ap.rearrange("p (a b) -> p a b", b=2)[pr:pr + 1, :, sub:sub + 1],
                S4[pr:pr + 1, 32:48]),
                 reads=[S4.k], writes=[negr_bf.k])
        def bcastfn(E):
            E.matmul(PS[3].ap()[0:128, 0:32], ones_bf[0:1, 0:128],
                     negr_bf[0:1, 0:32], start=True, stop=False)
            return E.matmul(PS[3].ap()[0:128, 0:32], ones_bf[32:33, 0:128],
                            negr_bf[32:33, 0:32], start=False, stop=True)
        S.op("tensor", bcastfn, reads=[ones_bf.k, negr_bf.k], writes=[PSK[3]])
        S.op("vector", lambda E: E.tensor_copy(bias_all.ap,
                                               PS[3].ap()[0:128, 0:32]),
             reads=[PSK[3]], writes=[bias_all.k])

        if dump == "qkv":
            _emit_dump(S, AR_F, DUMP)

        for g in range(4):
            sgl = sig if g % 2 == 0 else expT[0:4]
            for c in range(4):
                oc = 4 * g + c
                sA, sB = (S1, S2) if c % 2 == 0 else (S3, YS)
                S.op("vector", lambda E, oc=oc, sA=sA: E.tensor_scalar_min(
                    sA.ap, QT[oc].ap, 0.0), reads=[QT[oc].k], writes=[sA.k])
                S.op("scalar", lambda E, sA=sA, sB=sB: E.activation(
                    sB.ap, sA.ap, AF.Exp), reads=[sA.k], writes=[sB.k])
                S.op("vector", lambda E, oc=oc, sA=sA: E.tensor_scalar_max(
                    sA.ap, QT[oc].ap, 0.0), reads=[QT[oc].k], writes=[sA.k])
                S.op("vector", lambda E, c=c, sA=sA, sB=sB, sgl=sgl: E.scalar_tensor_tensor(
                    sgl[c].ap, sB.ap, -1.0, sA.ap, ALU.add, ALU.add),
                     reads=[sA.k, sB.k], writes=[sgl[c].k])
            def nrmfn(E, sgl=sgl):
                inst = None
                for th in range(2):
                    for c in range(4):
                        inst = E.matmul(
                            PS[3].ap()[0:1, th * 512:(th + 1) * 512],
                            mn_bf[0:128, c:c + 1],
                            sgl[c][:, th * 512:(th + 1) * 512],
                            start=(c == 0), stop=(c == 3))
                return inst
            S.op("tensor", nrmfn, reads=[mn_bf.k] + [t.k for t in sgl],
                 writes=[PSK[3]])
            S.op("vector", lambda E: E.reciprocal(S4[0:1, 0:1024],
                                                  PS[3].ap()[0:1, :]),
                 reads=[PSK[3]], writes=[S4.k])
            S.op("vector", lambda E: E.tensor_scalar(
                rowbf[0:1, 0:1024], S4[0:1, 0:1024], wcst[0:1, 1:2], None,
                ALU.mult), reads=[S4.k, wcst.k], writes=[rowbf.k])
            def psi2fn(E):
                inst = None
                for th in range(2):
                    inst = E.matmul(PS[3].ap()[0:128, th * 512:(th + 1) * 512],
                                    ones_bf[0:1, 0:128],
                                    rowbf[0:1, th * 512:(th + 1) * 512],
                                    start=True, stop=True)
                return inst
            S.op("tensor", psi2fn, reads=[ones_bf.k, rowbf.k], writes=[PSK[3]])
            S.op("vector", lambda E: E.tensor_copy(S3.ap, PS[3].ap()[:, :]),
                 reads=[PSK[3]], writes=[S3.k])
            for jc in range(4):
                pj = jc % 2
                def amfn(E, pj=pj, jc=jc, sgl=sgl):
                    inst = None
                    for th in range(2):
                        for c in range(4):
                            inst = E.matmul(
                                PS[pj].ap()[:, th * 512:(th + 1) * 512],
                                mem_bf[:, c * 512 + jc * 128: c * 512 + (jc + 1) * 128],
                                sgl[c][:, th * 512:(th + 1) * 512],
                                start=(c == 0), stop=(c == 3))
                    return inst
                S.op("tensor", amfn, reads=[mem_bf.k] + [t.k for t in sgl],
                     writes=[PSK[pj]])
                tmp = S1 if jc % 2 == 0 else S2
                S.op("vector", lambda E, pj=pj, tmp=tmp: E.tensor_tensor(
                    tmp.ap, PS[pj].ap()[:, :], S3.ap, ALU.mult),
                     reads=[PSK[pj], S3.k], writes=[tmp.k])
                hc = 4 * g + jc
                S.op("vector", lambda E, hc=hc, tmp=tmp: E.tensor_tensor(
                    y1[hc].ap, y1[hc].ap, tmp.ap, ALU.add),
                     reads=[tmp.k, y1[hc].k], writes=[y1[hc].k])

        if dump == "amem":
            _emit_dump(S, AR_F, DUMP)

        def emit_qk(p, tch):
            q, h = p // 8, p % 8
            oc, b, kc = q * 4 + h // 2, (h % 2) * 64, h // 2
            G = p * 16 + tch
            pc = 2 * (G % 2)
            sl = G % 8
            qh = QAb if b == 0 else QBb
            def qkfn(E, kc=kc, tch=tch, pc=pc, qh=qh):
                inst = None
                for sh in range(2):
                    inst = E.matmul(
                        PS[pc].ap()[:, sh * 512:(sh + 1) * 512],
                        khat[kc][0:128, tch * 128:(tch + 1) * 128],
                        qh[0:128, sh * 512:(sh + 1) * 512],
                        start=True, stop=True)
                return inst
            S.op("tensor", qkfn, reads=[khat[kc].k, qh.k], writes=[PSK[pc]])
            S.op("scalar", lambda E, sl=sl, p=p, pc=pc: E.activation(
                expT[sl].ap, PS[pc].ap()[:, :], AF.Exp,
                scale=0.125, bias=bias_all[:, p:p + 1]),
                 reads=[PSK[pc], bias_all.k], writes=[expT[sl].k])

        def emit_pv(p, tch):
            h = p % 8
            pa = 1 + 2 * (p % 2)
            sl = (p * 16 + tch) % 8
            def pvfn(E, h=h, tch=tch, sl=sl, pa=pa):
                inst = None
                for sh in range(2):
                    inst = E.matmul(
                        PS[pa].ap()[:, sh * 512:(sh + 1) * 512],
                        vhat[h][:, tch * 128: tch * 128 + 128],
                        expT[sl][:, sh * 512:(sh + 1) * 512],
                        start=(tch == 0), stop=(tch == 15))
                return inst
            S.op("tensor", pvfn, reads=[vhat[h].k, expT[sl].k],
                 writes=[PSK[pa]])

        def emit_tail(p, step):
            q, h = p // 8, p % 8
            oc, b = q * 4 + h // 2, (h % 2) * 64
            pa = 1 + 2 * (p % 2)
            if step == 0:
                S.op("scalar", lambda E, pa=pa: E.activation(
                    S4[64:65, 0:1024], PS[pa].ap()[64:65, :], AF.Ln),
                     reads=[PSK[pa]], writes=[S4.k])
                S.op("scalar", lambda E: E.activation(
                    rowbf[64:65, 0:1024], S4[64:65, 0:1024], AF.Exp,
                    scale=-1.0, bias=wcst[64:65, 3:4]),
                     reads=[S4.k, wcst.k], writes=[rowbf.k])
            elif step == 1:
                def psifn(E, pa=pa):
                    inst = None
                    for sh in range(2):
                        inst = E.matmul(
                            PS[pa].ap()[64:128, sh * 512:(sh + 1) * 512],
                            ones_bf[64:65, 0:64],
                            rowbf[64:65, sh * 512:(sh + 1) * 512],
                            start=True, stop=True, tile_position=(64, 64))
                    return inst
                S.op("tensor", psifn, reads=[ones_bf.k, rowbf.k],
                     writes=[PSK[pa]])
            elif step == 2:
                S.op("vector", lambda E, pa=pa: E.tensor_copy(
                    S2[0:64, 0:1024], PS[pa].ap()[64:128, :]),
                     reads=[PSK[pa]], writes=[S2.k])
            elif step == 3:
                if b == 0:
                    S.op("vector", lambda E, pa=pa: E.tensor_tensor(
                        S1[0:64, 0:1024], PS[pa].ap()[0:64, :],
                        S2[0:64, 0:1024], ALU.mult),
                         reads=[PSK[pa], S2.k], writes=[S1.k])
                else:
                    S.op("vector", lambda E, pa=pa: E.tensor_tensor(
                        rowbf[0:64, 0:1024], PS[pa].ap()[0:64, :],
                        S2[0:64, 0:1024], ALU.mult),
                         reads=[PSK[pa], S2.k], writes=[rowbf.k])
                    def shfn(E, pa=pa):
                        inst = None
                        for sh in range(2):
                            inst = E.matmul(
                                PS[pa].ap()[64:128, sh * 512:(sh + 1) * 512],
                                ident_bf[0:64, 0:64],
                                rowbf[0:64, sh * 512:(sh + 1) * 512],
                                start=True, stop=True, tile_position=(0, 64))
                        return inst
                    S.op("tensor", shfn, reads=[ident_bf.k, rowbf.k],
                         writes=[PSK[pa]])
            elif step == 4:
                if b == 0:
                    S.op("vector", lambda E, oc=oc: E.tensor_tensor(
                        y1[oc][0:64, :], y1[oc][0:64, :], S1[0:64, 0:1024],
                        ALU.add), reads=[S1.k, y1[oc].k], writes=[y1[oc].k])
                else:
                    S.op("vector", lambda E, oc=oc, pa=pa: E.tensor_tensor(
                        y1[oc][64:128, :], y1[oc][64:128, :],
                        PS[pa].ap()[64:128, :], ALU.add),
                         reads=[PSK[pa], y1[oc].k], writes=[y1[oc].k])

        PVLAG = 3
        TAIL_SLOTS = {5: 0, 7: 1, 9: 2, 11: 3, 13: 4}
        for p in range(32):
            if p % 2 == 0:
                oc0 = (p // 8) * 4 + (p % 8) // 2
                S.op("vector", lambda E, oc0=oc0: E.tensor_copy(
                    QAb[0:64, :], QT[oc0][0:64, :]),
                     reads=[QT[oc0].k], writes=[QAb.k])
                S.op("vector", lambda E, oc0=oc0: E.tensor_copy(
                    QBb[64:128, :], QT[oc0][64:128, :]),
                     reads=[QT[oc0].k], writes=[QBb.k])
            for tch in range(16):
                emit_qk(p, tch)
                G = p * 16 + tch
                if G >= PVLAG:
                    emit_pv((G - PVLAG) // 16, (G - PVLAG) % 16)
                if p >= 1 and tch in TAIL_SLOTS:
                    emit_tail(p - 1, TAIL_SLOTS[tch])
        for G in range(512 - PVLAG, 512):
            emit_pv(G // 16, G % 16)
        for step in range(5):
            emit_tail(31, step)

        if dump == "attn":
            _emit_dump(S, AR_F, DUMP)

        y1b2 = vb("S3", 1024)
        sqb2 = vb("S3", 1024, off_extra=2048)

        def layernorm(src, dst_fn, tag):
            for hc in range(16):
                yb = y1b if hc % 2 == 0 else y1b2
                sb_ = sqb if hc % 2 == 0 else sqb2
                S.op("vector", lambda E, hc=hc, yb=yb: E.tensor_copy(
                    yb.ap, src[hc].ap), reads=[src[hc].k], writes=[yb.k])
                S.op("vector", lambda E, yb=yb, sb_=sb_: E.tensor_tensor(
                    sb_.ap, yb.ap, yb.ap, ALU.mult),
                     reads=[yb.k], writes=[sb_.k])
                def s12fn(E, hc=hc, yb=yb, sb_=sb_):
                    inst = None
                    for th in range(2):
                        inst = E.matmul(
                            PS[0].ap()[0:1, th * 512:(th + 1) * 512],
                            ones_bf[0:128, 0:1],
                            yb[0:128, th * 512:(th + 1) * 512],
                            start=(hc == 0), stop=(hc == 15))
                        inst = E.matmul(
                            PS[3].ap()[0:1, th * 512:(th + 1) * 512],
                            ones_bf[0:128, 0:1],
                            sb_[0:128, th * 512:(th + 1) * 512],
                            start=(hc == 0), stop=(hc == 15))
                    return inst
                S.op("tensor", s12fn, reads=[yb.k, sb_.k, ones_bf.k],
                     writes=[PSK[0], PSK[3]])
            S.op("vector", lambda E: E.tensor_scalar(
                S4[0:1, 0:1024], PS[0].ap()[0:1, :], 1.0 / 2048.0, None,
                ALU.mult), reads=[PSK[0]], writes=[S4.k])
            S.op("vector", lambda E: E.tensor_tensor(
                S2[0:1, 0:1024], S4[0:1, 0:1024], S4[0:1, 0:1024], ALU.mult),
                 reads=[S4.k], writes=[S2.k])
            S.op("vector", lambda E: E.scalar_tensor_tensor(
                S3[0:1, 0:1024], PS[3].ap()[0:1, :], 1.0 / 2048.0,
                S2[0:1, 0:1024], ALU.mult, ALU.subtract),
                 reads=[PSK[3], S2.k], writes=[S3.k])
            S.op("vector", lambda E: E.tensor_scalar(
                S3[0:1, 0:1024], S3[0:1, 0:1024], 1e-5, None, ALU.add),
                 reads=[S3.k], writes=[S3.k])
            S.op("scalar", lambda E: E.activation(
                S2[0:1, 0:1024], S3[0:1, 0:1024], AF.Ln),
                 reads=[S3.k], writes=[S2.k])
            S.op("scalar", lambda E: E.activation(
                S3[0:1, 0:1024], S2[0:1, 0:1024], AF.Exp, scale=-0.5),
                 reads=[S2.k], writes=[S3.k])
            S.op("vector", lambda E: E.scalar_tensor_tensor(
                S4[0:1, 0:1024], S4[0:1, 0:1024], -1.0, S3[0:1, 0:1024],
                ALU.mult, ALU.mult), reads=[S3.k, S4.k], writes=[S4.k])
            def bcast(row, psi):
                def fn(E):
                    inst = None
                    for th in range(2):
                        inst = E.matmul(
                            PS[psi].ap()[0:128, th * 512:(th + 1) * 512],
                            ones_f[0:1, 0:128],
                            row[0:1, th * 512:(th + 1) * 512],
                            start=True, stop=True)
                    return inst
                return fn
            S.op("tensor", bcast(S3.ap, 1), reads=[ones_f.k, S3.k],
                 writes=[PSK[1]])
            S.op("tensor", bcast(S4.ap, 2), reads=[ones_f.k, S4.k],
                 writes=[PSK[2]])
            S.op("vector", lambda E: E.tensor_copy(S1.ap, PS[1].ap()[:, :]),
                 reads=[PSK[1]], writes=[S1.k])
            S.op("vector", lambda E: E.tensor_copy(S2.ap, PS[2].ap()[:, :]),
                 reads=[PSK[2]], writes=[S2.k])
            for hc in range(16):
                dst_fn(hc)

        QAf = TV(AR_F[:, L["QA"] // 4: L["QA"] // 4 + 1024], _keys(L["QA"], 4096))

        def ln1_dst(hc):
            tmp = S3 if hc % 2 == 0 else QAf
            S.op("vector", lambda E, hc=hc, tmp=tmp: E.tensor_tensor(
                tmp.ap, y1[hc].ap, S1.ap, ALU.mult),
                 reads=[y1[hc].k, S1.k], writes=[tmp.k])
            S.op("vector", lambda E, hc=hc, tmp=tmp: E.tensor_tensor(
                x1T[hc].ap, tmp.ap, S2.ap, ALU.add),
                 reads=[tmp.k, S2.k], writes=[x1T[hc].k])
        layernorm(y1, ln1_dst, "ln1")

        if dump == "ln1":
            _emit_dump(S, AR_F, DUMP)

        def stage(DRC, idx, slots_u, slots_b, nsl, ring):
            sl = idx % nsl
            S.op(ring, lambda E, sl=sl, DRC=DRC: E.dma_start(
                out=slots_u[sl].ap, in_=DRC), writes=[slots_b[sl].k], dma=True)
            S.op("scalar", lambda E, sl=sl: E.activation(
                slots_b[sl].ap, slots_u[sl].ap, AF.Copy, bias=-128.0),
                 reads=[slots_b[sl].k], writes=[slots_b[sl].k])
            return sl

        def down_block(g, cb):
            hs = g % 2
            dsl = stage(DCB[cb, g], 8 * g + cb, dstg_u, dstg, NDS, "gpsimd")
            for sub in range(2):
                co = 2 * cb + sub
                pd = 2 + (co % 2)
                def dfn(E, dsl=dsl, hs=hs, pd=pd, sub=sub):
                    inst = None
                    for th in range(2):
                        for il in range(8):
                            inst = E.matmul(
                                PS[pd].ap()[:, th * 512:(th + 1) * 512],
                                dstg[dsl][:, il * 256 + sub * 128:
                                          il * 256 + sub * 128 + 128],
                                hact[hs][:, il * 1024 + th * 512:
                                         il * 1024 + (th + 1) * 512],
                                start=(il == 0), stop=(il == 7))
                    return inst
                S.op("tensor", dfn, reads=[dstg[dsl].k, hact[hs].k],
                     writes=[PSK[pd]])
                if g == 0:
                    S.op("vector", lambda E, co=co, pd=pd: E.tensor_scalar(
                        zz[co].ap, PS[pd].ap()[:, :], ds_sc[:, co:co + 1],
                        None, ALU.mult),
                         reads=[PSK[pd], ds_sc.k], writes=[zz[co].k])
                else:
                    S.op("vector", lambda E, co=co, pd=pd: E.scalar_tensor_tensor(
                        zz[co].ap, PS[pd].ap()[:, :], ds_sc[:, co:co + 1],
                        zz[co].ap, ALU.mult, ALU.add),
                         reads=[PSK[pd], ds_sc.k, zz[co].k], writes=[zz[co].k])

        for g in range(8):
            hs = g % 2
            for ib in range(4):
                ssl = stage(GCB[4 * g + ib], 4 * g + ib, gstg_u, gstg, NGS,
                            "sync")
                usl = stage(UCB[4 * g + ib], 4 * g + ib, ustg_u, ustg, NGS,
                            "scalar")
                for sub in range(2):
                    il = 2 * ib + sub
                    ic = 8 * g + il
                    def gufn(E, ssl=ssl, usl=usl, sub=sub, w=0):
                        stg = gstg[ssl] if w == 0 else ustg[usl]
                        pi = 0 if w == 0 else 1
                        inst = None
                        for hc in range(16):
                            for th in range(2):
                                inst = E.matmul(
                                    PS[pi].ap()[:, th * 512:(th + 1) * 512],
                                    stg[:, hc * 256 + sub * 128:
                                        hc * 256 + sub * 128 + 128],
                                    x1T[hc][:, th * 512:(th + 1) * 512],
                                    start=(hc == 0), stop=(hc == 15))
                        return inst
                    S.op("tensor", lambda E, ssl=ssl, usl=usl, sub=sub: gufn(
                        E, ssl, usl, sub, 0),
                         reads=[gstg[ssl].k] + [x.k for x in x1T],
                         writes=[PSK[0]])
                    S.op("tensor", lambda E, ssl=ssl, usl=usl, sub=sub: gufn(
                        E, ssl, usl, sub, 1),
                         reads=[ustg[usl].k] + [x.k for x in x1T],
                         writes=[PSK[1]])
                    S.op("scalar", lambda E, ic=ic: E.activation(
                        S3.ap, PS[0].ap()[:, :], AF.Silu,
                        scale=gs_sc[:, ic:ic + 1]),
                         reads=[PSK[0], gs_sc.k], writes=[S3.k])
                    S.op("vector", lambda E, ic=ic, hs=hs, il=il: E.scalar_tensor_tensor(
                        hact[hs][:, il * 1024:(il + 1) * 1024],
                        PS[1].ap()[:, :], us_sc[:, ic:ic + 1], S3.ap,
                        ALU.mult, ALU.mult),
                         reads=[PSK[1], us_sc.k, S3.k], writes=[hact[hs].k])
                if g > 0:
                    down_block(g - 1, 2 * ib)
                    down_block(g - 1, 2 * ib + 1)
        for cb in range(8):
            down_block(7, cb)

        if dump == "ffn":
            _emit_dump(S, AR_F, DUMP)

        st_ops = []

        def ln2_dst(hc):
            outv = YS if hc % 2 == 0 else S4
            tmp = S3 if hc % 2 == 0 else QAf
            S.op("vector", lambda E, hc=hc, tmp=tmp: E.tensor_tensor(
                tmp.ap, zz[hc].ap, S1.ap, ALU.mult),
                 reads=[zz[hc].k, S1.k], writes=[tmp.k])
            S.op("vector", lambda E, hc=hc, outv=outv, tmp=tmp: E.tensor_tensor(
                outv.ap, tmp.ap, S2.ap, ALU.add),
                 reads=[tmp.k, S2.k], writes=[outv.k])
            st_ops.append(S.op("sync", lambda E, hc=hc, outv=outv: E.dma_start(
                out=OUT[hc * 128:(hc + 1) * 128, :], in_=outv.ap),
                reads=[outv.k], dma=True))
        layernorm(zz, ln2_dst, "ln2")

        S.op("sync", lambda E: E.nop(), deps=st_ops)

        block = ctx.enter_context(nc.Block())
        S.emit(block, csem, dsem)
    return nc


def _emit_dump(S, AR_F, DUMP):
    for hc in range(16):
        w = AR_F.shape[1] // 16 if hc < 15 else AR_F.shape[1] - 15 * (AR_F.shape[1] // 16)
        c0 = hc * (AR_F.shape[1] // 16)
        S.op("sync", lambda E, c0=c0, w=w: E.dma_start(
            out=DUMP[:, c0:c0 + w], in_=AR_F[:, c0:c0 + w]),
            reads=[tuple(("A", u) for u in range(0, 120))], dma=True)


def _prep(inp):
    x = np.asarray(inp["L0_attn_x"], np.float32)
    Wq = np.asarray(inp["L0_attn_Wq"], np.float32)
    Wk = np.asarray(inp["L0_attn_Wk"], np.float32)
    Wv = np.asarray(inp["L0_attn_Wv"], np.float32)
    mem = np.ascontiguousarray(np.asarray(inp["L0_attn_memory"], np.float32))
    mnorm = np.ascontiguousarray(
        np.asarray(inp["L0_attn_memory_norm"], np.float32).reshape(512, 1))
    mw = np.asarray(inp["L0_attn_memory_weight"], np.float32).reshape(1, 1)
    gw = np.asarray(inp["L0_ffn_gate_w"], np.int32)
    uw = np.asarray(inp["L0_ffn_up_w"], np.int32)
    dw = np.asarray(inp["L0_ffn_down_w"], np.int32)
    gw8 = np.ascontiguousarray(gw.view(np.uint8).reshape(8192, 2048, 4)[:, :, 0])
    uw8 = np.ascontiguousarray(uw.view(np.uint8).reshape(8192, 2048, 4)[:, :, 0])
    dw8 = np.ascontiguousarray(dw.view(np.uint8).reshape(2048, 8192, 4)[:, :, 0])
    gs = np.asarray(inp["L0_ffn_gate_scale"], np.float32)
    us = np.asarray(inp["L0_ffn_up_scale"], np.float32)
    ds = np.asarray(inp["L0_ffn_down_scale"], np.float32)

    common = dict(
        WQC=np.ascontiguousarray(
            Wq.reshape(16, 128, 16, 128).transpose(0, 3, 2, 1)
            .reshape(16, 128, 2048)),
        WKC=np.ascontiguousarray(
            Wk.reshape(4, 128, 16, 128).transpose(0, 3, 2, 1)
            .reshape(4, 128, 2048)),
        WVC=np.ascontiguousarray(
            Wv.reshape(4, 128, 16, 128).transpose(0, 3, 2, 1)
            .reshape(4, 128, 2048)),
        MEM=mem, MNORM=mnorm, MW=np.ascontiguousarray(mw),
        GCB=np.ascontiguousarray(
            gw8.reshape(32, 256, 16, 128).transpose(0, 3, 2, 1)
            .reshape(32, 128, 4096)),
        UCB=np.ascontiguousarray(
            uw8.reshape(32, 256, 16, 128).transpose(0, 3, 2, 1)
            .reshape(32, 128, 4096)),
        DCB=np.ascontiguousarray(
            dw8.reshape(8, 256, 8, 8, 128).transpose(0, 2, 4, 3, 1)
            .reshape(8, 8, 128, 2048)),
        GS=np.ascontiguousarray(gs.reshape(64, 128)),
        US=np.ascontiguousarray(us.reshape(64, 128)),
        DS=np.ascontiguousarray(ds.reshape(16, 128)),
    )
    in_maps = []
    xTs = {}
    for c in range(8):
        beta, half = c // 2, c % 2
        if beta not in xTs:
            xTs[beta] = np.ascontiguousarray(x[beta].T)
        xT = xTs[beta]
        if half == 0:
            XT = xT
        else:
            XT = np.ascontiguousarray(
                np.concatenate([xT[:, 1024:], xT[:, :1024]], axis=1))
        m = dict(common)
        m["XT"] = XT
        in_maps.append(m)
    return in_maps


def kernel(**inputs):
    in_maps = _prep(inputs)
    nc = build_nc()
    res = run_bass_kernel_spmd(nc, in_maps, core_ids=list(range(8)))
    out = np.empty((4, 2048, 2048), np.float32)
    for c in range(8):
        beta, half = c // 2, c % 2
        out[beta, half * 1024:(half + 1) * 1024, :] = res.results[c]["OUT"].T
    return out


# version 26
# speedup vs baseline: 1.0332x; 1.0260x over previous
import numpy as np
import contextlib
import concourse.bass as bass
import concourse.mybir as mybir
from concourse.bass_utils import run_bass_kernel_spmd

F32 = mybir.dt.float32
BF16 = mybir.dt.bfloat16
I32 = mybir.dt.int32
U8 = mybir.dt.uint8
AF = mybir.ActivationFunctionType
ALU = mybir.AluOpType
AX = mybir.AxisListType.X

ENGINES = ("tensor", "scalar", "vector", "gpsimd", "sync")


class _Op:
    __slots__ = ("eng", "fn", "deps", "dma", "ord", "awaited", "idx")


class Sched:

    def __init__(self):
        self.ops = []
        self.state = {}

    def op(self, eng, fn, reads=(), writes=(), deps=(), dma=False):
        o = _Op()
        o.eng, o.fn, o.dma = eng, fn, dma
        o.awaited, o.ord = False, None
        o.idx = len(self.ops)
        dep = set()
        for d in deps:
            if d is not None:
                dep.add(d)
        for ks in reads:
            for k in ks:
                st = self.state.get(k)
                if st and st[0] is not None:
                    dep.add(st[0])
        for ks in writes:
            for k in ks:
                st = self.state.get(k)
                if st:
                    if st[0] is not None:
                        dep.add(st[0])
                    dep.update(st[1])
        o.deps = list(dep)
        self.ops.append(o)
        for ks in reads:
            for k in ks:
                self.state.setdefault(k, [None, []])[1].append(o)
        for ks in writes:
            for k in ks:
                self.state[k] = [o, []]
        return o

    def emit(self, block, csem, dsem, nring=8):
        for o in self.ops:
            for d in o.deps:
                d.awaited = True
        ccnt = {e: 0 for e in ENGINES}
        dcnt = {e: 0 for e in ENGINES}
        for o in self.ops:
            if o.dma:
                i = dcnt[o.eng]
                dcnt[o.eng] += 1
                o.ord = (i % nring, 16 * (i // nring + 1), i)
                o.awaited = True
            elif o.awaited:
                ccnt[o.eng] += 1
                o.ord = ccnt[o.eng]

        def mk(eng):
            my_ops = [o for o in self.ops if o.eng == eng]

            def body(E):
                waited = {}
                for o in my_ops:
                    best = {}
                    for d in o.deps:
                        if d.dma:
                            k = (d.eng, "d", d.ord[0])
                            if k not in best or best[k].idx < d.idx:
                                best[k] = d
                        else:
                            k = (d.eng, "c")
                            if k not in best or best[k].idx < d.idx:
                                best[k] = d
                    for k, d in best.items():
                        if d.dma:
                            sem, val = dsem[d.eng][d.ord[0]], d.ord[1]
                        else:
                            sem, val = csem[d.eng], d.ord
                        if waited.get(k, 0) < val:
                            E.wait_ge(sem, val)
                            waited[k] = val
                    if o.dma:
                        slot, val, i = o.ord
                        if i >= nring:
                            k = (eng, "d", slot)
                            if waited.get(k, 0) < val - 16:
                                E.wait_ge(dsem[eng][slot], val - 16)
                                waited[k] = val - 16
                    inst = o.fn(E)
                    if o.dma:
                        inst.then_inc(dsem[eng][o.ord[0]], 16)
                    elif o.awaited:
                        inst.then_inc(csem[eng], 1)
            return body

        for eng in ENGINES:
            getattr(block, eng)(mk(eng))


L = {}


def _lay(name, nbytes, align=64):
    off = _lay.cur
    off = (off + align - 1) // align * align
    L[name] = off
    _lay.cur = off + nbytes
    return off


_lay.cur = 0
_lay("ident_bf", 256)
_lay("ident_f", 512)
_lay("ones_f", 512)
_lay("ones_bf", 256)
_lay("bias_all", 128)
_lay("brow", 384)
_lay("negr_bf", 64)
_lay("maxsq", 64)
_lay("maxk", 64)
_lay("gs_sc", 256)
_lay("us_sc", 256)
_lay("ds_sc", 64)
_lay("wcst", 32)
_lay("scnat", 512)
_lay("mn_bf", 16)
_lay("qmask", 128)
_lay("YS", 4096)
_lay("S1", 4096)
_lay("S2", 4096)
_lay("S3", 4096)
_lay("S4", 4096)
_lay("rowbf", 2048)
_lay("QA", 2048)
_lay("QB", 2048)
_lay("P2", 32768)
_lay("P3", 16384 + 32768)
_lay("P4", 65536)
_lay("P5", 32768)
AR_BYTES = (_lay.cur + 63) // 64 * 64
assert AR_BYTES <= 212800, AR_BYTES

D_KEY_UNIT = 2048


def _keys(off, nbytes):
    return tuple(("A", u) for u in range(off // D_KEY_UNIT,
                                         (off + nbytes - 1) // D_KEY_UNIT + 1))


class TV:

    def __init__(self, ap, keys):
        self.ap = ap
        self.k = keys

    def __getitem__(self, sl):
        return self.ap[sl]


def build_nc(dump=None):
    nc = bass.Bass("TRN2", target_bir_lowering=False)

    XT = nc.dram_tensor("XT", [2048, 2048], F32, kind="ExternalInput").ap()
    WQC = nc.dram_tensor("WQC", [16, 128, 2048], F32, kind="ExternalInput").ap()
    WKC = nc.dram_tensor("WKC", [4, 128, 2048], F32, kind="ExternalInput").ap()
    WVC = nc.dram_tensor("WVC", [4, 128, 2048], F32, kind="ExternalInput").ap()
    MEM = nc.dram_tensor("MEM", [512, 512], F32, kind="ExternalInput").ap()
    MNORM = nc.dram_tensor("MNORM", [512, 1], F32, kind="ExternalInput").ap()
    MW = nc.dram_tensor("MW", [1, 1], F32, kind="ExternalInput").ap()
    GCB = nc.dram_tensor("GCB", [32, 128, 4096], U8, kind="ExternalInput").ap()
    UCB = nc.dram_tensor("UCB", [32, 128, 4096], U8, kind="ExternalInput").ap()
    DCB = nc.dram_tensor("DCB", [8, 8, 128, 2048], U8, kind="ExternalInput").ap()
    GS = nc.dram_tensor("GS", [64, 128], F32, kind="ExternalInput").ap()
    US = nc.dram_tensor("US", [64, 128], F32, kind="ExternalInput").ap()
    DS = nc.dram_tensor("DS", [16, 128], F32, kind="ExternalInput").ap()
    OUT = nc.dram_tensor("OUT", [2048, 1024], F32, kind="ExternalOutput").ap()
    DUMP = None
    if dump is not None:
        DUMP = nc.dram_tensor("DUMP", [128, AR_BYTES // 4], F32,
                              kind="ExternalOutput").ap()

    with contextlib.ExitStack() as ctx:
        AR = ctx.enter_context(nc.sbuf_tensor("AR", [128, AR_BYTES // 4], F32))
        ARB = AR.bitcast(BF16)
        AR_F = AR.ap()
        AR_B = ARB.ap()
        PS = [ctx.enter_context(nc.psum_tensor("PS%d" % i, [128, 1024], F32))
              for i in range(4)]
        PSB = [p.bitcast(BF16) for p in PS]
        PSK = [(("PS", i),) for i in range(4)]

        csem = {e: ctx.enter_context(nc.semaphore("c_" + e)) for e in ENGINES}
        dsem = {e: [ctx.enter_context(nc.semaphore("d_%s%d" % (e, i)))
                    for i in range(8)]
                for e in ("gpsimd", "sync", "scalar")}

        def vf(name, cols, off_extra=0):
            off = L[name] + off_extra
            return TV(AR_F[:, off // 4: off // 4 + cols], _keys(off, cols * 4))

        def vb(name, cols, off_extra=0):
            off = L[name] + off_extra
            return TV(AR_B[:, off // 2: off // 2 + cols], _keys(off, cols * 2))

        ident_bf = vb("ident_bf", 128)
        ident_f = vf("ident_f", 128)
        ones_f = vf("ones_f", 128)
        ones_bf = vb("ones_bf", 128)
        bias_all = vf("bias_all", 32)
        brow = vf("brow", 96)
        negr_bf = vb("negr_bf", 32)
        maxsq = vf("maxsq", 16)
        maxk = vf("maxk", 16)
        gs_sc = vf("gs_sc", 64)
        us_sc = vf("us_sc", 64)
        ds_sc = vf("ds_sc", 16)
        wcst = vf("wcst", 8)
        scnat = vf("scnat", 128)
        mn_bf = vb("mn_bf", 8)
        qmask = vb("qmask", 33)
        YS = vf("YS", 1024)
        y1b = vb("YS", 1024)
        sqb = vb("YS", 1024, off_extra=2048)
        S1 = vf("S1", 1024)
        S2 = vf("S2", 1024)
        S3 = vf("S3", 1024)
        S4 = vf("S4", 1024)
        KSQ = TV(AR_F[:, L["S1"] // 4: L["S1"] // 4 + 2048],
                 _keys(L["S1"], 8192))
        rowbf = vb("rowbf", 1024)

        QT = [vb("P2", 1024, off_extra=i * 2048) for i in range(16)]
        x1T = QT
        QAb = vb("QA", 1024)
        QBb = vb("QB", 1024)
        khat = [vb("P3", 2048, off_extra=kc * 4096) for kc in range(4)]
        vhat = [vb("P3", 2048, off_extra=16384 + h * 4096) for h in range(8)]
        xbT = [vb("P4", 2048, off_extra=hc * 4096) for hc in range(16)]
        y1 = [vf("P4", 1024, off_extra=hc * 4096) for hc in range(16)]
        zz = y1
        NWS = 3
        wslot_f = [vf("P5", 2048, off_extra=s * 8192) for s in range(NWS)]
        wslot = [vb("P5", 2048, off_extra=s * 8192) for s in range(NWS)]
        vt = [vb("S1", 2048), vb("S2", 2048)]
        mem_bf = vb("P5", 2048, off_extra=24576)
        expT = [vb("P5", 1024, off_extra=s * 2048) for s in range(8)]
        sig = [vb("P5", 1024, off_extra=16384 + s * 2048) for s in range(4)]
        hact = [vb("P5", 8192, off_extra=s * 16384) for s in range(2)]
        AR_I = AR.bitcast(I32).ap()

        def vi(name, cols, off_extra=0):
            off = L[name] + off_extra
            return TV(AR_I[:, off // 4: off // 4 + cols], _keys(off, cols * 4))

        AR_U = AR.bitcast(U8).ap()

        def vu(name, cols, off_extra=0):
            off = L[name] + off_extra
            return TV(AR_U[:, off: off + cols], _keys(off, cols))

        NGS = 2
        NDS = 3
        gstg_u = [vu("P3", 4096, off_extra=s * 8192 + 4096) for s in range(NGS)]
        gstg = [vb("P3", 4096, off_extra=s * 8192) for s in range(NGS)]
        ustg_u = [vu("P3", 4096, off_extra=16384 + s * 8192 + 4096) for s in range(NGS)]
        ustg = [vb("P3", 4096, off_extra=16384 + s * 8192) for s in range(NGS)]
        dstg_u = [vu("P3", 2048, off_extra=32768 + s * 4096 + 2048) for s in range(NDS)]
        dstg = [vb("P3", 2048, off_extra=32768 + s * 4096) for s in range(NDS)]

        S = Sched()

        if dump is not None:
            S.op("gpsimd", lambda E: E.memset(AR_F[:, :], 0.0),
                 writes=[tuple(("A", u) for u in range(0, 120))])
        S.op("gpsimd", lambda E: E.memset(ones_f.ap, 1.0), writes=[ones_f.k])
        S.op("gpsimd", lambda E: E.memset(ones_bf.ap, 1.0), writes=[ones_bf.k])
        S.op("gpsimd", lambda E: E.memset(wcst[0:128, 4:5], -128.0),
             writes=[wcst.k])
        S.op("gpsimd", lambda E: E.memset(QAb[64:128, :], 0.0), writes=[QAb.k])
        S.op("gpsimd", lambda E: E.memset(QBb[0:64, :], 0.0), writes=[QBb.k])
        S.op("gpsimd", lambda E: E.memset(qmask.ap, 0.0), writes=[qmask.k])
        S.op("gpsimd", lambda E: E.memset(qmask[0:64, 0:1], 1.0),
             writes=[qmask.k])
        S.op("gpsimd", lambda E: E.memset(qmask[64:128, 32:33], 1.0),
             writes=[qmask.k])
        S.op("gpsimd", lambda E: E.affine_select(
            ident_bf.ap, ones_bf.ap, [[-1, 128]], ALU.is_equal, 0.0,
            base=0, channel_multiplier=1), reads=[ones_bf.k], writes=[ident_bf.k])
        S.op("gpsimd", lambda E: E.affine_select(
            ident_f.ap, ones_f.ap, [[-1, 128]], ALU.is_equal, 0.0,
            base=0, channel_multiplier=1), reads=[ones_f.k], writes=[ident_f.k])

        S.op("sync", lambda E: E.dma_start(out=wcst[0:1, 0:1], in_=MW[:, :]),
             writes=[wcst.k], dma=True)
        S.op("sync", lambda E: E.dma_start(out=wcst[64:65, 0:1], in_=MW[:, :]),
             writes=[wcst.k], dma=True)
        for b in (0, 64):
            S.op("scalar", lambda E, b=b: E.activation(
                wcst[b:b + 1, 1:2], wcst[b:b + 1, 0:1], AF.Sigmoid),
                reads=[wcst.k], writes=[wcst.k])
            S.op("vector", lambda E, b=b: E.tensor_scalar(
                wcst[b:b + 1, 2:3], wcst[b:b + 1, 1:2], -1.0, 1.0,
                ALU.mult, ALU.add), reads=[wcst.k], writes=[wcst.k])
            S.op("scalar", lambda E, b=b: E.activation(
                wcst[b:b + 1, 3:4], wcst[b:b + 1, 2:3], AF.Ln),
                reads=[wcst.k], writes=[wcst.k])

        for (nat_rows, DR, dst) in ((64, GS, gs_sc), (64, US, us_sc),
                                    (16, DS, ds_sc)):
            S.op("sync", lambda E, DR=DR, r=nat_rows: E.dma_start(
                out=scnat[0:r, 0:128], in_=DR[:, :]), writes=[scnat.k], dma=True)
            S.op("tensor", lambda E, r=nat_rows: E.transpose(
                PS[0].ap()[0:128, 0:r], scnat[0:r, 0:128], ident_f[0:r, 0:r]),
                reads=[scnat.k, ident_f.k], writes=[PSK[0]])
            S.op("vector", lambda E, dst=dst, r=nat_rows: E.tensor_copy(
                dst[0:128, 0:r], PS[0].ap()[0:128, 0:r]),
                reads=[PSK[0]], writes=[dst.k])

        S.op("gpsimd", lambda E: E.dma_start(
            out=mem_bf.ap, in_=MEM.rearrange("(c p) j -> p c j", p=128)),
            writes=[mem_bf.k], dma=True)
        S.op("sync", lambda E: E.dma_start(
            out=scnat[0:4, 0:128], in_=MNORM.rearrange("(c p) o -> c (p o)", p=128)),
            writes=[scnat.k], dma=True)
        S.op("tensor", lambda E: E.transpose(
            PS[0].ap()[0:128, 0:4], scnat[0:4, 0:128], ident_f[0:4, 0:4]),
            reads=[scnat.k, ident_f.k], writes=[PSK[0]])
        S.op("vector", lambda E: E.tensor_copy(mn_bf[0:128, 0:4],
                                               PS[0].ap()[0:128, 0:4]),
             reads=[PSK[0]], writes=[mn_bf.k])

        xstg = [TV(AR_F[:, L["S1"] // 4: L["S1"] // 4 + 2048],
                   _keys(L["S1"], 8192)),
                TV(AR_F[:, L["S3"] // 4: L["S3"] // 4 + 2048],
                   _keys(L["S3"], 8192))]
        for hc in range(16):
            xsl = hc % 2
            ring = "sync" if xsl == 0 else "gpsimd"
            S.op(ring, lambda E, hc=hc, xsl=xsl: E.dma_start(
                out=xstg[xsl].ap, in_=XT[hc * 128:(hc + 1) * 128, :]),
                writes=[xstg[xsl].k], dma=True)
            S.op("scalar", lambda E, hc=hc, xsl=xsl: E.activation(
                xbT[hc].ap, xstg[xsl].ap, AF.Copy),
                 reads=[xstg[xsl].k], writes=[xbT[hc].k])

        _wl = [0]

        def wload(DRC, idx):
            slot = _wl[0] % NWS
            _wl[0] += 1
            op = S.op("scalar", lambda E, slot=slot: E.dma_start(
                out=wslot_f[slot].ap, in_=DRC[idx]),
                writes=[wslot_f[slot].k], dma=True)
            S.op("scalar", lambda E, slot=slot: E.activation(
                wslot[slot].ap, wslot_f[slot].ap, AF.Copy),
                 reads=[wslot_f[slot].k], writes=[wslot_f[slot].k])
            return slot, op

        def proj_group(slot, ps_lo, ps_hi, tq_n, psk_writes):
            for hc in range(16):
                def fn(E, hc=hc):
                    inst = None
                    for tq in range(tq_n):
                        p = PS[ps_lo] if tq < 2 else PS[ps_hi]
                        col = (tq % 2) * 512
                        inst = E.matmul(
                            p.ap()[:, col:col + 512],
                            wslot[slot][:, hc * 128:(hc + 1) * 128],
                            xbT[hc][:, tq * 512:(tq + 1) * 512],
                            start=(hc == 0), stop=(hc == 15))
                    return inst
                S.op("tensor", fn, reads=[wslot[slot].k, xbT[hc].k],
                     writes=psk_writes)

        for kc in range(4):
            sl, _ = wload(WKC, kc)
            pl = 2 * (kc % 2)
            proj_group(sl, pl, pl + 1, 4, [PSK[pl], PSK[pl + 1]])
            for half in range(2):
                S.op("vector", lambda E, kc=kc, half=half, pl=pl: E.tensor_copy(
                    khat[kc][:, half * 1024:(half + 1) * 1024],
                    PS[pl + half].ap()[:, :]),
                    reads=[PSK[pl + half]], writes=[khat[kc].k])
        for kc in range(4):
            sl, _ = wload(WVC, kc)
            vsl = kc % 2
            pl = 2 * (kc % 2)
            proj_group(sl, pl, pl + 1, 4, [PSK[pl], PSK[pl + 1]])
            for half in range(2):
                S.op("vector", lambda E, half=half, vsl=vsl, pl=pl: E.tensor_copy(
                    vt[vsl][:, half * 1024:(half + 1) * 1024],
                    PS[pl + half].ap()[:, :]),
                    reads=[PSK[pl + half]], writes=[vt[vsl].k])
            for hsub in range(2):
                h = 2 * kc + hsub
                b = hsub * 64
                pt = 2 + (kc % 2)
                def trf(E, b=b, sl=vsl, pt=pt):
                    inst = None
                    for tch in range(16):
                        inst = E.transpose(
                            PSB[pt].ap()[0:128, tch * 64:(tch + 1) * 64],
                            vt[sl][b:b + 64, tch * 128:(tch + 1) * 128],
                            ident_bf[b:b + 64, b:b + 64])
                    return inst
                S.op("tensor", trf, reads=[vt[vsl].k, ident_bf.k],
                     writes=[PSK[pt]])
                S.op("vector", lambda E, h=h, pt=pt: E.tensor_copy(
                    vhat[h].ap.rearrange("p (c w) -> p c w", w=128)[:, :, 0:64],
                    PSB[pt].ap()[0:128, 0:1024].rearrange("p (c w) -> p c w", w=64)),
                    reads=[PSK[pt]], writes=[vhat[h].k])
                S.op("gpsimd", lambda E, h=h: E.memset(
                    vhat[h].ap.rearrange("p (c w) -> p c w", w=128)[:, :, 64:65],
                    1.0), writes=[vhat[h].k])
                S.op("gpsimd", lambda E, h=h: E.memset(
                    vhat[h].ap.rearrange("p (c w) -> p c w", w=128)[:, :, 65:128],
                    0.0), writes=[vhat[h].k])
        sqA = vb("S3", 1024)
        sqB = vb("YS", 1024)
        KSQB = vb("S1", 2048)

        def qbias(oc):
            sq = sqA if oc % 2 == 0 else sqB
            pb = 1 + 2 * (oc % 2)
            S.op("vector", lambda E, oc=oc, sq=sq: E.tensor_tensor(
                sq.ap, QT[oc].ap, QT[oc].ap, ALU.mult),
                reads=[QT[oc].k], writes=[sq.k])
            def qrowfn(E, pb=pb, sq=sq):
                inst = None
                for th in range(2):
                    inst = E.matmul(
                        PS[pb].ap()[0:33, th * 512:(th + 1) * 512],
                        qmask[0:128, 0:33],
                        sq[0:128, th * 512:(th + 1) * 512],
                        start=True, stop=True)
                return inst
            S.op("tensor", qrowfn, reads=[qmask.k, sq.k], writes=[PSK[pb]])
            S.op("vector", lambda E, oc=oc, pb=pb: E.reduce_max(
                brow[0:33, oc:oc + 1], PS[pb].ap()[0:33, 0:1024], axis=AX),
                reads=[PSK[pb]], writes=[brow.k])

        def kbias(kc):
            S.op("vector", lambda E, kc=kc: E.tensor_tensor(
                KSQB.ap, khat[kc].ap, khat[kc].ap, ALU.mult),
                reads=[khat[kc].k], writes=[KSQB.k])
            for tha in range(2):
                pb = 1 + 2 * tha
                def krowfn(E, pb=pb, tha=tha):
                    inst = None
                    for tq in range(2):
                        inst = E.matmul(
                            PS[pb].ap()[0:33, tq * 512:(tq + 1) * 512],
                            qmask[0:128, 0:33],
                            KSQB.ap[0:128, (2 * tha + tq) * 512:
                                    (2 * tha + tq + 1) * 512],
                            start=True, stop=True)
                    return inst
                S.op("tensor", krowfn, reads=[qmask.k, KSQB.k],
                     writes=[PSK[pb]])
                S.op("vector", lambda E, kc=kc, pb=pb, tha=tha: E.reduce_max(
                    brow[0:33, 16 + 4 * tha + kc:17 + 4 * tha + kc],
                    PS[pb].ap()[0:33, 0:1024], axis=AX),
                    reads=[PSK[pb]], writes=[brow.k])
            S.op("vector", lambda E, kc=kc: E.tensor_tensor(
                brow[0:33, 24 + kc:25 + kc], brow[0:33, 16 + kc:17 + kc],
                brow[0:33, 20 + kc:21 + kc], ALU.max),
                reads=[brow.k], writes=[brow.k])

        for oc in range(16):
            sl, _ = wload(WQC, oc)
            pp = 2 * (oc % 2)
            proj_group(sl, pp, pp + 1, 2, [PSK[pp]])
            S.op("vector", lambda E, oc=oc, pp=pp: E.tensor_copy(
                QT[oc].ap, PS[pp].ap()[:, :]),
                reads=[PSK[pp]], writes=[QT[oc].k])
            qbias(oc)
            if oc < 4:
                kbias(oc)

        for hc in range(16):
            S.op("sync", lambda E, hc=hc: E.dma_start(
                out=y1[hc].ap, in_=XT[hc * 128:(hc + 1) * 128, 0:1024]),
                writes=[y1[hc].k], dma=True)
        def prodfn(E):
            inst = None
            for oc in range(16):
                kc = oc % 4
                inst = E.tensor_tensor(brow[0:33, 28 + oc:29 + oc],
                                       brow[0:33, oc:oc + 1],
                                       brow[0:33, 24 + kc:25 + kc], ALU.mult)
            return inst
        S.op("vector", prodfn, reads=[brow.k], writes=[brow.k])
        S.op("vector", lambda E: E.memset(negr_bf[0:33, 0:32], 0.0),
             writes=[negr_bf.k])
        for par, sub in ((0, 0), (64, 1)):
            pr = 32 if par else 0
            S.op("scalar", lambda E, pr=pr: E.activation(
                S4[pr:pr + 1, 0:16], brow[pr:pr + 1, 28:44], AF.Ln),
                 reads=[brow.k], writes=[S4.k])
            S.op("scalar", lambda E, pr=pr: E.activation(
                S4[pr:pr + 1, 16:32], S4[pr:pr + 1, 0:16], AF.Exp, scale=0.5),
                 reads=[S4.k], writes=[S4.k])
            S.op("vector", lambda E, pr=pr: E.tensor_scalar(
                S4[pr:pr + 1, 32:48], S4[pr:pr + 1, 16:32], -0.125, None,
                ALU.mult), reads=[S4.k], writes=[S4.k])
            S.op("vector", lambda E, pr=pr, sub=sub: E.tensor_copy(
                negr_bf.ap.rearrange("p (a b) -> p a b", b=2)[pr:pr + 1, :, sub:sub + 1],
                S4[pr:pr + 1, 32:48]),
                 reads=[S4.k], writes=[negr_bf.k])
        def bcastfn(E):
            E.matmul(PS[3].ap()[0:128, 0:32], ones_bf[0:1, 0:128],
                     negr_bf[0:1, 0:32], start=True, stop=False)
            return E.matmul(PS[3].ap()[0:128, 0:32], ones_bf[32:33, 0:128],
                            negr_bf[32:33, 0:32], start=False, stop=True)
        S.op("tensor", bcastfn, reads=[ones_bf.k, negr_bf.k], writes=[PSK[3]])
        S.op("vector", lambda E: E.tensor_copy(bias_all.ap,
                                               PS[3].ap()[0:128, 0:32]),
             reads=[PSK[3]], writes=[bias_all.k])

        if dump == "qkv":
            _emit_dump(S, AR_F, DUMP)

        for g in range(4):
            sgl = sig if g % 2 == 0 else expT[4:8]
            for c in range(4):
                oc = 4 * g + c
                sA, sB = (S1, S2) if c % 2 == 0 else (S3, YS)
                S.op("vector", lambda E, oc=oc, sA=sA: E.tensor_scalar_min(
                    sA.ap, QT[oc].ap, 0.0), reads=[QT[oc].k], writes=[sA.k])
                S.op("scalar", lambda E, sA=sA, sB=sB: E.activation(
                    sB.ap, sA.ap, AF.Exp), reads=[sA.k], writes=[sB.k])
                S.op("vector", lambda E, oc=oc, sA=sA: E.tensor_scalar_max(
                    sA.ap, QT[oc].ap, 0.0), reads=[QT[oc].k], writes=[sA.k])
                S.op("vector", lambda E, c=c, sA=sA, sB=sB, sgl=sgl: E.scalar_tensor_tensor(
                    sgl[c].ap, sB.ap, -1.0, sA.ap, ALU.add, ALU.add),
                     reads=[sA.k, sB.k], writes=[sgl[c].k])
            def nrmfn(E, sgl=sgl):
                inst = None
                for th in range(2):
                    for c in range(4):
                        inst = E.matmul(
                            PS[3].ap()[0:1, th * 512:(th + 1) * 512],
                            mn_bf[0:128, c:c + 1],
                            sgl[c][:, th * 512:(th + 1) * 512],
                            start=(c == 0), stop=(c == 3))
                return inst
            S.op("tensor", nrmfn, reads=[mn_bf.k] + [t.k for t in sgl],
                 writes=[PSK[3]])
            S.op("vector", lambda E: E.reciprocal(S4[0:1, 0:1024],
                                                  PS[3].ap()[0:1, :]),
                 reads=[PSK[3]], writes=[S4.k])
            S.op("vector", lambda E: E.tensor_scalar(
                rowbf[0:1, 0:1024], S4[0:1, 0:1024], wcst[0:1, 1:2], None,
                ALU.mult), reads=[S4.k, wcst.k], writes=[rowbf.k])
            def psi2fn(E):
                inst = None
                for th in range(2):
                    inst = E.matmul(PS[3].ap()[0:128, th * 512:(th + 1) * 512],
                                    ones_bf[0:1, 0:128],
                                    rowbf[0:1, th * 512:(th + 1) * 512],
                                    start=True, stop=True)
                return inst
            S.op("tensor", psi2fn, reads=[ones_bf.k, rowbf.k], writes=[PSK[3]])
            S.op("vector", lambda E: E.tensor_copy(S3.ap, PS[3].ap()[:, :]),
                 reads=[PSK[3]], writes=[S3.k])
            for jc in range(4):
                pj = jc % 2
                def amfn(E, pj=pj, jc=jc, sgl=sgl):
                    inst = None
                    for th in range(2):
                        for c in range(4):
                            inst = E.matmul(
                                PS[pj].ap()[:, th * 512:(th + 1) * 512],
                                mem_bf[:, c * 512 + jc * 128: c * 512 + (jc + 1) * 128],
                                sgl[c][:, th * 512:(th + 1) * 512],
                                start=(c == 0), stop=(c == 3))
                    return inst
                S.op("tensor", amfn, reads=[mem_bf.k] + [t.k for t in sgl],
                     writes=[PSK[pj]])
                tmp = S1 if jc % 2 == 0 else S2
                S.op("vector", lambda E, pj=pj, tmp=tmp: E.tensor_tensor(
                    tmp.ap, PS[pj].ap()[:, :], S3.ap, ALU.mult),
                     reads=[PSK[pj], S3.k], writes=[tmp.k])
                hc = 4 * g + jc
                S.op("vector", lambda E, hc=hc, tmp=tmp: E.tensor_tensor(
                    y1[hc].ap, y1[hc].ap, tmp.ap, ALU.add),
                     reads=[tmp.k, y1[hc].k], writes=[y1[hc].k])

        if dump == "amem":
            _emit_dump(S, AR_F, DUMP)

        def emit_qk(p, tch):
            q, h = p // 8, p % 8
            oc, b, kc = q * 4 + h // 2, (h % 2) * 64, h // 2
            G = p * 16 + tch
            pc = 2 * (G % 2)
            sl = G % 8
            qh = QAb if b == 0 else QBb
            def qkfn(E, kc=kc, tch=tch, pc=pc, qh=qh):
                inst = None
                for sh in range(2):
                    inst = E.matmul(
                        PS[pc].ap()[:, sh * 512:(sh + 1) * 512],
                        khat[kc][0:128, tch * 128:(tch + 1) * 128],
                        qh[0:128, sh * 512:(sh + 1) * 512],
                        start=True, stop=True)
                return inst
            S.op("tensor", qkfn, reads=[khat[kc].k, qh.k], writes=[PSK[pc]])
            S.op("scalar", lambda E, sl=sl, p=p, pc=pc: E.activation(
                expT[sl].ap, PS[pc].ap()[:, :], AF.Exp,
                scale=0.125, bias=bias_all[:, p:p + 1]),
                 reads=[PSK[pc], bias_all.k], writes=[expT[sl].k])

        def emit_pv(p, tch):
            h = p % 8
            pa = 1 + 2 * (p % 2)
            sl = (p * 16 + tch) % 8
            def pvfn(E, h=h, tch=tch, sl=sl, pa=pa):
                inst = None
                for sh in range(2):
                    inst = E.matmul(
                        PS[pa].ap()[:, sh * 512:(sh + 1) * 512],
                        vhat[h][:, tch * 128: tch * 128 + 128],
                        expT[sl][:, sh * 512:(sh + 1) * 512],
                        start=(tch == 0), stop=(tch == 15))
                return inst
            S.op("tensor", pvfn, reads=[vhat[h].k, expT[sl].k],
                 writes=[PSK[pa]])

        def emit_tail(p, step):
            q, h = p // 8, p % 8
            oc, b = q * 4 + h // 2, (h % 2) * 64
            pa = 1 + 2 * (p % 2)
            if step == 0:
                S.op("scalar", lambda E, pa=pa: E.activation(
                    S4[64:65, 0:1024], PS[pa].ap()[64:65, :], AF.Ln),
                     reads=[PSK[pa]], writes=[S4.k])
                S.op("scalar", lambda E: E.activation(
                    rowbf[64:65, 0:1024], S4[64:65, 0:1024], AF.Exp,
                    scale=-1.0, bias=wcst[64:65, 3:4]),
                     reads=[S4.k, wcst.k], writes=[rowbf.k])
            elif step == 1:
                def psifn(E, pa=pa):
                    inst = None
                    for sh in range(2):
                        inst = E.matmul(
                            PS[pa].ap()[64:128, sh * 512:(sh + 1) * 512],
                            ones_bf[64:65, 0:64],
                            rowbf[64:65, sh * 512:(sh + 1) * 512],
                            start=True, stop=True, tile_position=(64, 64))
                    return inst
                S.op("tensor", psifn, reads=[ones_bf.k, rowbf.k],
                     writes=[PSK[pa]])
            elif step == 2:
                S.op("vector", lambda E, pa=pa: E.tensor_copy(
                    S2[0:64, 0:1024], PS[pa].ap()[64:128, :]),
                     reads=[PSK[pa]], writes=[S2.k])
            elif step == 3:
                if b == 0:
                    S.op("vector", lambda E, pa=pa: E.tensor_tensor(
                        S1[0:64, 0:1024], PS[pa].ap()[0:64, :],
                        S2[0:64, 0:1024], ALU.mult),
                         reads=[PSK[pa], S2.k], writes=[S1.k])
                else:
                    S.op("vector", lambda E, pa=pa: E.tensor_tensor(
                        rowbf[0:64, 0:1024], PS[pa].ap()[0:64, :],
                        S2[0:64, 0:1024], ALU.mult),
                         reads=[PSK[pa], S2.k], writes=[rowbf.k])
                    def shfn(E, pa=pa):
                        inst = None
                        for sh in range(2):
                            inst = E.matmul(
                                PS[pa].ap()[64:128, sh * 512:(sh + 1) * 512],
                                ident_bf[0:64, 0:64],
                                rowbf[0:64, sh * 512:(sh + 1) * 512],
                                start=True, stop=True, tile_position=(0, 64))
                        return inst
                    S.op("tensor", shfn, reads=[ident_bf.k, rowbf.k],
                         writes=[PSK[pa]])
            elif step == 4:
                if b == 0:
                    S.op("vector", lambda E, oc=oc: E.tensor_tensor(
                        y1[oc][0:64, :], y1[oc][0:64, :], S1[0:64, 0:1024],
                        ALU.add), reads=[S1.k, y1[oc].k], writes=[y1[oc].k])
                else:
                    S.op("vector", lambda E, oc=oc, pa=pa: E.tensor_tensor(
                        y1[oc][64:128, :], y1[oc][64:128, :],
                        PS[pa].ap()[64:128, :], ALU.add),
                         reads=[PSK[pa], y1[oc].k], writes=[y1[oc].k])

        PVLAG = 3
        TAIL_SLOTS = {5: 0, 7: 1, 9: 2, 11: 3, 13: 4}
        for p in range(32):
            if p % 2 == 0:
                oc0 = (p // 8) * 4 + (p % 8) // 2
                S.op("vector", lambda E, oc0=oc0: E.tensor_copy(
                    QAb[0:64, :], QT[oc0][0:64, :]),
                     reads=[QT[oc0].k], writes=[QAb.k])
                S.op("vector", lambda E, oc0=oc0: E.tensor_copy(
                    QBb[64:128, :], QT[oc0][64:128, :]),
                     reads=[QT[oc0].k], writes=[QBb.k])
            for tch in range(16):
                emit_qk(p, tch)
                G = p * 16 + tch
                if G >= PVLAG:
                    emit_pv((G - PVLAG) // 16, (G - PVLAG) % 16)
                if p >= 1 and tch in TAIL_SLOTS:
                    emit_tail(p - 1, TAIL_SLOTS[tch])
        for G in range(512 - PVLAG, 512):
            emit_pv(G // 16, G % 16)
        for step in range(5):
            emit_tail(31, step)

        if dump == "attn":
            _emit_dump(S, AR_F, DUMP)

        y1b2 = vb("S3", 1024)
        sqb2 = vb("S3", 1024, off_extra=2048)

        def layernorm(src, dst_fn, tag):
            for hc in range(16):
                yb = y1b if hc % 2 == 0 else y1b2
                sb_ = sqb if hc % 2 == 0 else sqb2
                S.op("vector", lambda E, hc=hc, yb=yb: E.tensor_copy(
                    yb.ap, src[hc].ap), reads=[src[hc].k], writes=[yb.k])
                S.op("vector", lambda E, yb=yb, sb_=sb_: E.tensor_tensor(
                    sb_.ap, yb.ap, yb.ap, ALU.mult),
                     reads=[yb.k], writes=[sb_.k])
                def s12fn(E, hc=hc, yb=yb, sb_=sb_):
                    inst = None
                    for th in range(2):
                        inst = E.matmul(
                            PS[0].ap()[0:1, th * 512:(th + 1) * 512],
                            ones_bf[0:128, 0:1],
                            yb[0:128, th * 512:(th + 1) * 512],
                            start=(hc == 0), stop=(hc == 15))
                        inst = E.matmul(
                            PS[3].ap()[0:1, th * 512:(th + 1) * 512],
                            ones_bf[0:128, 0:1],
                            sb_[0:128, th * 512:(th + 1) * 512],
                            start=(hc == 0), stop=(hc == 15))
                    return inst
                S.op("tensor", s12fn, reads=[yb.k, sb_.k, ones_bf.k],
                     writes=[PSK[0], PSK[3]])
            S.op("vector", lambda E: E.tensor_scalar(
                S4[0:1, 0:1024], PS[0].ap()[0:1, :], 1.0 / 2048.0, None,
                ALU.mult), reads=[PSK[0]], writes=[S4.k])
            S.op("vector", lambda E: E.tensor_tensor(
                S2[0:1, 0:1024], S4[0:1, 0:1024], S4[0:1, 0:1024], ALU.mult),
                 reads=[S4.k], writes=[S2.k])
            S.op("vector", lambda E: E.scalar_tensor_tensor(
                S3[0:1, 0:1024], PS[3].ap()[0:1, :], 1.0 / 2048.0,
                S2[0:1, 0:1024], ALU.mult, ALU.subtract),
                 reads=[PSK[3], S2.k], writes=[S3.k])
            S.op("vector", lambda E: E.tensor_scalar(
                S3[0:1, 0:1024], S3[0:1, 0:1024], 1e-5, None, ALU.add),
                 reads=[S3.k], writes=[S3.k])
            S.op("scalar", lambda E: E.activation(
                S2[0:1, 0:1024], S3[0:1, 0:1024], AF.Ln),
                 reads=[S3.k], writes=[S2.k])
            S.op("scalar", lambda E: E.activation(
                S3[0:1, 0:1024], S2[0:1, 0:1024], AF.Exp, scale=-0.5),
                 reads=[S2.k], writes=[S3.k])
            S.op("vector", lambda E: E.scalar_tensor_tensor(
                S4[0:1, 0:1024], S4[0:1, 0:1024], -1.0, S3[0:1, 0:1024],
                ALU.mult, ALU.mult), reads=[S3.k, S4.k], writes=[S4.k])
            def bcast(row, psi):
                def fn(E):
                    inst = None
                    for th in range(2):
                        inst = E.matmul(
                            PS[psi].ap()[0:128, th * 512:(th + 1) * 512],
                            ones_f[0:1, 0:128],
                            row[0:1, th * 512:(th + 1) * 512],
                            start=True, stop=True)
                    return inst
                return fn
            S.op("tensor", bcast(S3.ap, 1), reads=[ones_f.k, S3.k],
                 writes=[PSK[1]])
            S.op("tensor", bcast(S4.ap, 2), reads=[ones_f.k, S4.k],
                 writes=[PSK[2]])
            S.op("vector", lambda E: E.tensor_copy(S1.ap, PS[1].ap()[:, :]),
                 reads=[PSK[1]], writes=[S1.k])
            S.op("vector", lambda E: E.tensor_copy(S2.ap, PS[2].ap()[:, :]),
                 reads=[PSK[2]], writes=[S2.k])
            for hc in range(16):
                dst_fn(hc)

        QAf = TV(AR_F[:, L["QA"] // 4: L["QA"] // 4 + 1024], _keys(L["QA"], 4096))

        def ln1_dst(hc):
            tmp = S3 if hc % 2 == 0 else QAf
            S.op("vector", lambda E, hc=hc, tmp=tmp: E.tensor_tensor(
                tmp.ap, y1[hc].ap, S1.ap, ALU.mult),
                 reads=[y1[hc].k, S1.k], writes=[tmp.k])
            S.op("vector", lambda E, hc=hc, tmp=tmp: E.tensor_tensor(
                x1T[hc].ap, tmp.ap, S2.ap, ALU.add),
                 reads=[tmp.k, S2.k], writes=[x1T[hc].k])
        layernorm(y1, ln1_dst, "ln1")

        if dump == "ln1":
            _emit_dump(S, AR_F, DUMP)

        def stage(DRC, idx, slots_u, slots_b, nsl, ring):
            sl = idx % nsl
            S.op(ring, lambda E, sl=sl, DRC=DRC: E.dma_start(
                out=slots_u[sl].ap, in_=DRC), writes=[slots_b[sl].k], dma=True)
            S.op("scalar", lambda E, sl=sl: E.activation(
                slots_b[sl].ap, slots_u[sl].ap, AF.Copy, bias=-128.0),
                 reads=[slots_b[sl].k], writes=[slots_b[sl].k])
            return sl

        def down_block(g, cb):
            hs = g % 2
            dsl = stage(DCB[cb, g], 8 * g + cb, dstg_u, dstg, NDS, "gpsimd")
            for sub in range(2):
                co = 2 * cb + sub
                pd = 2 + (co % 2)
                def dfn(E, dsl=dsl, hs=hs, pd=pd, sub=sub):
                    inst = None
                    for th in range(2):
                        for il in range(8):
                            inst = E.matmul(
                                PS[pd].ap()[:, th * 512:(th + 1) * 512],
                                dstg[dsl][:, il * 256 + sub * 128:
                                          il * 256 + sub * 128 + 128],
                                hact[hs][:, il * 1024 + th * 512:
                                         il * 1024 + (th + 1) * 512],
                                start=(il == 0), stop=(il == 7))
                    return inst
                S.op("tensor", dfn, reads=[dstg[dsl].k, hact[hs].k],
                     writes=[PSK[pd]])
                if g == 0:
                    S.op("vector", lambda E, co=co, pd=pd: E.tensor_scalar(
                        zz[co].ap, PS[pd].ap()[:, :], ds_sc[:, co:co + 1],
                        None, ALU.mult),
                         reads=[PSK[pd], ds_sc.k], writes=[zz[co].k])
                else:
                    S.op("vector", lambda E, co=co, pd=pd: E.scalar_tensor_tensor(
                        zz[co].ap, PS[pd].ap()[:, :], ds_sc[:, co:co + 1],
                        zz[co].ap, ALU.mult, ALU.add),
                         reads=[PSK[pd], ds_sc.k, zz[co].k], writes=[zz[co].k])

        for g in range(8):
            hs = g % 2
            for ib in range(4):
                ssl = stage(GCB[4 * g + ib], 4 * g + ib, gstg_u, gstg, NGS,
                            "sync")
                usl = stage(UCB[4 * g + ib], 4 * g + ib, ustg_u, ustg, NGS,
                            "scalar")
                for sub in range(2):
                    il = 2 * ib + sub
                    ic = 8 * g + il
                    def gufn(E, ssl=ssl, usl=usl, sub=sub, w=0):
                        stg = gstg[ssl] if w == 0 else ustg[usl]
                        pi = 0 if w == 0 else 1
                        inst = None
                        for hc in range(16):
                            for th in range(2):
                                inst = E.matmul(
                                    PS[pi].ap()[:, th * 512:(th + 1) * 512],
                                    stg[:, hc * 256 + sub * 128:
                                        hc * 256 + sub * 128 + 128],
                                    x1T[hc][:, th * 512:(th + 1) * 512],
                                    start=(hc == 0), stop=(hc == 15))
                        return inst
                    S.op("tensor", lambda E, ssl=ssl, usl=usl, sub=sub: gufn(
                        E, ssl, usl, sub, 0),
                         reads=[gstg[ssl].k] + [x.k for x in x1T],
                         writes=[PSK[0]])
                    S.op("tensor", lambda E, ssl=ssl, usl=usl, sub=sub: gufn(
                        E, ssl, usl, sub, 1),
                         reads=[ustg[usl].k] + [x.k for x in x1T],
                         writes=[PSK[1]])
                    S.op("scalar", lambda E, ic=ic: E.activation(
                        S3.ap, PS[0].ap()[:, :], AF.Silu,
                        scale=gs_sc[:, ic:ic + 1]),
                         reads=[PSK[0], gs_sc.k], writes=[S3.k])
                    S.op("vector", lambda E, ic=ic, hs=hs, il=il: E.scalar_tensor_tensor(
                        hact[hs][:, il * 1024:(il + 1) * 1024],
                        PS[1].ap()[:, :], us_sc[:, ic:ic + 1], S3.ap,
                        ALU.mult, ALU.mult),
                         reads=[PSK[1], us_sc.k, S3.k], writes=[hact[hs].k])
                if g > 0:
                    down_block(g - 1, 2 * ib)
                    down_block(g - 1, 2 * ib + 1)
        for cb in range(8):
            down_block(7, cb)

        if dump == "ffn":
            _emit_dump(S, AR_F, DUMP)

        st_ops = []

        def ln2_dst(hc):
            outv = YS if hc % 2 == 0 else S4
            tmp = S3 if hc % 2 == 0 else QAf
            S.op("vector", lambda E, hc=hc, tmp=tmp: E.tensor_tensor(
                tmp.ap, zz[hc].ap, S1.ap, ALU.mult),
                 reads=[zz[hc].k, S1.k], writes=[tmp.k])
            S.op("vector", lambda E, hc=hc, outv=outv, tmp=tmp: E.tensor_tensor(
                outv.ap, tmp.ap, S2.ap, ALU.add),
                 reads=[tmp.k, S2.k], writes=[outv.k])
            st_ops.append(S.op("sync", lambda E, hc=hc, outv=outv: E.dma_start(
                out=OUT[hc * 128:(hc + 1) * 128, :], in_=outv.ap),
                reads=[outv.k], dma=True))
        layernorm(zz, ln2_dst, "ln2")

        S.op("sync", lambda E: E.nop(), deps=st_ops)

        block = ctx.enter_context(nc.Block())
        S.emit(block, csem, dsem)
    return nc


def _emit_dump(S, AR_F, DUMP):
    for hc in range(16):
        w = AR_F.shape[1] // 16 if hc < 15 else AR_F.shape[1] - 15 * (AR_F.shape[1] // 16)
        c0 = hc * (AR_F.shape[1] // 16)
        S.op("sync", lambda E, c0=c0, w=w: E.dma_start(
            out=DUMP[:, c0:c0 + w], in_=AR_F[:, c0:c0 + w]),
            reads=[tuple(("A", u) for u in range(0, 120))], dma=True)


def _prep(inp):
    x = np.asarray(inp["L0_attn_x"], np.float32)
    Wq = np.asarray(inp["L0_attn_Wq"], np.float32)
    Wk = np.asarray(inp["L0_attn_Wk"], np.float32)
    Wv = np.asarray(inp["L0_attn_Wv"], np.float32)
    mem = np.ascontiguousarray(np.asarray(inp["L0_attn_memory"], np.float32))
    mnorm = np.ascontiguousarray(
        np.asarray(inp["L0_attn_memory_norm"], np.float32).reshape(512, 1))
    mw = np.asarray(inp["L0_attn_memory_weight"], np.float32).reshape(1, 1)
    gw = np.asarray(inp["L0_ffn_gate_w"], np.int32)
    uw = np.asarray(inp["L0_ffn_up_w"], np.int32)
    dw = np.asarray(inp["L0_ffn_down_w"], np.int32)
    gw8 = np.ascontiguousarray(gw.view(np.uint8).reshape(8192, 2048, 4)[:, :, 0])
    uw8 = np.ascontiguousarray(uw.view(np.uint8).reshape(8192, 2048, 4)[:, :, 0])
    dw8 = np.ascontiguousarray(dw.view(np.uint8).reshape(2048, 8192, 4)[:, :, 0])
    gs = np.asarray(inp["L0_ffn_gate_scale"], np.float32)
    us = np.asarray(inp["L0_ffn_up_scale"], np.float32)
    ds = np.asarray(inp["L0_ffn_down_scale"], np.float32)

    common = dict(
        WQC=np.ascontiguousarray(
            Wq.reshape(16, 128, 16, 128).transpose(0, 3, 2, 1)
            .reshape(16, 128, 2048)),
        WKC=np.ascontiguousarray(
            Wk.reshape(4, 128, 16, 128).transpose(0, 3, 2, 1)
            .reshape(4, 128, 2048)),
        WVC=np.ascontiguousarray(
            Wv.reshape(4, 128, 16, 128).transpose(0, 3, 2, 1)
            .reshape(4, 128, 2048)),
        MEM=mem, MNORM=mnorm, MW=np.ascontiguousarray(mw),
        GCB=np.ascontiguousarray(
            gw8.reshape(32, 256, 16, 128).transpose(0, 3, 2, 1)
            .reshape(32, 128, 4096)),
        UCB=np.ascontiguousarray(
            uw8.reshape(32, 256, 16, 128).transpose(0, 3, 2, 1)
            .reshape(32, 128, 4096)),
        DCB=np.ascontiguousarray(
            dw8.reshape(8, 256, 8, 8, 128).transpose(0, 2, 4, 3, 1)
            .reshape(8, 8, 128, 2048)),
        GS=np.ascontiguousarray(gs.reshape(64, 128)),
        US=np.ascontiguousarray(us.reshape(64, 128)),
        DS=np.ascontiguousarray(ds.reshape(16, 128)),
    )
    in_maps = []
    xTs = {}
    for c in range(8):
        beta, half = c // 2, c % 2
        if beta not in xTs:
            xTs[beta] = np.ascontiguousarray(x[beta].T)
        xT = xTs[beta]
        if half == 0:
            XT = xT
        else:
            XT = np.ascontiguousarray(
                np.concatenate([xT[:, 1024:], xT[:, :1024]], axis=1))
        m = dict(common)
        m["XT"] = XT
        in_maps.append(m)
    return in_maps


def kernel(**inputs):
    in_maps = _prep(inputs)
    nc = build_nc()
    res = run_bass_kernel_spmd(nc, in_maps, core_ids=list(range(8)))
    out = np.empty((4, 2048, 2048), np.float32)
    for c in range(8):
        beta, half = c // 2, c % 2
        out[beta, half * 1024:(half + 1) * 1024, :] = res.results[c]["OUT"].T
    return out


# version 27
# speedup vs baseline: 1.0468x; 1.0132x over previous
import numpy as np
import contextlib
import concourse.bass as bass
import concourse.mybir as mybir
from concourse.bass_utils import run_bass_kernel_spmd

F32 = mybir.dt.float32
BF16 = mybir.dt.bfloat16
I32 = mybir.dt.int32
U8 = mybir.dt.uint8
AF = mybir.ActivationFunctionType
ALU = mybir.AluOpType
AX = mybir.AxisListType.X

ENGINES = ("tensor", "scalar", "vector", "gpsimd", "sync")


class _Op:
    __slots__ = ("eng", "fn", "deps", "dma", "ord", "awaited", "idx")


class Sched:

    def __init__(self):
        self.ops = []
        self.state = {}

    def op(self, eng, fn, reads=(), writes=(), deps=(), dma=False):
        o = _Op()
        o.eng, o.fn, o.dma = eng, fn, dma
        o.awaited, o.ord = False, None
        o.idx = len(self.ops)
        dep = set()
        for d in deps:
            if d is not None:
                dep.add(d)
        for ks in reads:
            for k in ks:
                st = self.state.get(k)
                if st and st[0] is not None:
                    dep.add(st[0])
        for ks in writes:
            for k in ks:
                st = self.state.get(k)
                if st:
                    if st[0] is not None:
                        dep.add(st[0])
                    dep.update(st[1])
        o.deps = list(dep)
        self.ops.append(o)
        for ks in reads:
            for k in ks:
                self.state.setdefault(k, [None, []])[1].append(o)
        for ks in writes:
            for k in ks:
                self.state[k] = [o, []]
        return o

    def emit(self, block, csem, dsem, nring=8):
        for o in self.ops:
            for d in o.deps:
                d.awaited = True
        ccnt = {e: 0 for e in ENGINES}
        dcnt = {e: 0 for e in ENGINES}
        for o in self.ops:
            if o.dma:
                i = dcnt[o.eng]
                dcnt[o.eng] += 1
                o.ord = (i % nring, 16 * (i // nring + 1), i)
                o.awaited = True
            elif o.awaited:
                ccnt[o.eng] += 1
                o.ord = ccnt[o.eng]

        def mk(eng):
            my_ops = [o for o in self.ops if o.eng == eng]

            def body(E):
                waited = {}
                for o in my_ops:
                    best = {}
                    for d in o.deps:
                        if d.dma:
                            k = (d.eng, "d", d.ord[0])
                            if k not in best or best[k].idx < d.idx:
                                best[k] = d
                        else:
                            k = (d.eng, "c")
                            if k not in best or best[k].idx < d.idx:
                                best[k] = d
                    for k, d in best.items():
                        if d.dma:
                            sem, val = dsem[d.eng][d.ord[0]], d.ord[1]
                        else:
                            sem, val = csem[d.eng], d.ord
                        if waited.get(k, 0) < val:
                            E.wait_ge(sem, val)
                            waited[k] = val
                    if o.dma:
                        slot, val, i = o.ord
                        if i >= nring:
                            k = (eng, "d", slot)
                            if waited.get(k, 0) < val - 16:
                                E.wait_ge(dsem[eng][slot], val - 16)
                                waited[k] = val - 16
                    inst = o.fn(E)
                    if o.dma:
                        inst.then_inc(dsem[eng][o.ord[0]], 16)
                    elif o.awaited:
                        inst.then_inc(csem[eng], 1)
            return body

        for eng in ENGINES:
            getattr(block, eng)(mk(eng))


L = {}


def _lay(name, nbytes, align=64):
    off = _lay.cur
    off = (off + align - 1) // align * align
    L[name] = off
    _lay.cur = off + nbytes
    return off


_lay.cur = 0
_lay("ident_bf", 256)
_lay("ident_f", 512)
_lay("ones_f", 512)
_lay("ones_bf", 256)
_lay("bias_all", 128)
_lay("brow", 384)
_lay("negr_bf", 64)
_lay("maxsq", 64)
_lay("maxk", 64)
_lay("gs_sc", 256)
_lay("us_sc", 256)
_lay("ds_sc", 64)
_lay("wcst", 32)
_lay("scnat", 512)
_lay("mn_bf", 16)
_lay("qmask", 128)
_lay("YS", 4096)
_lay("S1", 4096)
_lay("S2", 4096)
_lay("S3", 4096)
_lay("S4", 4096)
_lay("rowbf", 2048)
_lay("QA", 2048)
_lay("QB", 2048)
_lay("P2", 32768)
_lay("P3", 16384 + 32768)
_lay("P4", 65536)
_lay("P5", 32768)
AR_BYTES = (_lay.cur + 63) // 64 * 64
assert AR_BYTES <= 212800, AR_BYTES

D_KEY_UNIT = 2048


def _keys(off, nbytes):
    return tuple(("A", u) for u in range(off // D_KEY_UNIT,
                                         (off + nbytes - 1) // D_KEY_UNIT + 1))


class TV:

    def __init__(self, ap, keys):
        self.ap = ap
        self.k = keys

    def __getitem__(self, sl):
        return self.ap[sl]


def build_nc(dump=None):
    nc = bass.Bass("TRN2", target_bir_lowering=False)

    XT = nc.dram_tensor("XT", [2048, 2048], F32, kind="ExternalInput").ap()
    WQC = nc.dram_tensor("WQC", [16, 128, 2048], F32, kind="ExternalInput").ap()
    WKC = nc.dram_tensor("WKC", [4, 128, 2048], F32, kind="ExternalInput").ap()
    WVC = nc.dram_tensor("WVC", [4, 128, 2048], F32, kind="ExternalInput").ap()
    MEM = nc.dram_tensor("MEM", [512, 512], F32, kind="ExternalInput").ap()
    MNORM = nc.dram_tensor("MNORM", [512, 1], F32, kind="ExternalInput").ap()
    MW = nc.dram_tensor("MW", [1, 1], F32, kind="ExternalInput").ap()
    GCB = nc.dram_tensor("GCB", [32, 128, 4096], U8, kind="ExternalInput").ap()
    UCB = nc.dram_tensor("UCB", [32, 128, 4096], U8, kind="ExternalInput").ap()
    DCB = nc.dram_tensor("DCB", [8, 8, 128, 2048], U8, kind="ExternalInput").ap()
    GS = nc.dram_tensor("GS", [64, 128], F32, kind="ExternalInput").ap()
    US = nc.dram_tensor("US", [64, 128], F32, kind="ExternalInput").ap()
    DS = nc.dram_tensor("DS", [16, 128], F32, kind="ExternalInput").ap()
    OUT = nc.dram_tensor("OUT", [2048, 1024], F32, kind="ExternalOutput").ap()
    DUMP = None
    if dump is not None:
        DUMP = nc.dram_tensor("DUMP", [128, AR_BYTES // 4], F32,
                              kind="ExternalOutput").ap()

    with contextlib.ExitStack() as ctx:
        AR = ctx.enter_context(nc.sbuf_tensor("AR", [128, AR_BYTES // 4], F32))
        ARB = AR.bitcast(BF16)
        AR_F = AR.ap()
        AR_B = ARB.ap()
        PS = [ctx.enter_context(nc.psum_tensor("PS%d" % i, [128, 1024], F32))
              for i in range(4)]
        PSB = [p.bitcast(BF16) for p in PS]
        PSK = [(("PS", i),) for i in range(4)]

        csem = {e: ctx.enter_context(nc.semaphore("c_" + e)) for e in ENGINES}
        dsem = {e: [ctx.enter_context(nc.semaphore("d_%s%d" % (e, i)))
                    for i in range(8)]
                for e in ("gpsimd", "sync", "scalar")}

        def vf(name, cols, off_extra=0):
            off = L[name] + off_extra
            return TV(AR_F[:, off // 4: off // 4 + cols], _keys(off, cols * 4))

        def vb(name, cols, off_extra=0):
            off = L[name] + off_extra
            return TV(AR_B[:, off // 2: off // 2 + cols], _keys(off, cols * 2))

        ident_bf = vb("ident_bf", 128)
        ident_f = vf("ident_f", 128)
        ones_f = vf("ones_f", 128)
        ones_bf = vb("ones_bf", 128)
        bias_all = vf("bias_all", 32)
        brow = vf("brow", 96)
        negr_bf = vb("negr_bf", 32)
        maxsq = vf("maxsq", 16)
        maxk = vf("maxk", 16)
        gs_sc = vf("gs_sc", 64)
        us_sc = vf("us_sc", 64)
        ds_sc = vf("ds_sc", 16)
        wcst = vf("wcst", 8)
        scnat = vf("scnat", 128)
        mn_bf = vb("mn_bf", 8)
        qmask = vb("qmask", 33)
        YS = vf("YS", 1024)
        y1b = vb("YS", 1024)
        sqb = vb("YS", 1024, off_extra=2048)
        S1 = vf("S1", 1024)
        S2 = vf("S2", 1024)
        S3 = vf("S3", 1024)
        S4 = vf("S4", 1024)
        KSQ = TV(AR_F[:, L["S1"] // 4: L["S1"] // 4 + 2048],
                 _keys(L["S1"], 8192))
        rowbf = vb("rowbf", 1024)

        QT = [vb("P2", 1024, off_extra=i * 2048) for i in range(16)]
        x1T = QT
        QAb = vb("QA", 1024)
        QBb = vb("QB", 1024)
        khat = [vb("P3", 2048, off_extra=kc * 4096) for kc in range(4)]
        vhat = [vb("P3", 2048, off_extra=16384 + h * 4096) for h in range(8)]
        xbT = [vb("P4", 2048, off_extra=hc * 4096) for hc in range(16)]
        y1 = [vf("P4", 1024, off_extra=hc * 4096) for hc in range(16)]
        zz = y1
        NWS = 3
        wslot_f = [vf("P5", 2048, off_extra=s * 8192) for s in range(NWS)]
        wslot = [vb("P5", 2048, off_extra=s * 8192) for s in range(NWS)]
        vt = [vb("S1", 2048), vb("S2", 2048)]
        mem_bf = vb("P5", 2048, off_extra=24576)
        expT = [vb("P5", 1024, off_extra=s * 2048) for s in range(8)]
        sig = [vb("P5", 1024, off_extra=16384 + s * 2048) for s in range(4)]
        hact = [vb("P5", 8192, off_extra=s * 16384) for s in range(2)]
        AR_I = AR.bitcast(I32).ap()

        def vi(name, cols, off_extra=0):
            off = L[name] + off_extra
            return TV(AR_I[:, off // 4: off // 4 + cols], _keys(off, cols * 4))

        AR_U = AR.bitcast(U8).ap()

        def vu(name, cols, off_extra=0):
            off = L[name] + off_extra
            return TV(AR_U[:, off: off + cols], _keys(off, cols))

        NGS = 2
        NDS = 3
        gstg_u = [vu("P3", 4096, off_extra=s * 8192 + 4096) for s in range(NGS)]
        gstg = [vb("P3", 4096, off_extra=s * 8192) for s in range(NGS)]
        ustg_u = [vu("P3", 4096, off_extra=16384 + s * 8192 + 4096) for s in range(NGS)]
        ustg = [vb("P3", 4096, off_extra=16384 + s * 8192) for s in range(NGS)]
        dstg_u = [vu("P3", 2048, off_extra=32768 + s * 4096 + 2048) for s in range(NDS)]
        dstg = [vb("P3", 2048, off_extra=32768 + s * 4096) for s in range(NDS)]

        S = Sched()

        if dump is not None:
            S.op("gpsimd", lambda E: E.memset(AR_F[:, :], 0.0),
                 writes=[tuple(("A", u) for u in range(0, 120))])
        S.op("gpsimd", lambda E: E.memset(ones_f.ap, 1.0), writes=[ones_f.k])
        S.op("gpsimd", lambda E: E.memset(ones_bf.ap, 1.0), writes=[ones_bf.k])
        S.op("gpsimd", lambda E: E.memset(wcst[0:128, 4:5], -128.0),
             writes=[wcst.k])
        S.op("gpsimd", lambda E: E.memset(QAb[64:128, :], 0.0), writes=[QAb.k])
        S.op("gpsimd", lambda E: E.memset(QBb[0:64, :], 0.0), writes=[QBb.k])
        S.op("gpsimd", lambda E: E.memset(qmask.ap, 0.0), writes=[qmask.k])
        S.op("gpsimd", lambda E: E.memset(qmask[0:64, 0:1], 1.0),
             writes=[qmask.k])
        S.op("gpsimd", lambda E: E.memset(qmask[64:128, 32:33], 1.0),
             writes=[qmask.k])
        S.op("gpsimd", lambda E: E.affine_select(
            ident_bf.ap, ones_bf.ap, [[-1, 128]], ALU.is_equal, 0.0,
            base=0, channel_multiplier=1), reads=[ones_bf.k], writes=[ident_bf.k])
        S.op("gpsimd", lambda E: E.affine_select(
            ident_f.ap, ones_f.ap, [[-1, 128]], ALU.is_equal, 0.0,
            base=0, channel_multiplier=1), reads=[ones_f.k], writes=[ident_f.k])

        S.op("sync", lambda E: E.dma_start(out=wcst[0:1, 0:1], in_=MW[:, :]),
             writes=[wcst.k], dma=True)
        S.op("sync", lambda E: E.dma_start(out=wcst[64:65, 0:1], in_=MW[:, :]),
             writes=[wcst.k], dma=True)
        for b in (0, 64):
            S.op("scalar", lambda E, b=b: E.activation(
                wcst[b:b + 1, 1:2], wcst[b:b + 1, 0:1], AF.Sigmoid),
                reads=[wcst.k], writes=[wcst.k])
            S.op("vector", lambda E, b=b: E.tensor_scalar(
                wcst[b:b + 1, 2:3], wcst[b:b + 1, 1:2], -1.0, 1.0,
                ALU.mult, ALU.add), reads=[wcst.k], writes=[wcst.k])
            S.op("scalar", lambda E, b=b: E.activation(
                wcst[b:b + 1, 3:4], wcst[b:b + 1, 2:3], AF.Ln),
                reads=[wcst.k], writes=[wcst.k])

        for (nat_rows, DR, dst) in ((64, GS, gs_sc), (64, US, us_sc),
                                    (16, DS, ds_sc)):
            S.op("sync", lambda E, DR=DR, r=nat_rows: E.dma_start(
                out=scnat[0:r, 0:128], in_=DR[:, :]), writes=[scnat.k], dma=True)
            S.op("tensor", lambda E, r=nat_rows: E.transpose(
                PS[0].ap()[0:128, 0:r], scnat[0:r, 0:128], ident_f[0:r, 0:r]),
                reads=[scnat.k, ident_f.k], writes=[PSK[0]])
            S.op("vector", lambda E, dst=dst, r=nat_rows: E.tensor_copy(
                dst[0:128, 0:r], PS[0].ap()[0:128, 0:r]),
                reads=[PSK[0]], writes=[dst.k])

        S.op("gpsimd", lambda E: E.dma_start(
            out=mem_bf.ap, in_=MEM.rearrange("(c p) j -> p c j", p=128)),
            writes=[mem_bf.k], dma=True)
        S.op("sync", lambda E: E.dma_start(
            out=scnat[0:4, 0:128], in_=MNORM.rearrange("(c p) o -> c (p o)", p=128)),
            writes=[scnat.k], dma=True)
        S.op("tensor", lambda E: E.transpose(
            PS[0].ap()[0:128, 0:4], scnat[0:4, 0:128], ident_f[0:4, 0:4]),
            reads=[scnat.k, ident_f.k], writes=[PSK[0]])
        S.op("vector", lambda E: E.tensor_copy(mn_bf[0:128, 0:4],
                                               PS[0].ap()[0:128, 0:4]),
             reads=[PSK[0]], writes=[mn_bf.k])

        xstg = [TV(AR_F[:, L["S1"] // 4: L["S1"] // 4 + 2048],
                   _keys(L["S1"], 8192)),
                TV(AR_F[:, L["S3"] // 4: L["S3"] // 4 + 2048],
                   _keys(L["S3"], 8192))]
        for hc in range(16):
            xsl = hc % 2
            ring = "sync" if xsl == 0 else "gpsimd"
            S.op(ring, lambda E, hc=hc, xsl=xsl: E.dma_start(
                out=xstg[xsl].ap, in_=XT[hc * 128:(hc + 1) * 128, :]),
                writes=[xstg[xsl].k], dma=True)
            S.op("scalar", lambda E, hc=hc, xsl=xsl: E.activation(
                xbT[hc].ap, xstg[xsl].ap, AF.Copy),
                 reads=[xstg[xsl].k], writes=[xbT[hc].k])

        _wl = [0]

        def wload(DRC, idx):
            slot = _wl[0] % NWS
            _wl[0] += 1
            op = S.op("scalar", lambda E, slot=slot: E.dma_start(
                out=wslot_f[slot].ap, in_=DRC[idx]),
                writes=[wslot_f[slot].k], dma=True)
            S.op("scalar", lambda E, slot=slot: E.activation(
                wslot[slot].ap, wslot_f[slot].ap, AF.Copy),
                 reads=[wslot_f[slot].k], writes=[wslot_f[slot].k])
            return slot, op

        def proj_group(slot, ps_lo, ps_hi, tq_n, psk_writes):
            for hc in range(16):
                def fn(E, hc=hc):
                    inst = None
                    for tq in range(tq_n):
                        p = PS[ps_lo] if tq < 2 else PS[ps_hi]
                        col = (tq % 2) * 512
                        inst = E.matmul(
                            p.ap()[:, col:col + 512],
                            wslot[slot][:, hc * 128:(hc + 1) * 128],
                            xbT[hc][:, tq * 512:(tq + 1) * 512],
                            start=(hc == 0), stop=(hc == 15))
                    return inst
                S.op("tensor", fn, reads=[wslot[slot].k, xbT[hc].k],
                     writes=psk_writes)

        for kc in range(4):
            sl, _ = wload(WKC, kc)
            pl = 2 * (kc % 2)
            proj_group(sl, pl, pl + 1, 4, [PSK[pl], PSK[pl + 1]])
            for half in range(2):
                S.op("vector", lambda E, kc=kc, half=half, pl=pl: E.tensor_copy(
                    khat[kc][:, half * 1024:(half + 1) * 1024],
                    PS[pl + half].ap()[:, :]),
                    reads=[PSK[pl + half]], writes=[khat[kc].k])
        for kc in range(4):
            sl, _ = wload(WVC, kc)
            vsl = kc % 2
            pl = 2 * (kc % 2)
            proj_group(sl, pl, pl + 1, 4, [PSK[pl], PSK[pl + 1]])
            for half in range(2):
                S.op("vector", lambda E, half=half, vsl=vsl, pl=pl: E.tensor_copy(
                    vt[vsl][:, half * 1024:(half + 1) * 1024],
                    PS[pl + half].ap()[:, :]),
                    reads=[PSK[pl + half]], writes=[vt[vsl].k])
            for hsub in range(2):
                h = 2 * kc + hsub
                b = hsub * 64
                pt = 2 + (kc % 2)
                def trf(E, b=b, sl=vsl, pt=pt):
                    inst = None
                    for tch in range(16):
                        inst = E.transpose(
                            PSB[pt].ap()[0:128, tch * 64:(tch + 1) * 64],
                            vt[sl][b:b + 64, tch * 128:(tch + 1) * 128],
                            ident_bf[b:b + 64, b:b + 64])
                    return inst
                S.op("tensor", trf, reads=[vt[vsl].k, ident_bf.k],
                     writes=[PSK[pt]])
                S.op("vector", lambda E, h=h, pt=pt: E.tensor_copy(
                    vhat[h].ap.rearrange("p (c w) -> p c w", w=128)[:, :, 0:64],
                    PSB[pt].ap()[0:128, 0:1024].rearrange("p (c w) -> p c w", w=64)),
                    reads=[PSK[pt]], writes=[vhat[h].k])
                S.op("gpsimd", lambda E, h=h: E.memset(
                    vhat[h].ap.rearrange("p (c w) -> p c w", w=128)[:, :, 64:65],
                    1.0), writes=[vhat[h].k])
                S.op("gpsimd", lambda E, h=h: E.memset(
                    vhat[h].ap.rearrange("p (c w) -> p c w", w=128)[:, :, 65:128],
                    0.0), writes=[vhat[h].k])
        sqA = vb("S3", 1024)
        sqB = vb("YS", 1024)
        KSQB = vb("S1", 2048)

        def qbias(oc):
            sq = sqA if oc % 2 == 0 else sqB
            pb = 1 + 2 * (oc % 2)
            S.op("vector", lambda E, oc=oc, sq=sq: E.tensor_tensor(
                sq.ap, QT[oc].ap, QT[oc].ap, ALU.mult),
                reads=[QT[oc].k], writes=[sq.k])
            def qrowfn(E, pb=pb, sq=sq):
                inst = None
                for th in range(2):
                    inst = E.matmul(
                        PS[pb].ap()[0:33, th * 512:(th + 1) * 512],
                        qmask[0:128, 0:33],
                        sq[0:128, th * 512:(th + 1) * 512],
                        start=True, stop=True)
                return inst
            S.op("tensor", qrowfn, reads=[qmask.k, sq.k], writes=[PSK[pb]])
            S.op("vector", lambda E, oc=oc, pb=pb: E.reduce_max(
                brow[0:33, oc:oc + 1], PS[pb].ap()[0:33, 0:1024], axis=AX),
                reads=[PSK[pb]], writes=[brow.k])

        def kbias(kc):
            S.op("vector", lambda E, kc=kc: E.tensor_tensor(
                KSQB.ap, khat[kc].ap, khat[kc].ap, ALU.mult),
                reads=[khat[kc].k], writes=[KSQB.k])
            for tha in range(2):
                pb = 1 + 2 * tha
                def krowfn(E, pb=pb, tha=tha):
                    inst = None
                    for tq in range(2):
                        inst = E.matmul(
                            PS[pb].ap()[0:33, tq * 512:(tq + 1) * 512],
                            qmask[0:128, 0:33],
                            KSQB.ap[0:128, (2 * tha + tq) * 512:
                                    (2 * tha + tq + 1) * 512],
                            start=True, stop=True)
                    return inst
                S.op("tensor", krowfn, reads=[qmask.k, KSQB.k],
                     writes=[PSK[pb]])
                S.op("vector", lambda E, kc=kc, pb=pb, tha=tha: E.reduce_max(
                    brow[0:33, 16 + 4 * tha + kc:17 + 4 * tha + kc],
                    PS[pb].ap()[0:33, 0:1024], axis=AX),
                    reads=[PSK[pb]], writes=[brow.k])
            S.op("vector", lambda E, kc=kc: E.tensor_tensor(
                brow[0:33, 24 + kc:25 + kc], brow[0:33, 16 + kc:17 + kc],
                brow[0:33, 20 + kc:21 + kc], ALU.max),
                reads=[brow.k], writes=[brow.k])

        for oc in range(16):
            sl, _ = wload(WQC, oc)
            pp = 2 * (oc % 2)
            proj_group(sl, pp, pp + 1, 2, [PSK[pp]])
            S.op("vector", lambda E, oc=oc, pp=pp: E.tensor_copy(
                QT[oc].ap, PS[pp].ap()[:, :]),
                reads=[PSK[pp]], writes=[QT[oc].k])
            qbias(oc)
            if oc < 4:
                kbias(oc)

        for hc in range(16):
            S.op("sync", lambda E, hc=hc: E.dma_start(
                out=y1[hc].ap, in_=XT[hc * 128:(hc + 1) * 128, 0:1024]),
                writes=[y1[hc].k], dma=True)
        def prodfn(E):
            inst = None
            for oc in range(16):
                kc = oc % 4
                inst = E.tensor_tensor(brow[0:33, 28 + oc:29 + oc],
                                       brow[0:33, oc:oc + 1],
                                       brow[0:33, 24 + kc:25 + kc], ALU.mult)
            return inst
        S.op("vector", prodfn, reads=[brow.k], writes=[brow.k])
        S.op("vector", lambda E: E.memset(negr_bf[0:33, 0:32], 0.0),
             writes=[negr_bf.k])
        for par, sub in ((0, 0), (64, 1)):
            pr = 32 if par else 0
            S.op("scalar", lambda E, pr=pr: E.activation(
                S4[pr:pr + 1, 0:16], brow[pr:pr + 1, 28:44], AF.Ln),
                 reads=[brow.k], writes=[S4.k])
            S.op("scalar", lambda E, pr=pr: E.activation(
                S4[pr:pr + 1, 16:32], S4[pr:pr + 1, 0:16], AF.Exp, scale=0.5),
                 reads=[S4.k], writes=[S4.k])
            S.op("vector", lambda E, pr=pr: E.tensor_scalar(
                S4[pr:pr + 1, 32:48], S4[pr:pr + 1, 16:32], -0.125, None,
                ALU.mult), reads=[S4.k], writes=[S4.k])
            S.op("vector", lambda E, pr=pr, sub=sub: E.tensor_copy(
                negr_bf.ap.rearrange("p (a b) -> p a b", b=2)[pr:pr + 1, :, sub:sub + 1],
                S4[pr:pr + 1, 32:48]),
                 reads=[S4.k], writes=[negr_bf.k])
        def bcastfn(E):
            E.matmul(PS[3].ap()[0:128, 0:32], ones_bf[0:1, 0:128],
                     negr_bf[0:1, 0:32], start=True, stop=False)
            return E.matmul(PS[3].ap()[0:128, 0:32], ones_bf[32:33, 0:128],
                            negr_bf[32:33, 0:32], start=False, stop=True)
        S.op("tensor", bcastfn, reads=[ones_bf.k, negr_bf.k], writes=[PSK[3]])
        S.op("vector", lambda E: E.tensor_copy(bias_all.ap,
                                               PS[3].ap()[0:128, 0:32]),
             reads=[PSK[3]], writes=[bias_all.k])

        if dump == "qkv":
            _emit_dump(S, AR_F, DUMP)

        for g in range(4):
            for c in range(4):
                oc = 4 * g + c
                sA, sB = (S1, S2) if c % 2 == 0 else (S3, YS)
                S.op("vector", lambda E, oc=oc, sA=sA: E.tensor_scalar_min(
                    sA.ap, QT[oc].ap, 0.0), reads=[QT[oc].k], writes=[sA.k])
                S.op("scalar", lambda E, sA=sA, sB=sB: E.activation(
                    sB.ap, sA.ap, AF.Exp), reads=[sA.k], writes=[sB.k])
                S.op("vector", lambda E, oc=oc, sA=sA: E.tensor_scalar_max(
                    sA.ap, QT[oc].ap, 0.0), reads=[QT[oc].k], writes=[sA.k])
                S.op("vector", lambda E, c=c, sA=sA, sB=sB: E.scalar_tensor_tensor(
                    sig[c].ap, sB.ap, -1.0, sA.ap, ALU.add, ALU.add),
                     reads=[sA.k, sB.k], writes=[sig[c].k])
            def nrmfn(E):
                inst = None
                for th in range(2):
                    for c in range(4):
                        inst = E.matmul(
                            PS[3].ap()[0:1, th * 512:(th + 1) * 512],
                            mn_bf[0:128, c:c + 1],
                            sig[c][:, th * 512:(th + 1) * 512],
                            start=(c == 0), stop=(c == 3))
                return inst
            S.op("tensor", nrmfn, reads=[mn_bf.k] + [t.k for t in sig],
                 writes=[PSK[3]])
            S.op("vector", lambda E: E.reciprocal(S4[0:1, 0:1024],
                                                  PS[3].ap()[0:1, :]),
                 reads=[PSK[3]], writes=[S4.k])
            S.op("vector", lambda E: E.tensor_scalar(
                rowbf[0:1, 0:1024], S4[0:1, 0:1024], wcst[0:1, 1:2], None,
                ALU.mult), reads=[S4.k, wcst.k], writes=[rowbf.k])
            def psi2fn(E):
                inst = None
                for th in range(2):
                    inst = E.matmul(PS[3].ap()[0:128, th * 512:(th + 1) * 512],
                                    ones_bf[0:1, 0:128],
                                    rowbf[0:1, th * 512:(th + 1) * 512],
                                    start=True, stop=True)
                return inst
            S.op("tensor", psi2fn, reads=[ones_bf.k, rowbf.k], writes=[PSK[3]])
            S.op("vector", lambda E: E.tensor_copy(S3.ap, PS[3].ap()[:, :]),
                 reads=[PSK[3]], writes=[S3.k])
            for jc in range(4):
                pj = jc % 2
                def amfn(E, pj=pj, jc=jc):
                    inst = None
                    for th in range(2):
                        for c in range(4):
                            inst = E.matmul(
                                PS[pj].ap()[:, th * 512:(th + 1) * 512],
                                mem_bf[:, c * 512 + jc * 128: c * 512 + (jc + 1) * 128],
                                sig[c][:, th * 512:(th + 1) * 512],
                                start=(c == 0), stop=(c == 3))
                    return inst
                S.op("tensor", amfn, reads=[mem_bf.k] + [t.k for t in sig],
                     writes=[PSK[pj]])
                tmp = S1 if jc % 2 == 0 else S2
                S.op("vector", lambda E, pj=pj, tmp=tmp: E.tensor_tensor(
                    tmp.ap, PS[pj].ap()[:, :], S3.ap, ALU.mult),
                     reads=[PSK[pj], S3.k], writes=[tmp.k])
                hc = 4 * g + jc
                S.op("vector", lambda E, hc=hc, tmp=tmp: E.tensor_tensor(
                    y1[hc].ap, y1[hc].ap, tmp.ap, ALU.add),
                     reads=[tmp.k, y1[hc].k], writes=[y1[hc].k])

        if dump == "amem":
            _emit_dump(S, AR_F, DUMP)

        def emit_qk(p, tch):
            q, h = p // 8, p % 8
            oc, b, kc = q * 4 + h // 2, (h % 2) * 64, h // 2
            G = p * 16 + tch
            pc = 2 * (G % 2)
            sl = G % 8
            qh = QAb if b == 0 else QBb
            def qkfn(E, kc=kc, tch=tch, pc=pc, qh=qh):
                inst = None
                for sh in range(2):
                    inst = E.matmul(
                        PS[pc].ap()[:, sh * 512:(sh + 1) * 512],
                        khat[kc][0:128, tch * 128:(tch + 1) * 128],
                        qh[0:128, sh * 512:(sh + 1) * 512],
                        start=True, stop=True)
                return inst
            S.op("tensor", qkfn, reads=[khat[kc].k, qh.k], writes=[PSK[pc]])
            S.op("scalar", lambda E, sl=sl, p=p, pc=pc: E.activation(
                expT[sl].ap, PS[pc].ap()[:, :], AF.Exp,
                scale=0.125, bias=bias_all[:, p:p + 1]),
                 reads=[PSK[pc], bias_all.k], writes=[expT[sl].k])

        def emit_pv(p, tch):
            h = p % 8
            pa = 1 + 2 * (p % 2)
            sl = (p * 16 + tch) % 8
            def pvfn(E, h=h, tch=tch, sl=sl, pa=pa):
                inst = None
                for sh in range(2):
                    inst = E.matmul(
                        PS[pa].ap()[:, sh * 512:(sh + 1) * 512],
                        vhat[h][:, tch * 128: tch * 128 + 128],
                        expT[sl][:, sh * 512:(sh + 1) * 512],
                        start=(tch == 0), stop=(tch == 15))
                return inst
            S.op("tensor", pvfn, reads=[vhat[h].k, expT[sl].k],
                 writes=[PSK[pa]])

        def emit_tail(p, step):
            q, h = p // 8, p % 8
            oc, b = q * 4 + h // 2, (h % 2) * 64
            pa = 1 + 2 * (p % 2)
            if step == 0:
                S.op("scalar", lambda E, pa=pa: E.activation(
                    S4[64:65, 0:1024], PS[pa].ap()[64:65, :], AF.Ln),
                     reads=[PSK[pa]], writes=[S4.k])
                S.op("scalar", lambda E: E.activation(
                    rowbf[64:65, 0:1024], S4[64:65, 0:1024], AF.Exp,
                    scale=-1.0, bias=wcst[64:65, 3:4]),
                     reads=[S4.k, wcst.k], writes=[rowbf.k])
            elif step == 1:
                def psifn(E, pa=pa):
                    inst = None
                    for sh in range(2):
                        inst = E.matmul(
                            PS[pa].ap()[64:128, sh * 512:(sh + 1) * 512],
                            ones_bf[64:65, 0:64],
                            rowbf[64:65, sh * 512:(sh + 1) * 512],
                            start=True, stop=True, tile_position=(64, 64))
                    return inst
                S.op("tensor", psifn, reads=[ones_bf.k, rowbf.k],
                     writes=[PSK[pa]])
            elif step == 2:
                S.op("vector", lambda E, pa=pa: E.tensor_copy(
                    S2[0:64, 0:1024], PS[pa].ap()[64:128, :]),
                     reads=[PSK[pa]], writes=[S2.k])
            elif step == 3:
                if b == 0:
                    S.op("vector", lambda E, pa=pa: E.tensor_tensor(
                        S1[0:64, 0:1024], PS[pa].ap()[0:64, :],
                        S2[0:64, 0:1024], ALU.mult),
                         reads=[PSK[pa], S2.k], writes=[S1.k])
                else:
                    S.op("vector", lambda E, pa=pa: E.tensor_tensor(
                        rowbf[0:64, 0:1024], PS[pa].ap()[0:64, :],
                        S2[0:64, 0:1024], ALU.mult),
                         reads=[PSK[pa], S2.k], writes=[rowbf.k])
                    def shfn(E, pa=pa):
                        inst = None
                        for sh in range(2):
                            inst = E.matmul(
                                PS[pa].ap()[64:128, sh * 512:(sh + 1) * 512],
                                ident_bf[0:64, 0:64],
                                rowbf[0:64, sh * 512:(sh + 1) * 512],
                                start=True, stop=True, tile_position=(0, 64))
                        return inst
                    S.op("tensor", shfn, reads=[ident_bf.k, rowbf.k],
                         writes=[PSK[pa]])
            elif step == 4:
                if b == 0:
                    S.op("vector", lambda E, oc=oc: E.tensor_tensor(
                        y1[oc][0:64, :], y1[oc][0:64, :], S1[0:64, 0:1024],
                        ALU.add), reads=[S1.k, y1[oc].k], writes=[y1[oc].k])
                else:
                    S.op("vector", lambda E, oc=oc, pa=pa: E.tensor_tensor(
                        y1[oc][64:128, :], y1[oc][64:128, :],
                        PS[pa].ap()[64:128, :], ALU.add),
                         reads=[PSK[pa], y1[oc].k], writes=[y1[oc].k])

        PVLAG = 3
        TAIL_SLOTS = {5: 0, 7: 1, 9: 2, 11: 3, 13: 4}
        for p in range(32):
            if p % 2 == 0:
                oc0 = (p // 8) * 4 + (p % 8) // 2
                S.op("vector", lambda E, oc0=oc0: E.tensor_copy(
                    QAb[0:64, :], QT[oc0][0:64, :]),
                     reads=[QT[oc0].k], writes=[QAb.k])
                S.op("vector", lambda E, oc0=oc0: E.tensor_copy(
                    QBb[64:128, :], QT[oc0][64:128, :]),
                     reads=[QT[oc0].k], writes=[QBb.k])
            for tch in range(16):
                emit_qk(p, tch)
                G = p * 16 + tch
                if G >= PVLAG:
                    emit_pv((G - PVLAG) // 16, (G - PVLAG) % 16)
                if p >= 1 and tch in TAIL_SLOTS:
                    emit_tail(p - 1, TAIL_SLOTS[tch])
        for G in range(512 - PVLAG, 512):
            emit_pv(G // 16, G % 16)
        for step in range(5):
            emit_tail(31, step)

        if dump == "attn":
            _emit_dump(S, AR_F, DUMP)

        y1b2 = vb("S3", 1024)
        sqb2 = vb("S3", 1024, off_extra=2048)

        def ln_stats(src, hc, psA, psB):
            yb = y1b if hc % 2 == 0 else y1b2
            sb_ = sqb if hc % 2 == 0 else sqb2
            S.op("vector", lambda E, hc=hc, yb=yb: E.tensor_copy(
                yb.ap, src[hc].ap), reads=[src[hc].k], writes=[yb.k])
            S.op("vector", lambda E, yb=yb, sb_=sb_: E.tensor_tensor(
                sb_.ap, yb.ap, yb.ap, ALU.mult),
                 reads=[yb.k], writes=[sb_.k])
            def s12fn(E, hc=hc, yb=yb, sb_=sb_):
                inst = None
                for th in range(2):
                    inst = E.matmul(
                        PS[psA].ap()[0:1, th * 512:(th + 1) * 512],
                        ones_bf[0:128, 0:1],
                        yb[0:128, th * 512:(th + 1) * 512],
                        start=(hc == 0), stop=(hc == 15))
                    inst = E.matmul(
                        PS[psB].ap()[0:1, th * 512:(th + 1) * 512],
                        ones_bf[0:128, 0:1],
                        sb_[0:128, th * 512:(th + 1) * 512],
                        start=(hc == 0), stop=(hc == 15))
                return inst
            S.op("tensor", s12fn, reads=[yb.k, sb_.k, ones_bf.k],
                 writes=[PSK[psA], PSK[psB]])

        def ln_finish(dst_fn, psA, psB):
            S.op("vector", lambda E: E.tensor_scalar(
                S4[0:1, 0:1024], PS[psA].ap()[0:1, :], 1.0 / 2048.0, None,
                ALU.mult), reads=[PSK[psA]], writes=[S4.k])
            S.op("vector", lambda E: E.tensor_tensor(
                S2[0:1, 0:1024], S4[0:1, 0:1024], S4[0:1, 0:1024], ALU.mult),
                 reads=[S4.k], writes=[S2.k])
            S.op("vector", lambda E: E.scalar_tensor_tensor(
                S3[0:1, 0:1024], PS[psB].ap()[0:1, :], 1.0 / 2048.0,
                S2[0:1, 0:1024], ALU.mult, ALU.subtract),
                 reads=[PSK[psB], S2.k], writes=[S3.k])
            S.op("vector", lambda E: E.tensor_scalar(
                S3[0:1, 0:1024], S3[0:1, 0:1024], 1e-5, None, ALU.add),
                 reads=[S3.k], writes=[S3.k])
            S.op("scalar", lambda E: E.activation(
                S2[0:1, 0:1024], S3[0:1, 0:1024], AF.Ln),
                 reads=[S3.k], writes=[S2.k])
            S.op("scalar", lambda E: E.activation(
                S3[0:1, 0:1024], S2[0:1, 0:1024], AF.Exp, scale=-0.5),
                 reads=[S2.k], writes=[S3.k])
            S.op("vector", lambda E: E.scalar_tensor_tensor(
                S4[0:1, 0:1024], S4[0:1, 0:1024], -1.0, S3[0:1, 0:1024],
                ALU.mult, ALU.mult), reads=[S3.k, S4.k], writes=[S4.k])
            def bcast(row, psi):
                def fn(E):
                    inst = None
                    for th in range(2):
                        inst = E.matmul(
                            PS[psi].ap()[0:128, th * 512:(th + 1) * 512],
                            ones_f[0:1, 0:128],
                            row[0:1, th * 512:(th + 1) * 512],
                            start=True, stop=True)
                    return inst
                return fn
            S.op("tensor", bcast(S3.ap, 1), reads=[ones_f.k, S3.k],
                 writes=[PSK[1]])
            S.op("tensor", bcast(S4.ap, 2), reads=[ones_f.k, S4.k],
                 writes=[PSK[2]])
            S.op("vector", lambda E: E.tensor_copy(S1.ap, PS[1].ap()[:, :]),
                 reads=[PSK[1]], writes=[S1.k])
            S.op("vector", lambda E: E.tensor_copy(S2.ap, PS[2].ap()[:, :]),
                 reads=[PSK[2]], writes=[S2.k])
            for hc in range(16):
                dst_fn(hc)

        def layernorm(src, dst_fn, tag, psA=0, psB=3):
            for hc in range(16):
                ln_stats(src, hc, psA, psB)
            ln_finish(dst_fn, psA, psB)

        QAf = TV(AR_F[:, L["QA"] // 4: L["QA"] // 4 + 1024], _keys(L["QA"], 4096))

        def ln1_dst(hc):
            tmp = S3 if hc % 2 == 0 else QAf
            S.op("vector", lambda E, hc=hc, tmp=tmp: E.tensor_tensor(
                tmp.ap, y1[hc].ap, S1.ap, ALU.mult),
                 reads=[y1[hc].k, S1.k], writes=[tmp.k])
            S.op("vector", lambda E, hc=hc, tmp=tmp: E.tensor_tensor(
                x1T[hc].ap, tmp.ap, S2.ap, ALU.add),
                 reads=[tmp.k, S2.k], writes=[x1T[hc].k])
        layernorm(y1, ln1_dst, "ln1")

        if dump == "ln1":
            _emit_dump(S, AR_F, DUMP)

        def stage(DRC, idx, slots_u, slots_b, nsl, ring):
            sl = idx % nsl
            S.op(ring, lambda E, sl=sl, DRC=DRC: E.dma_start(
                out=slots_u[sl].ap, in_=DRC), writes=[slots_b[sl].k], dma=True)
            S.op("scalar", lambda E, sl=sl: E.activation(
                slots_b[sl].ap, slots_u[sl].ap, AF.Copy, bias=-128.0),
                 reads=[slots_b[sl].k], writes=[slots_b[sl].k])
            return sl

        def down_block(g, cb):
            hs = g % 2
            dsl = stage(DCB[cb, g], 8 * g + cb, dstg_u, dstg, NDS, "gpsimd")
            for sub in range(2):
                co = 2 * cb + sub
                pd = 2 + (co % 2)
                def dfn(E, dsl=dsl, hs=hs, pd=pd, sub=sub):
                    inst = None
                    for th in range(2):
                        for il in range(8):
                            inst = E.matmul(
                                PS[pd].ap()[:, th * 512:(th + 1) * 512],
                                dstg[dsl][:, il * 256 + sub * 128:
                                          il * 256 + sub * 128 + 128],
                                hact[hs][:, il * 1024 + th * 512:
                                         il * 1024 + (th + 1) * 512],
                                start=(il == 0), stop=(il == 7))
                    return inst
                S.op("tensor", dfn, reads=[dstg[dsl].k, hact[hs].k],
                     writes=[PSK[pd]])
                if g == 0:
                    S.op("vector", lambda E, co=co, pd=pd: E.tensor_scalar(
                        zz[co].ap, PS[pd].ap()[:, :], ds_sc[:, co:co + 1],
                        None, ALU.mult),
                         reads=[PSK[pd], ds_sc.k], writes=[zz[co].k])
                else:
                    S.op("vector", lambda E, co=co, pd=pd: E.scalar_tensor_tensor(
                        zz[co].ap, PS[pd].ap()[:, :], ds_sc[:, co:co + 1],
                        zz[co].ap, ALU.mult, ALU.add),
                         reads=[PSK[pd], ds_sc.k, zz[co].k], writes=[zz[co].k])

        for g in range(8):
            hs = g % 2
            for ib in range(4):
                ssl = stage(GCB[4 * g + ib], 4 * g + ib, gstg_u, gstg, NGS,
                            "sync")
                usl = stage(UCB[4 * g + ib], 4 * g + ib, ustg_u, ustg, NGS,
                            "scalar")
                for sub in range(2):
                    il = 2 * ib + sub
                    ic = 8 * g + il
                    def gufn(E, ssl=ssl, usl=usl, sub=sub, w=0):
                        stg = gstg[ssl] if w == 0 else ustg[usl]
                        pi = 0 if w == 0 else 1
                        inst = None
                        for hc in range(16):
                            for th in range(2):
                                inst = E.matmul(
                                    PS[pi].ap()[:, th * 512:(th + 1) * 512],
                                    stg[:, hc * 256 + sub * 128:
                                        hc * 256 + sub * 128 + 128],
                                    x1T[hc][:, th * 512:(th + 1) * 512],
                                    start=(hc == 0), stop=(hc == 15))
                        return inst
                    S.op("tensor", lambda E, ssl=ssl, usl=usl, sub=sub: gufn(
                        E, ssl, usl, sub, 0),
                         reads=[gstg[ssl].k] + [x.k for x in x1T],
                         writes=[PSK[0]])
                    S.op("tensor", lambda E, ssl=ssl, usl=usl, sub=sub: gufn(
                        E, ssl, usl, sub, 1),
                         reads=[ustg[usl].k] + [x.k for x in x1T],
                         writes=[PSK[1]])
                    S.op("scalar", lambda E, ic=ic: E.activation(
                        S3.ap, PS[0].ap()[:, :], AF.Silu,
                        scale=gs_sc[:, ic:ic + 1]),
                         reads=[PSK[0], gs_sc.k], writes=[S3.k])
                    S.op("vector", lambda E, ic=ic, hs=hs, il=il: E.scalar_tensor_tensor(
                        hact[hs][:, il * 1024:(il + 1) * 1024],
                        PS[1].ap()[:, :], us_sc[:, ic:ic + 1], S3.ap,
                        ALU.mult, ALU.mult),
                         reads=[PSK[1], us_sc.k, S3.k], writes=[hact[hs].k])
                if g > 0:
                    down_block(g - 1, 2 * ib)
                    down_block(g - 1, 2 * ib + 1)
        for cb in range(8):
            down_block(7, cb)
            if cb > 0:
                ln_stats(zz, 2 * (cb - 1), 0, 1)
                ln_stats(zz, 2 * cb - 1, 0, 1)

        if dump == "ffn":
            _emit_dump(S, AR_F, DUMP)

        st_ops = []

        def ln2_dst(hc):
            outv = YS if hc % 2 == 0 else S4
            tmp = S3 if hc % 2 == 0 else QAf
            S.op("vector", lambda E, hc=hc, tmp=tmp: E.tensor_tensor(
                tmp.ap, zz[hc].ap, S1.ap, ALU.mult),
                 reads=[zz[hc].k, S1.k], writes=[tmp.k])
            S.op("vector", lambda E, hc=hc, outv=outv, tmp=tmp: E.tensor_tensor(
                outv.ap, tmp.ap, S2.ap, ALU.add),
                 reads=[tmp.k, S2.k], writes=[outv.k])
            st_ops.append(S.op("sync", lambda E, hc=hc, outv=outv: E.dma_start(
                out=OUT[hc * 128:(hc + 1) * 128, :], in_=outv.ap),
                reads=[outv.k], dma=True))
        ln_stats(zz, 14, 0, 1)
        ln_stats(zz, 15, 0, 1)
        ln_finish(ln2_dst, 0, 1)

        S.op("sync", lambda E: E.nop(), deps=st_ops)

        block = ctx.enter_context(nc.Block())
        S.emit(block, csem, dsem)
    return nc


def _emit_dump(S, AR_F, DUMP):
    for hc in range(16):
        w = AR_F.shape[1] // 16 if hc < 15 else AR_F.shape[1] - 15 * (AR_F.shape[1] // 16)
        c0 = hc * (AR_F.shape[1] // 16)
        S.op("sync", lambda E, c0=c0, w=w: E.dma_start(
            out=DUMP[:, c0:c0 + w], in_=AR_F[:, c0:c0 + w]),
            reads=[tuple(("A", u) for u in range(0, 120))], dma=True)


def _prep(inp):
    x = np.asarray(inp["L0_attn_x"], np.float32)
    Wq = np.asarray(inp["L0_attn_Wq"], np.float32)
    Wk = np.asarray(inp["L0_attn_Wk"], np.float32)
    Wv = np.asarray(inp["L0_attn_Wv"], np.float32)
    mem = np.ascontiguousarray(np.asarray(inp["L0_attn_memory"], np.float32))
    mnorm = np.ascontiguousarray(
        np.asarray(inp["L0_attn_memory_norm"], np.float32).reshape(512, 1))
    mw = np.asarray(inp["L0_attn_memory_weight"], np.float32).reshape(1, 1)
    gw = np.asarray(inp["L0_ffn_gate_w"], np.int32)
    uw = np.asarray(inp["L0_ffn_up_w"], np.int32)
    dw = np.asarray(inp["L0_ffn_down_w"], np.int32)
    gw8 = np.ascontiguousarray(gw.view(np.uint8).reshape(8192, 2048, 4)[:, :, 0])
    uw8 = np.ascontiguousarray(uw.view(np.uint8).reshape(8192, 2048, 4)[:, :, 0])
    dw8 = np.ascontiguousarray(dw.view(np.uint8).reshape(2048, 8192, 4)[:, :, 0])
    gs = np.asarray(inp["L0_ffn_gate_scale"], np.float32)
    us = np.asarray(inp["L0_ffn_up_scale"], np.float32)
    ds = np.asarray(inp["L0_ffn_down_scale"], np.float32)

    common = dict(
        WQC=np.ascontiguousarray(
            Wq.reshape(16, 128, 16, 128).transpose(0, 3, 2, 1)
            .reshape(16, 128, 2048)),
        WKC=np.ascontiguousarray(
            Wk.reshape(4, 128, 16, 128).transpose(0, 3, 2, 1)
            .reshape(4, 128, 2048)),
        WVC=np.ascontiguousarray(
            Wv.reshape(4, 128, 16, 128).transpose(0, 3, 2, 1)
            .reshape(4, 128, 2048)),
        MEM=mem, MNORM=mnorm, MW=np.ascontiguousarray(mw),
        GCB=np.ascontiguousarray(
            gw8.reshape(32, 256, 16, 128).transpose(0, 3, 2, 1)
            .reshape(32, 128, 4096)),
        UCB=np.ascontiguousarray(
            uw8.reshape(32, 256, 16, 128).transpose(0, 3, 2, 1)
            .reshape(32, 128, 4096)),
        DCB=np.ascontiguousarray(
            dw8.reshape(8, 256, 8, 8, 128).transpose(0, 2, 4, 3, 1)
            .reshape(8, 8, 128, 2048)),
        GS=np.ascontiguousarray(gs.reshape(64, 128)),
        US=np.ascontiguousarray(us.reshape(64, 128)),
        DS=np.ascontiguousarray(ds.reshape(16, 128)),
    )
    in_maps = []
    xTs = {}
    for c in range(8):
        beta, half = c // 2, c % 2
        if beta not in xTs:
            xTs[beta] = np.ascontiguousarray(x[beta].T)
        xT = xTs[beta]
        if half == 0:
            XT = xT
        else:
            XT = np.ascontiguousarray(
                np.concatenate([xT[:, 1024:], xT[:, :1024]], axis=1))
        m = dict(common)
        m["XT"] = XT
        in_maps.append(m)
    return in_maps


def kernel(**inputs):
    in_maps = _prep(inputs)
    nc = build_nc()
    res = run_bass_kernel_spmd(nc, in_maps, core_ids=list(range(8)))
    out = np.empty((4, 2048, 2048), np.float32)
    for c in range(8):
        beta, half = c // 2, c % 2
        out[beta, half * 1024:(half + 1) * 1024, :] = res.results[c]["OUT"].T
    return out


# version 28
# speedup vs baseline: 1.0580x; 1.0107x over previous
import numpy as np
import contextlib
import concourse.bass as bass
import concourse.mybir as mybir
from concourse.bass_utils import run_bass_kernel_spmd

F32 = mybir.dt.float32
BF16 = mybir.dt.bfloat16
I32 = mybir.dt.int32
U8 = mybir.dt.uint8
AF = mybir.ActivationFunctionType
ALU = mybir.AluOpType
AX = mybir.AxisListType.X

ENGINES = ("tensor", "scalar", "vector", "gpsimd", "sync")


class _Op:
    __slots__ = ("eng", "fn", "deps", "dma", "ord", "awaited", "idx")


class Sched:

    def __init__(self):
        self.ops = []
        self.state = {}

    def op(self, eng, fn, reads=(), writes=(), deps=(), dma=False):
        o = _Op()
        o.eng, o.fn, o.dma = eng, fn, dma
        o.awaited, o.ord = False, None
        o.idx = len(self.ops)
        dep = set()
        for d in deps:
            if d is not None:
                dep.add(d)
        for ks in reads:
            for k in ks:
                st = self.state.get(k)
                if st and st[0] is not None:
                    dep.add(st[0])
        for ks in writes:
            for k in ks:
                st = self.state.get(k)
                if st:
                    if st[0] is not None:
                        dep.add(st[0])
                    dep.update(st[1])
        o.deps = list(dep)
        self.ops.append(o)
        for ks in reads:
            for k in ks:
                self.state.setdefault(k, [None, []])[1].append(o)
        for ks in writes:
            for k in ks:
                self.state[k] = [o, []]
        return o

    def emit(self, block, csem, dsem, nring=8):
        for o in self.ops:
            for d in o.deps:
                d.awaited = True
        ccnt = {e: 0 for e in ENGINES}
        dcnt = {e: 0 for e in ENGINES}
        for o in self.ops:
            if o.dma:
                i = dcnt[o.eng]
                dcnt[o.eng] += 1
                o.ord = (i % nring, 16 * (i // nring + 1), i)
                o.awaited = True
            elif o.awaited:
                ccnt[o.eng] += 1
                o.ord = ccnt[o.eng]

        def mk(eng):
            my_ops = [o for o in self.ops if o.eng == eng]

            def body(E):
                waited = {}
                for o in my_ops:
                    best = {}
                    for d in o.deps:
                        if d.dma:
                            k = (d.eng, "d", d.ord[0])
                            if k not in best or best[k].idx < d.idx:
                                best[k] = d
                        else:
                            k = (d.eng, "c")
                            if k not in best or best[k].idx < d.idx:
                                best[k] = d
                    for k, d in best.items():
                        if d.dma:
                            sem, val = dsem[d.eng][d.ord[0]], d.ord[1]
                        else:
                            sem, val = csem[d.eng], d.ord
                        if waited.get(k, 0) < val:
                            E.wait_ge(sem, val)
                            waited[k] = val
                    if o.dma:
                        slot, val, i = o.ord
                        if i >= nring:
                            k = (eng, "d", slot)
                            if waited.get(k, 0) < val - 16:
                                E.wait_ge(dsem[eng][slot], val - 16)
                                waited[k] = val - 16
                    inst = o.fn(E)
                    if o.dma:
                        inst.then_inc(dsem[eng][o.ord[0]], 16)
                    elif o.awaited:
                        inst.then_inc(csem[eng], 1)
            return body

        for eng in ENGINES:
            getattr(block, eng)(mk(eng))


L = {}


def _lay(name, nbytes, align=64):
    off = _lay.cur
    off = (off + align - 1) // align * align
    L[name] = off
    _lay.cur = off + nbytes
    return off


_lay.cur = 0
_lay("ident_bf", 256)
_lay("ident_f", 512)
_lay("ones_f", 512)
_lay("ones_bf", 256)
_lay("bias_all", 128)
_lay("brow", 384)
_lay("negr_bf", 64)
_lay("maxsq", 64)
_lay("maxk", 64)
_lay("gs_sc", 256)
_lay("us_sc", 256)
_lay("ds_sc", 64)
_lay("wcst", 32)
_lay("scnat", 512)
_lay("mn_bf", 16)
_lay("qmask", 128)
_lay("YS", 4096)
_lay("S1", 4096)
_lay("S2", 4096)
_lay("S3", 4096)
_lay("S4", 4096)
_lay("rowbf", 2048)
_lay("QA", 2048)
_lay("QB", 2048)
_lay("P2", 32768)
_lay("P3", 16384 + 32768)
_lay("P4", 65536)
_lay("P5", 32768)
AR_BYTES = (_lay.cur + 63) // 64 * 64
assert AR_BYTES <= 212800, AR_BYTES

D_KEY_UNIT = 2048


def _keys(off, nbytes):
    return tuple(("A", u) for u in range(off // D_KEY_UNIT,
                                         (off + nbytes - 1) // D_KEY_UNIT + 1))


class TV:

    def __init__(self, ap, keys):
        self.ap = ap
        self.k = keys

    def __getitem__(self, sl):
        return self.ap[sl]


def build_nc(dump=None):
    nc = bass.Bass("TRN2", target_bir_lowering=False)

    XT = nc.dram_tensor("XT", [2048, 2048], F32, kind="ExternalInput").ap()
    WQC = nc.dram_tensor("WQC", [16, 128, 2048], F32, kind="ExternalInput").ap()
    WKC = nc.dram_tensor("WKC", [4, 128, 2048], F32, kind="ExternalInput").ap()
    WVC = nc.dram_tensor("WVC", [4, 128, 2048], F32, kind="ExternalInput").ap()
    MEM = nc.dram_tensor("MEM", [512, 512], F32, kind="ExternalInput").ap()
    MNORM = nc.dram_tensor("MNORM", [512, 1], F32, kind="ExternalInput").ap()
    MW = nc.dram_tensor("MW", [1, 1], F32, kind="ExternalInput").ap()
    GCB = nc.dram_tensor("GCB", [32, 128, 4096], U8, kind="ExternalInput").ap()
    UCB = nc.dram_tensor("UCB", [32, 128, 4096], U8, kind="ExternalInput").ap()
    DCB = nc.dram_tensor("DCB", [8, 8, 128, 2048], U8, kind="ExternalInput").ap()
    GS = nc.dram_tensor("GS", [64, 128], F32, kind="ExternalInput").ap()
    US = nc.dram_tensor("US", [64, 128], F32, kind="ExternalInput").ap()
    DS = nc.dram_tensor("DS", [16, 128], F32, kind="ExternalInput").ap()
    OUT = nc.dram_tensor("OUT", [2048, 1024], F32, kind="ExternalOutput").ap()
    DUMP = None
    if dump is not None:
        DUMP = nc.dram_tensor("DUMP", [128, AR_BYTES // 4], F32,
                              kind="ExternalOutput").ap()

    with contextlib.ExitStack() as ctx:
        AR = ctx.enter_context(nc.sbuf_tensor("AR", [128, AR_BYTES // 4], F32))
        ARB = AR.bitcast(BF16)
        AR_F = AR.ap()
        AR_B = ARB.ap()
        PS = [ctx.enter_context(nc.psum_tensor("PS%d" % i, [128, 1024], F32))
              for i in range(4)]
        PSB = [p.bitcast(BF16) for p in PS]
        PSK = [(("PS", i),) for i in range(4)]

        csem = {e: ctx.enter_context(nc.semaphore("c_" + e)) for e in ENGINES}
        dsem = {e: [ctx.enter_context(nc.semaphore("d_%s%d" % (e, i)))
                    for i in range(8)]
                for e in ("gpsimd", "sync", "scalar")}

        def vf(name, cols, off_extra=0):
            off = L[name] + off_extra
            return TV(AR_F[:, off // 4: off // 4 + cols], _keys(off, cols * 4))

        def vb(name, cols, off_extra=0):
            off = L[name] + off_extra
            return TV(AR_B[:, off // 2: off // 2 + cols], _keys(off, cols * 2))

        ident_bf = vb("ident_bf", 128)
        ident_f = vf("ident_f", 128)
        ones_f = vf("ones_f", 128)
        ones_bf = vb("ones_bf", 128)
        bias_all = vf("bias_all", 32)
        brow = vf("brow", 96)
        negr_bf = vb("negr_bf", 32)
        maxsq = vf("maxsq", 16)
        maxk = vf("maxk", 16)
        gs_sc = vf("gs_sc", 64)
        us_sc = vf("us_sc", 64)
        ds_sc = vf("ds_sc", 16)
        wcst = vf("wcst", 8)
        scnat = vf("scnat", 128)
        mn_bf = vb("mn_bf", 8)
        qmask = vb("qmask", 33)
        YS = vf("YS", 1024)
        y1b = vb("YS", 1024)
        sqb = vb("YS", 1024, off_extra=2048)
        S1 = vf("S1", 1024)
        S2 = vf("S2", 1024)
        S3 = vf("S3", 1024)
        S4 = vf("S4", 1024)
        KSQ = TV(AR_F[:, L["S1"] // 4: L["S1"] // 4 + 2048],
                 _keys(L["S1"], 8192))
        rowbf = vb("rowbf", 1024)

        QT = [vb("P2", 1024, off_extra=i * 2048) for i in range(16)]
        x1T = QT
        QAb = vb("QA", 1024)
        QBb = vb("QB", 1024)
        khat = [vb("P3", 2048, off_extra=kc * 4096) for kc in range(4)]
        vhat = [vb("P3", 2048, off_extra=16384 + h * 4096) for h in range(8)]
        xbT = [vb("P4", 2048, off_extra=hc * 4096) for hc in range(16)]
        y1 = [vf("P4", 1024, off_extra=hc * 4096) for hc in range(16)]
        zz = y1
        NWS = 3
        wslot_f = [vf("P5", 2048, off_extra=s * 8192) for s in range(NWS)]
        wslot = [vb("P5", 2048, off_extra=s * 8192) for s in range(NWS)]
        vt = [vb("S1", 2048), vb("S2", 2048)]
        mem_bf = vb("P5", 2048, off_extra=24576)
        expT = [vb("P5", 1024, off_extra=s * 2048) for s in range(8)]
        sig = [vb("P5", 1024, off_extra=16384 + s * 2048) for s in range(4)]
        hact = [vb("P5", 8192, off_extra=s * 16384) for s in range(2)]
        AR_I = AR.bitcast(I32).ap()

        def vi(name, cols, off_extra=0):
            off = L[name] + off_extra
            return TV(AR_I[:, off // 4: off // 4 + cols], _keys(off, cols * 4))

        AR_U = AR.bitcast(U8).ap()

        def vu(name, cols, off_extra=0):
            off = L[name] + off_extra
            return TV(AR_U[:, off: off + cols], _keys(off, cols))

        NGS = 2
        NDS = 3
        gstg_u = [vu("P3", 4096, off_extra=s * 8192 + 4096) for s in range(NGS)]
        gstg = [vb("P3", 4096, off_extra=s * 8192) for s in range(NGS)]
        ustg_u = [vu("P3", 4096, off_extra=16384 + s * 8192 + 4096) for s in range(NGS)]
        ustg = [vb("P3", 4096, off_extra=16384 + s * 8192) for s in range(NGS)]
        dstg_u = [vu("P3", 2048, off_extra=32768 + s * 4096 + 2048) for s in range(NDS)]
        dstg = [vb("P3", 2048, off_extra=32768 + s * 4096) for s in range(NDS)]
        g8 = [vu("S1", 4096), vu("S2", 4096)]
        u8_ = [vu("YS", 4096), vu("S4", 4096)]
        d8 = [vu("QA", 2048), vu("QB", 2048), vu("rowbf", 2048)]

        S = Sched()

        if dump is not None:
            S.op("gpsimd", lambda E: E.memset(AR_F[:, :], 0.0),
                 writes=[tuple(("A", u) for u in range(0, 120))])
        S.op("gpsimd", lambda E: E.memset(ones_f.ap, 1.0), writes=[ones_f.k])
        S.op("gpsimd", lambda E: E.memset(ones_bf.ap, 1.0), writes=[ones_bf.k])
        S.op("gpsimd", lambda E: E.memset(wcst[0:128, 4:5], -128.0),
             writes=[wcst.k])
        S.op("gpsimd", lambda E: E.memset(QAb[64:128, :], 0.0), writes=[QAb.k])
        S.op("gpsimd", lambda E: E.memset(QBb[0:64, :], 0.0), writes=[QBb.k])
        S.op("gpsimd", lambda E: E.memset(qmask.ap, 0.0), writes=[qmask.k])
        S.op("gpsimd", lambda E: E.memset(qmask[0:64, 0:1], 1.0),
             writes=[qmask.k])
        S.op("gpsimd", lambda E: E.memset(qmask[64:128, 32:33], 1.0),
             writes=[qmask.k])
        S.op("gpsimd", lambda E: E.affine_select(
            ident_bf.ap, ones_bf.ap, [[-1, 128]], ALU.is_equal, 0.0,
            base=0, channel_multiplier=1), reads=[ones_bf.k], writes=[ident_bf.k])
        S.op("gpsimd", lambda E: E.affine_select(
            ident_f.ap, ones_f.ap, [[-1, 128]], ALU.is_equal, 0.0,
            base=0, channel_multiplier=1), reads=[ones_f.k], writes=[ident_f.k])

        S.op("sync", lambda E: E.dma_start(out=wcst[0:1, 0:1], in_=MW[:, :]),
             writes=[wcst.k], dma=True)
        S.op("sync", lambda E: E.dma_start(out=wcst[64:65, 0:1], in_=MW[:, :]),
             writes=[wcst.k], dma=True)
        for b in (0, 64):
            S.op("scalar", lambda E, b=b: E.activation(
                wcst[b:b + 1, 1:2], wcst[b:b + 1, 0:1], AF.Sigmoid),
                reads=[wcst.k], writes=[wcst.k])
            S.op("vector", lambda E, b=b: E.tensor_scalar(
                wcst[b:b + 1, 2:3], wcst[b:b + 1, 1:2], -1.0, 1.0,
                ALU.mult, ALU.add), reads=[wcst.k], writes=[wcst.k])
            S.op("scalar", lambda E, b=b: E.activation(
                wcst[b:b + 1, 3:4], wcst[b:b + 1, 2:3], AF.Ln),
                reads=[wcst.k], writes=[wcst.k])

        for (nat_rows, DR, dst) in ((64, GS, gs_sc), (64, US, us_sc),
                                    (16, DS, ds_sc)):
            S.op("sync", lambda E, DR=DR, r=nat_rows: E.dma_start(
                out=scnat[0:r, 0:128], in_=DR[:, :]), writes=[scnat.k], dma=True)
            S.op("tensor", lambda E, r=nat_rows: E.transpose(
                PS[0].ap()[0:128, 0:r], scnat[0:r, 0:128], ident_f[0:r, 0:r]),
                reads=[scnat.k, ident_f.k], writes=[PSK[0]])
            S.op("vector", lambda E, dst=dst, r=nat_rows: E.tensor_copy(
                dst[0:128, 0:r], PS[0].ap()[0:128, 0:r]),
                reads=[PSK[0]], writes=[dst.k])

        S.op("gpsimd", lambda E: E.dma_start(
            out=mem_bf.ap, in_=MEM.rearrange("(c p) j -> p c j", p=128)),
            writes=[mem_bf.k], dma=True)
        S.op("sync", lambda E: E.dma_start(
            out=scnat[0:4, 0:128], in_=MNORM.rearrange("(c p) o -> c (p o)", p=128)),
            writes=[scnat.k], dma=True)
        S.op("tensor", lambda E: E.transpose(
            PS[0].ap()[0:128, 0:4], scnat[0:4, 0:128], ident_f[0:4, 0:4]),
            reads=[scnat.k, ident_f.k], writes=[PSK[0]])
        S.op("vector", lambda E: E.tensor_copy(mn_bf[0:128, 0:4],
                                               PS[0].ap()[0:128, 0:4]),
             reads=[PSK[0]], writes=[mn_bf.k])

        xstg = [TV(AR_F[:, L["S1"] // 4: L["S1"] // 4 + 2048],
                   _keys(L["S1"], 8192)),
                TV(AR_F[:, L["S3"] // 4: L["S3"] // 4 + 2048],
                   _keys(L["S3"], 8192))]
        for hc in range(16):
            xsl = hc % 2
            ring = "sync" if xsl == 0 else "gpsimd"
            S.op(ring, lambda E, hc=hc, xsl=xsl: E.dma_start(
                out=xstg[xsl].ap, in_=XT[hc * 128:(hc + 1) * 128, :]),
                writes=[xstg[xsl].k], dma=True)
            S.op("scalar", lambda E, hc=hc, xsl=xsl: E.activation(
                xbT[hc].ap, xstg[xsl].ap, AF.Copy),
                 reads=[xstg[xsl].k], writes=[xbT[hc].k])

        _wl = [0]

        def wload(DRC, idx):
            slot = _wl[0] % NWS
            _wl[0] += 1
            op = S.op("scalar", lambda E, slot=slot: E.dma_start(
                out=wslot_f[slot].ap, in_=DRC[idx]),
                writes=[wslot_f[slot].k], dma=True)
            S.op("scalar", lambda E, slot=slot: E.activation(
                wslot[slot].ap, wslot_f[slot].ap, AF.Copy),
                 reads=[wslot_f[slot].k], writes=[wslot_f[slot].k])
            return slot, op

        def proj_group(slot, ps_lo, ps_hi, tq_n, psk_writes):
            for hc in range(16):
                def fn(E, hc=hc):
                    inst = None
                    for tq in range(tq_n):
                        p = PS[ps_lo] if tq < 2 else PS[ps_hi]
                        col = (tq % 2) * 512
                        inst = E.matmul(
                            p.ap()[:, col:col + 512],
                            wslot[slot][:, hc * 128:(hc + 1) * 128],
                            xbT[hc][:, tq * 512:(tq + 1) * 512],
                            start=(hc == 0), stop=(hc == 15))
                    return inst
                S.op("tensor", fn, reads=[wslot[slot].k, xbT[hc].k],
                     writes=psk_writes)

        for kc in range(4):
            sl, _ = wload(WKC, kc)
            pl = 2 * (kc % 2)
            proj_group(sl, pl, pl + 1, 4, [PSK[pl], PSK[pl + 1]])
            for half in range(2):
                S.op("vector", lambda E, kc=kc, half=half, pl=pl: E.tensor_copy(
                    khat[kc][:, half * 1024:(half + 1) * 1024],
                    PS[pl + half].ap()[:, :]),
                    reads=[PSK[pl + half]], writes=[khat[kc].k])
        for kc in range(4):
            sl, _ = wload(WVC, kc)
            vsl = kc % 2
            pl = 2 * (kc % 2)
            proj_group(sl, pl, pl + 1, 4, [PSK[pl], PSK[pl + 1]])
            for half in range(2):
                S.op("vector", lambda E, half=half, vsl=vsl, pl=pl: E.tensor_copy(
                    vt[vsl][:, half * 1024:(half + 1) * 1024],
                    PS[pl + half].ap()[:, :]),
                    reads=[PSK[pl + half]], writes=[vt[vsl].k])
            for hsub in range(2):
                h = 2 * kc + hsub
                b = hsub * 64
                pt = 2 + (kc % 2)
                def trf(E, b=b, sl=vsl, pt=pt):
                    inst = None
                    for tch in range(16):
                        inst = E.transpose(
                            PSB[pt].ap()[0:128, tch * 64:(tch + 1) * 64],
                            vt[sl][b:b + 64, tch * 128:(tch + 1) * 128],
                            ident_bf[b:b + 64, b:b + 64])
                    return inst
                S.op("tensor", trf, reads=[vt[vsl].k, ident_bf.k],
                     writes=[PSK[pt]])
                S.op("vector", lambda E, h=h, pt=pt: E.tensor_copy(
                    vhat[h].ap.rearrange("p (c w) -> p c w", w=128)[:, :, 0:64],
                    PSB[pt].ap()[0:128, 0:1024].rearrange("p (c w) -> p c w", w=64)),
                    reads=[PSK[pt]], writes=[vhat[h].k])
                S.op("gpsimd", lambda E, h=h: E.memset(
                    vhat[h].ap.rearrange("p (c w) -> p c w", w=128)[:, :, 64:65],
                    1.0), writes=[vhat[h].k])
                S.op("gpsimd", lambda E, h=h: E.memset(
                    vhat[h].ap.rearrange("p (c w) -> p c w", w=128)[:, :, 65:128],
                    0.0), writes=[vhat[h].k])
        sqA = vb("S3", 1024)
        sqB = vb("YS", 1024)
        KSQB = vb("S1", 2048)

        def qbias(oc):
            sq = sqA if oc % 2 == 0 else sqB
            pb = 1 + 2 * (oc % 2)
            S.op("vector", lambda E, oc=oc, sq=sq: E.tensor_tensor(
                sq.ap, QT[oc].ap, QT[oc].ap, ALU.mult),
                reads=[QT[oc].k], writes=[sq.k])
            def qrowfn(E, pb=pb, sq=sq):
                inst = None
                for th in range(2):
                    inst = E.matmul(
                        PS[pb].ap()[0:33, th * 512:(th + 1) * 512],
                        qmask[0:128, 0:33],
                        sq[0:128, th * 512:(th + 1) * 512],
                        start=True, stop=True)
                return inst
            S.op("tensor", qrowfn, reads=[qmask.k, sq.k], writes=[PSK[pb]])
            S.op("vector", lambda E, oc=oc, pb=pb: E.reduce_max(
                brow[0:33, oc:oc + 1], PS[pb].ap()[0:33, 0:1024], axis=AX),
                reads=[PSK[pb]], writes=[brow.k])

        def kbias(kc):
            S.op("vector", lambda E, kc=kc: E.tensor_tensor(
                KSQB.ap, khat[kc].ap, khat[kc].ap, ALU.mult),
                reads=[khat[kc].k], writes=[KSQB.k])
            for tha in range(2):
                pb = 1 + 2 * tha
                def krowfn(E, pb=pb, tha=tha):
                    inst = None
                    for tq in range(2):
                        inst = E.matmul(
                            PS[pb].ap()[0:33, tq * 512:(tq + 1) * 512],
                            qmask[0:128, 0:33],
                            KSQB.ap[0:128, (2 * tha + tq) * 512:
                                    (2 * tha + tq + 1) * 512],
                            start=True, stop=True)
                    return inst
                S.op("tensor", krowfn, reads=[qmask.k, KSQB.k],
                     writes=[PSK[pb]])
                S.op("vector", lambda E, kc=kc, pb=pb, tha=tha: E.reduce_max(
                    brow[0:33, 16 + 4 * tha + kc:17 + 4 * tha + kc],
                    PS[pb].ap()[0:33, 0:1024], axis=AX),
                    reads=[PSK[pb]], writes=[brow.k])
            S.op("vector", lambda E, kc=kc: E.tensor_tensor(
                brow[0:33, 24 + kc:25 + kc], brow[0:33, 16 + kc:17 + kc],
                brow[0:33, 20 + kc:21 + kc], ALU.max),
                reads=[brow.k], writes=[brow.k])

        for oc in range(16):
            sl, _ = wload(WQC, oc)
            pp = 2 * (oc % 2)
            proj_group(sl, pp, pp + 1, 2, [PSK[pp]])
            S.op("vector", lambda E, oc=oc, pp=pp: E.tensor_copy(
                QT[oc].ap, PS[pp].ap()[:, :]),
                reads=[PSK[pp]], writes=[QT[oc].k])
            qbias(oc)
            if oc < 4:
                kbias(oc)

        for hc in range(16):
            S.op("sync", lambda E, hc=hc: E.dma_start(
                out=y1[hc].ap, in_=XT[hc * 128:(hc + 1) * 128, 0:1024]),
                writes=[y1[hc].k], dma=True)
        def prodfn(E):
            inst = None
            for oc in range(16):
                kc = oc % 4
                inst = E.tensor_tensor(brow[0:33, 28 + oc:29 + oc],
                                       brow[0:33, oc:oc + 1],
                                       brow[0:33, 24 + kc:25 + kc], ALU.mult)
            return inst
        S.op("vector", prodfn, reads=[brow.k], writes=[brow.k])
        S.op("vector", lambda E: E.memset(negr_bf[0:33, 0:32], 0.0),
             writes=[negr_bf.k])
        for par, sub in ((0, 0), (64, 1)):
            pr = 32 if par else 0
            S.op("scalar", lambda E, pr=pr: E.activation(
                S4[pr:pr + 1, 0:16], brow[pr:pr + 1, 28:44], AF.Ln),
                 reads=[brow.k], writes=[S4.k])
            S.op("scalar", lambda E, pr=pr: E.activation(
                S4[pr:pr + 1, 16:32], S4[pr:pr + 1, 0:16], AF.Exp, scale=0.5),
                 reads=[S4.k], writes=[S4.k])
            S.op("vector", lambda E, pr=pr: E.tensor_scalar(
                S4[pr:pr + 1, 32:48], S4[pr:pr + 1, 16:32], -0.125, None,
                ALU.mult), reads=[S4.k], writes=[S4.k])
            S.op("vector", lambda E, pr=pr, sub=sub: E.tensor_copy(
                negr_bf.ap.rearrange("p (a b) -> p a b", b=2)[pr:pr + 1, :, sub:sub + 1],
                S4[pr:pr + 1, 32:48]),
                 reads=[S4.k], writes=[negr_bf.k])
        def bcastfn(E):
            E.matmul(PS[3].ap()[0:128, 0:32], ones_bf[0:1, 0:128],
                     negr_bf[0:1, 0:32], start=True, stop=False)
            return E.matmul(PS[3].ap()[0:128, 0:32], ones_bf[32:33, 0:128],
                            negr_bf[32:33, 0:32], start=False, stop=True)
        S.op("tensor", bcastfn, reads=[ones_bf.k, negr_bf.k], writes=[PSK[3]])
        S.op("vector", lambda E: E.tensor_copy(bias_all.ap,
                                               PS[3].ap()[0:128, 0:32]),
             reads=[PSK[3]], writes=[bias_all.k])

        if dump == "qkv":
            _emit_dump(S, AR_F, DUMP)

        for g in range(4):
            for c in range(4):
                oc = 4 * g + c
                sA, sB = (S1, S2) if c % 2 == 0 else (S3, YS)
                S.op("vector", lambda E, oc=oc, sA=sA: E.tensor_scalar_min(
                    sA.ap, QT[oc].ap, 0.0), reads=[QT[oc].k], writes=[sA.k])
                S.op("scalar", lambda E, sA=sA, sB=sB: E.activation(
                    sB.ap, sA.ap, AF.Exp), reads=[sA.k], writes=[sB.k])
                S.op("vector", lambda E, oc=oc, sA=sA: E.tensor_scalar_max(
                    sA.ap, QT[oc].ap, 0.0), reads=[QT[oc].k], writes=[sA.k])
                S.op("vector", lambda E, c=c, sA=sA, sB=sB: E.scalar_tensor_tensor(
                    sig[c].ap, sB.ap, -1.0, sA.ap, ALU.add, ALU.add),
                     reads=[sA.k, sB.k], writes=[sig[c].k])
            def nrmfn(E):
                inst = None
                for th in range(2):
                    for c in range(4):
                        inst = E.matmul(
                            PS[3].ap()[0:1, th * 512:(th + 1) * 512],
                            mn_bf[0:128, c:c + 1],
                            sig[c][:, th * 512:(th + 1) * 512],
                            start=(c == 0), stop=(c == 3))
                return inst
            S.op("tensor", nrmfn, reads=[mn_bf.k] + [t.k for t in sig],
                 writes=[PSK[3]])
            S.op("vector", lambda E: E.reciprocal(S4[0:1, 0:1024],
                                                  PS[3].ap()[0:1, :]),
                 reads=[PSK[3]], writes=[S4.k])
            S.op("vector", lambda E: E.tensor_scalar(
                rowbf[0:1, 0:1024], S4[0:1, 0:1024], wcst[0:1, 1:2], None,
                ALU.mult), reads=[S4.k, wcst.k], writes=[rowbf.k])
            def psi2fn(E):
                inst = None
                for th in range(2):
                    inst = E.matmul(PS[3].ap()[0:128, th * 512:(th + 1) * 512],
                                    ones_bf[0:1, 0:128],
                                    rowbf[0:1, th * 512:(th + 1) * 512],
                                    start=True, stop=True)
                return inst
            S.op("tensor", psi2fn, reads=[ones_bf.k, rowbf.k], writes=[PSK[3]])
            S.op("vector", lambda E: E.tensor_copy(S3.ap, PS[3].ap()[:, :]),
                 reads=[PSK[3]], writes=[S3.k])
            for jc in range(4):
                pj = jc % 2
                def amfn(E, pj=pj, jc=jc):
                    inst = None
                    for th in range(2):
                        for c in range(4):
                            inst = E.matmul(
                                PS[pj].ap()[:, th * 512:(th + 1) * 512],
                                mem_bf[:, c * 512 + jc * 128: c * 512 + (jc + 1) * 128],
                                sig[c][:, th * 512:(th + 1) * 512],
                                start=(c == 0), stop=(c == 3))
                    return inst
                S.op("tensor", amfn, reads=[mem_bf.k] + [t.k for t in sig],
                     writes=[PSK[pj]])
                tmp = S1 if jc % 2 == 0 else S2
                S.op("vector", lambda E, pj=pj, tmp=tmp: E.tensor_tensor(
                    tmp.ap, PS[pj].ap()[:, :], S3.ap, ALU.mult),
                     reads=[PSK[pj], S3.k], writes=[tmp.k])
                hc = 4 * g + jc
                S.op("vector", lambda E, hc=hc, tmp=tmp: E.tensor_tensor(
                    y1[hc].ap, y1[hc].ap, tmp.ap, ALU.add),
                     reads=[tmp.k, y1[hc].k], writes=[y1[hc].k])

        if dump == "amem":
            _emit_dump(S, AR_F, DUMP)

        def emit_qk(p, tch):
            q, h = p // 8, p % 8
            oc, b, kc = q * 4 + h // 2, (h % 2) * 64, h // 2
            G = p * 16 + tch
            pc = 2 * (G % 2)
            sl = G % 8
            qh = QAb if b == 0 else QBb
            def qkfn(E, kc=kc, tch=tch, pc=pc, qh=qh):
                inst = None
                for sh in range(2):
                    inst = E.matmul(
                        PS[pc].ap()[:, sh * 512:(sh + 1) * 512],
                        khat[kc][0:128, tch * 128:(tch + 1) * 128],
                        qh[0:128, sh * 512:(sh + 1) * 512],
                        start=True, stop=True)
                return inst
            S.op("tensor", qkfn, reads=[khat[kc].k, qh.k], writes=[PSK[pc]])
            S.op("scalar", lambda E, sl=sl, p=p, pc=pc: E.activation(
                expT[sl].ap, PS[pc].ap()[:, :], AF.Exp,
                scale=0.125, bias=bias_all[:, p:p + 1]),
                 reads=[PSK[pc], bias_all.k], writes=[expT[sl].k])

        def emit_pv(p, tch):
            h = p % 8
            pa = 1 + 2 * (p % 2)
            sl = (p * 16 + tch) % 8
            def pvfn(E, h=h, tch=tch, sl=sl, pa=pa):
                inst = None
                for sh in range(2):
                    inst = E.matmul(
                        PS[pa].ap()[:, sh * 512:(sh + 1) * 512],
                        vhat[h][:, tch * 128: tch * 128 + 128],
                        expT[sl][:, sh * 512:(sh + 1) * 512],
                        start=(tch == 0), stop=(tch == 15))
                return inst
            S.op("tensor", pvfn, reads=[vhat[h].k, expT[sl].k],
                 writes=[PSK[pa]])

        def emit_tail(p, step):
            q, h = p // 8, p % 8
            oc, b = q * 4 + h // 2, (h % 2) * 64
            pa = 1 + 2 * (p % 2)
            if step == 0:
                S.op("scalar", lambda E, pa=pa: E.activation(
                    S4[64:65, 0:1024], PS[pa].ap()[64:65, :], AF.Ln),
                     reads=[PSK[pa]], writes=[S4.k])
                S.op("scalar", lambda E: E.activation(
                    rowbf[64:65, 0:1024], S4[64:65, 0:1024], AF.Exp,
                    scale=-1.0, bias=wcst[64:65, 3:4]),
                     reads=[S4.k, wcst.k], writes=[rowbf.k])
            elif step == 1:
                def psifn(E, pa=pa):
                    inst = None
                    for sh in range(2):
                        inst = E.matmul(
                            PS[pa].ap()[64:128, sh * 512:(sh + 1) * 512],
                            ones_bf[64:65, 0:64],
                            rowbf[64:65, sh * 512:(sh + 1) * 512],
                            start=True, stop=True, tile_position=(64, 64))
                    return inst
                S.op("tensor", psifn, reads=[ones_bf.k, rowbf.k],
                     writes=[PSK[pa]])
            elif step == 2:
                S.op("vector", lambda E, pa=pa: E.tensor_copy(
                    S2[0:64, 0:1024], PS[pa].ap()[64:128, :]),
                     reads=[PSK[pa]], writes=[S2.k])
            elif step == 3:
                if b == 0:
                    S.op("vector", lambda E, pa=pa: E.tensor_tensor(
                        S1[0:64, 0:1024], PS[pa].ap()[0:64, :],
                        S2[0:64, 0:1024], ALU.mult),
                         reads=[PSK[pa], S2.k], writes=[S1.k])
                else:
                    S.op("vector", lambda E, pa=pa: E.tensor_tensor(
                        rowbf[0:64, 0:1024], PS[pa].ap()[0:64, :],
                        S2[0:64, 0:1024], ALU.mult),
                         reads=[PSK[pa], S2.k], writes=[rowbf.k])
                    def shfn(E, pa=pa):
                        inst = None
                        for sh in range(2):
                            inst = E.matmul(
                                PS[pa].ap()[64:128, sh * 512:(sh + 1) * 512],
                                ident_bf[0:64, 0:64],
                                rowbf[0:64, sh * 512:(sh + 1) * 512],
                                start=True, stop=True, tile_position=(0, 64))
                        return inst
                    S.op("tensor", shfn, reads=[ident_bf.k, rowbf.k],
                         writes=[PSK[pa]])
            elif step == 4:
                if b == 0:
                    S.op("vector", lambda E, oc=oc: E.tensor_tensor(
                        y1[oc][0:64, :], y1[oc][0:64, :], S1[0:64, 0:1024],
                        ALU.add), reads=[S1.k, y1[oc].k], writes=[y1[oc].k])
                else:
                    S.op("vector", lambda E, oc=oc, pa=pa: E.tensor_tensor(
                        y1[oc][64:128, :], y1[oc][64:128, :],
                        PS[pa].ap()[64:128, :], ALU.add),
                         reads=[PSK[pa], y1[oc].k], writes=[y1[oc].k])

        PVLAG = 3
        TAIL_SLOTS = {5: 0, 7: 1, 9: 2, 11: 3, 13: 4}
        for p in range(32):
            if p % 2 == 0:
                oc0 = (p // 8) * 4 + (p % 8) // 2
                S.op("vector", lambda E, oc0=oc0: E.tensor_copy(
                    QAb[0:64, :], QT[oc0][0:64, :]),
                     reads=[QT[oc0].k], writes=[QAb.k])
                S.op("vector", lambda E, oc0=oc0: E.tensor_copy(
                    QBb[64:128, :], QT[oc0][64:128, :]),
                     reads=[QT[oc0].k], writes=[QBb.k])
            for tch in range(16):
                emit_qk(p, tch)
                G = p * 16 + tch
                if G >= PVLAG:
                    emit_pv((G - PVLAG) // 16, (G - PVLAG) % 16)
                if p >= 1 and tch in TAIL_SLOTS:
                    emit_tail(p - 1, TAIL_SLOTS[tch])
        for G in range(512 - PVLAG, 512):
            emit_pv(G // 16, G % 16)
        for step in range(5):
            emit_tail(31, step)

        if dump == "attn":
            _emit_dump(S, AR_F, DUMP)

        y1b2 = vb("S3", 1024)
        sqb2 = vb("S3", 1024, off_extra=2048)

        def layernorm(src, dst_fn, tag):
            for hc in range(16):
                yb = y1b if hc % 2 == 0 else y1b2
                sb_ = sqb if hc % 2 == 0 else sqb2
                S.op("vector", lambda E, hc=hc, yb=yb: E.tensor_copy(
                    yb.ap, src[hc].ap), reads=[src[hc].k], writes=[yb.k])
                S.op("vector", lambda E, yb=yb, sb_=sb_: E.tensor_tensor(
                    sb_.ap, yb.ap, yb.ap, ALU.mult),
                     reads=[yb.k], writes=[sb_.k])
                def s12fn(E, hc=hc, yb=yb, sb_=sb_):
                    inst = None
                    for th in range(2):
                        inst = E.matmul(
                            PS[0].ap()[0:1, th * 512:(th + 1) * 512],
                            ones_bf[0:128, 0:1],
                            yb[0:128, th * 512:(th + 1) * 512],
                            start=(hc == 0), stop=(hc == 15))
                        inst = E.matmul(
                            PS[3].ap()[0:1, th * 512:(th + 1) * 512],
                            ones_bf[0:128, 0:1],
                            sb_[0:128, th * 512:(th + 1) * 512],
                            start=(hc == 0), stop=(hc == 15))
                    return inst
                S.op("tensor", s12fn, reads=[yb.k, sb_.k, ones_bf.k],
                     writes=[PSK[0], PSK[3]])
            S.op("vector", lambda E: E.tensor_scalar(
                S4[0:1, 0:1024], PS[0].ap()[0:1, :], 1.0 / 2048.0, None,
                ALU.mult), reads=[PSK[0]], writes=[S4.k])
            S.op("vector", lambda E: E.tensor_tensor(
                S2[0:1, 0:1024], S4[0:1, 0:1024], S4[0:1, 0:1024], ALU.mult),
                 reads=[S4.k], writes=[S2.k])
            S.op("vector", lambda E: E.scalar_tensor_tensor(
                S3[0:1, 0:1024], PS[3].ap()[0:1, :], 1.0 / 2048.0,
                S2[0:1, 0:1024], ALU.mult, ALU.subtract),
                 reads=[PSK[3], S2.k], writes=[S3.k])
            S.op("vector", lambda E: E.tensor_scalar(
                S3[0:1, 0:1024], S3[0:1, 0:1024], 1e-5, None, ALU.add),
                 reads=[S3.k], writes=[S3.k])
            S.op("scalar", lambda E: E.activation(
                S2[0:1, 0:1024], S3[0:1, 0:1024], AF.Ln),
                 reads=[S3.k], writes=[S2.k])
            S.op("scalar", lambda E: E.activation(
                S3[0:1, 0:1024], S2[0:1, 0:1024], AF.Exp, scale=-0.5),
                 reads=[S2.k], writes=[S3.k])
            S.op("vector", lambda E: E.scalar_tensor_tensor(
                S4[0:1, 0:1024], S4[0:1, 0:1024], -1.0, S3[0:1, 0:1024],
                ALU.mult, ALU.mult), reads=[S3.k, S4.k], writes=[S4.k])
            def bcast(row, psi):
                def fn(E):
                    inst = None
                    for th in range(2):
                        inst = E.matmul(
                            PS[psi].ap()[0:128, th * 512:(th + 1) * 512],
                            ones_f[0:1, 0:128],
                            row[0:1, th * 512:(th + 1) * 512],
                            start=True, stop=True)
                    return inst
                return fn
            S.op("tensor", bcast(S3.ap, 1), reads=[ones_f.k, S3.k],
                 writes=[PSK[1]])
            S.op("tensor", bcast(S4.ap, 2), reads=[ones_f.k, S4.k],
                 writes=[PSK[2]])
            S.op("vector", lambda E: E.tensor_copy(S1.ap, PS[1].ap()[:, :]),
                 reads=[PSK[1]], writes=[S1.k])
            S.op("vector", lambda E: E.tensor_copy(S2.ap, PS[2].ap()[:, :]),
                 reads=[PSK[2]], writes=[S2.k])
            for hc in range(16):
                dst_fn(hc)

        QAf = TV(AR_F[:, L["QA"] // 4: L["QA"] // 4 + 1024], _keys(L["QA"], 4096))

        def ln1_dst(hc):
            tmp = S3 if hc % 2 == 0 else QAf
            S.op("vector", lambda E, hc=hc, tmp=tmp: E.tensor_tensor(
                tmp.ap, y1[hc].ap, S1.ap, ALU.mult),
                 reads=[y1[hc].k, S1.k], writes=[tmp.k])
            S.op("vector", lambda E, hc=hc, tmp=tmp: E.tensor_tensor(
                x1T[hc].ap, tmp.ap, S2.ap, ALU.add),
                 reads=[tmp.k, S2.k], writes=[x1T[hc].k])
        layernorm(y1, ln1_dst, "ln1")

        if dump == "ln1":
            _emit_dump(S, AR_F, DUMP)

        def stage(DRC, idx, slots_u, slots_b, nsl, ring):
            sl = idx % nsl
            S.op(ring, lambda E, sl=sl, DRC=DRC: E.dma_start(
                out=slots_u[sl].ap, in_=DRC), writes=[slots_b[sl].k], dma=True)
            S.op("scalar", lambda E, sl=sl: E.activation(
                slots_b[sl].ap, slots_u[sl].ap, AF.Copy, bias=-128.0),
                 reads=[slots_b[sl].k], writes=[slots_b[sl].k])
            return sl

        def stage2(DRC, idx, lands, slots_b, nsl, ring):
            ld = lands[idx % len(lands)]
            sl = idx % nsl
            S.op(ring, lambda E, ld=ld, DRC=DRC: E.dma_start(
                out=ld.ap, in_=DRC), writes=[ld.k], dma=True)
            S.op("scalar", lambda E, sl=sl, ld=ld: E.activation(
                slots_b[sl].ap, ld.ap, AF.Copy, bias=-128.0),
                 reads=[ld.k], writes=[slots_b[sl].k])
            return sl

        def down_block(g, cb):
            hs = g % 2
            dsl = stage2(DCB[cb, g], 8 * g + cb, d8, dstg, NDS, "gpsimd")
            for sub in range(2):
                co = 2 * cb + sub
                pd = 2 + (co % 2)
                def dfn(E, dsl=dsl, hs=hs, pd=pd, sub=sub):
                    inst = None
                    for th in range(2):
                        for il in range(8):
                            inst = E.matmul(
                                PS[pd].ap()[:, th * 512:(th + 1) * 512],
                                dstg[dsl][:, il * 256 + sub * 128:
                                          il * 256 + sub * 128 + 128],
                                hact[hs][:, il * 1024 + th * 512:
                                         il * 1024 + (th + 1) * 512],
                                start=(il == 0), stop=(il == 7))
                    return inst
                S.op("tensor", dfn, reads=[dstg[dsl].k, hact[hs].k],
                     writes=[PSK[pd]])
                if g == 0:
                    S.op("vector", lambda E, co=co, pd=pd: E.tensor_scalar(
                        zz[co].ap, PS[pd].ap()[:, :], ds_sc[:, co:co + 1],
                        None, ALU.mult),
                         reads=[PSK[pd], ds_sc.k], writes=[zz[co].k])
                else:
                    S.op("vector", lambda E, co=co, pd=pd: E.scalar_tensor_tensor(
                        zz[co].ap, PS[pd].ap()[:, :], ds_sc[:, co:co + 1],
                        zz[co].ap, ALU.mult, ALU.add),
                         reads=[PSK[pd], ds_sc.k, zz[co].k], writes=[zz[co].k])

        for g in range(8):
            hs = g % 2
            for ib in range(4):
                idx = 4 * g + ib
                if idx < 2:
                    ssl = stage(GCB[idx], idx, gstg_u, gstg, NGS, "sync")
                    usl = stage(UCB[idx], idx, ustg_u, ustg, NGS, "scalar")
                else:
                    ssl = stage2(GCB[idx], idx, g8, gstg, NGS, "sync")
                    usl = stage2(UCB[idx], idx, u8_, ustg, NGS, "scalar")
                for sub in range(2):
                    il = 2 * ib + sub
                    ic = 8 * g + il
                    def gufn(E, ssl=ssl, usl=usl, sub=sub, w=0):
                        stg = gstg[ssl] if w == 0 else ustg[usl]
                        pi = 0 if w == 0 else 1
                        inst = None
                        for hc in range(16):
                            for th in range(2):
                                inst = E.matmul(
                                    PS[pi].ap()[:, th * 512:(th + 1) * 512],
                                    stg[:, hc * 256 + sub * 128:
                                        hc * 256 + sub * 128 + 128],
                                    x1T[hc][:, th * 512:(th + 1) * 512],
                                    start=(hc == 0), stop=(hc == 15))
                        return inst
                    S.op("tensor", lambda E, ssl=ssl, usl=usl, sub=sub: gufn(
                        E, ssl, usl, sub, 0),
                         reads=[gstg[ssl].k] + [x.k for x in x1T],
                         writes=[PSK[0]])
                    S.op("tensor", lambda E, ssl=ssl, usl=usl, sub=sub: gufn(
                        E, ssl, usl, sub, 1),
                         reads=[ustg[usl].k] + [x.k for x in x1T],
                         writes=[PSK[1]])
                    S.op("scalar", lambda E, ic=ic: E.activation(
                        S3.ap, PS[0].ap()[:, :], AF.Silu,
                        scale=gs_sc[:, ic:ic + 1]),
                         reads=[PSK[0], gs_sc.k], writes=[S3.k])
                    S.op("vector", lambda E, ic=ic, hs=hs, il=il: E.scalar_tensor_tensor(
                        hact[hs][:, il * 1024:(il + 1) * 1024],
                        PS[1].ap()[:, :], us_sc[:, ic:ic + 1], S3.ap,
                        ALU.mult, ALU.mult),
                         reads=[PSK[1], us_sc.k, S3.k], writes=[hact[hs].k])
                if g > 0:
                    down_block(g - 1, 2 * ib)
                    down_block(g - 1, 2 * ib + 1)
        for cb in range(8):
            down_block(7, cb)

        if dump == "ffn":
            _emit_dump(S, AR_F, DUMP)

        st_ops = []

        def ln2_dst(hc):
            outv = YS if hc % 2 == 0 else S4
            tmp = S3 if hc % 2 == 0 else QAf
            S.op("vector", lambda E, hc=hc, tmp=tmp: E.tensor_tensor(
                tmp.ap, zz[hc].ap, S1.ap, ALU.mult),
                 reads=[zz[hc].k, S1.k], writes=[tmp.k])
            S.op("vector", lambda E, hc=hc, outv=outv, tmp=tmp: E.tensor_tensor(
                outv.ap, tmp.ap, S2.ap, ALU.add),
                 reads=[tmp.k, S2.k], writes=[outv.k])
            st_ops.append(S.op("sync", lambda E, hc=hc, outv=outv: E.dma_start(
                out=OUT[hc * 128:(hc + 1) * 128, :], in_=outv.ap),
                reads=[outv.k], dma=True))
        layernorm(zz, ln2_dst, "ln2")

        S.op("sync", lambda E: E.nop(), deps=st_ops)

        block = ctx.enter_context(nc.Block())
        S.emit(block, csem, dsem)
    return nc


def _emit_dump(S, AR_F, DUMP):
    for hc in range(16):
        w = AR_F.shape[1] // 16 if hc < 15 else AR_F.shape[1] - 15 * (AR_F.shape[1] // 16)
        c0 = hc * (AR_F.shape[1] // 16)
        S.op("sync", lambda E, c0=c0, w=w: E.dma_start(
            out=DUMP[:, c0:c0 + w], in_=AR_F[:, c0:c0 + w]),
            reads=[tuple(("A", u) for u in range(0, 120))], dma=True)


def _prep(inp):
    x = np.asarray(inp["L0_attn_x"], np.float32)
    Wq = np.asarray(inp["L0_attn_Wq"], np.float32)
    Wk = np.asarray(inp["L0_attn_Wk"], np.float32)
    Wv = np.asarray(inp["L0_attn_Wv"], np.float32)
    mem = np.ascontiguousarray(np.asarray(inp["L0_attn_memory"], np.float32))
    mnorm = np.ascontiguousarray(
        np.asarray(inp["L0_attn_memory_norm"], np.float32).reshape(512, 1))
    mw = np.asarray(inp["L0_attn_memory_weight"], np.float32).reshape(1, 1)
    gw = np.asarray(inp["L0_ffn_gate_w"], np.int32)
    uw = np.asarray(inp["L0_ffn_up_w"], np.int32)
    dw = np.asarray(inp["L0_ffn_down_w"], np.int32)
    gw8 = np.ascontiguousarray(gw.view(np.uint8).reshape(8192, 2048, 4)[:, :, 0])
    uw8 = np.ascontiguousarray(uw.view(np.uint8).reshape(8192, 2048, 4)[:, :, 0])
    dw8 = np.ascontiguousarray(dw.view(np.uint8).reshape(2048, 8192, 4)[:, :, 0])
    gs = np.asarray(inp["L0_ffn_gate_scale"], np.float32)
    us = np.asarray(inp["L0_ffn_up_scale"], np.float32)
    ds = np.asarray(inp["L0_ffn_down_scale"], np.float32)

    common = dict(
        WQC=np.ascontiguousarray(
            Wq.reshape(16, 128, 16, 128).transpose(0, 3, 2, 1)
            .reshape(16, 128, 2048)),
        WKC=np.ascontiguousarray(
            Wk.reshape(4, 128, 16, 128).transpose(0, 3, 2, 1)
            .reshape(4, 128, 2048)),
        WVC=np.ascontiguousarray(
            Wv.reshape(4, 128, 16, 128).transpose(0, 3, 2, 1)
            .reshape(4, 128, 2048)),
        MEM=mem, MNORM=mnorm, MW=np.ascontiguousarray(mw),
        GCB=np.ascontiguousarray(
            gw8.reshape(32, 256, 16, 128).transpose(0, 3, 2, 1)
            .reshape(32, 128, 4096)),
        UCB=np.ascontiguousarray(
            uw8.reshape(32, 256, 16, 128).transpose(0, 3, 2, 1)
            .reshape(32, 128, 4096)),
        DCB=np.ascontiguousarray(
            dw8.reshape(8, 256, 8, 8, 128).transpose(0, 2, 4, 3, 1)
            .reshape(8, 8, 128, 2048)),
        GS=np.ascontiguousarray(gs.reshape(64, 128)),
        US=np.ascontiguousarray(us.reshape(64, 128)),
        DS=np.ascontiguousarray(ds.reshape(16, 128)),
    )
    in_maps = []
    xTs = {}
    for c in range(8):
        beta, half = c // 2, c % 2
        if beta not in xTs:
            xTs[beta] = np.ascontiguousarray(x[beta].T)
        xT = xTs[beta]
        if half == 0:
            XT = xT
        else:
            XT = np.ascontiguousarray(
                np.concatenate([xT[:, 1024:], xT[:, :1024]], axis=1))
        m = dict(common)
        m["XT"] = XT
        in_maps.append(m)
    return in_maps


def kernel(**inputs):
    in_maps = _prep(inputs)
    nc = build_nc()
    res = run_bass_kernel_spmd(nc, in_maps, core_ids=list(range(8)))
    out = np.empty((4, 2048, 2048), np.float32)
    for c in range(8):
        beta, half = c // 2, c % 2
        out[beta, half * 1024:(half + 1) * 1024, :] = res.results[c]["OUT"].T
    return out


# version 29 (final)
# speedup vs baseline: 1.0689x; 1.0104x over previous
import numpy as np
import contextlib
import concourse.bass as bass
import concourse.mybir as mybir
from concourse.bass_utils import run_bass_kernel_spmd

F32 = mybir.dt.float32
BF16 = mybir.dt.bfloat16
I32 = mybir.dt.int32
U8 = mybir.dt.uint8
AF = mybir.ActivationFunctionType
ALU = mybir.AluOpType
AX = mybir.AxisListType.X

ENGINES = ("tensor", "scalar", "vector", "gpsimd", "sync")


class _Op:
    __slots__ = ("eng", "fn", "deps", "dma", "ord", "awaited", "idx")


class Sched:

    def __init__(self):
        self.ops = []
        self.state = {}

    def op(self, eng, fn, reads=(), writes=(), deps=(), dma=False):
        o = _Op()
        o.eng, o.fn, o.dma = eng, fn, dma
        o.awaited, o.ord = False, None
        o.idx = len(self.ops)
        dep = set()
        for d in deps:
            if d is not None:
                dep.add(d)
        for ks in reads:
            for k in ks:
                st = self.state.get(k)
                if st and st[0] is not None:
                    dep.add(st[0])
        for ks in writes:
            for k in ks:
                st = self.state.get(k)
                if st:
                    if st[0] is not None:
                        dep.add(st[0])
                    dep.update(st[1])
        o.deps = list(dep)
        self.ops.append(o)
        for ks in reads:
            for k in ks:
                self.state.setdefault(k, [None, []])[1].append(o)
        for ks in writes:
            for k in ks:
                self.state[k] = [o, []]
        return o

    def emit(self, block, csem, dsem, nring=8):
        for o in self.ops:
            for d in o.deps:
                d.awaited = True
        ccnt = {e: 0 for e in ENGINES}
        dcnt = {e: 0 for e in ENGINES}
        for o in self.ops:
            if o.dma:
                i = dcnt[o.eng]
                dcnt[o.eng] += 1
                o.ord = (i % nring, 16 * (i // nring + 1), i)
                o.awaited = True
            elif o.awaited:
                ccnt[o.eng] += 1
                o.ord = ccnt[o.eng]

        def mk(eng):
            my_ops = [o for o in self.ops if o.eng == eng]

            def body(E):
                waited = {}
                for o in my_ops:
                    best = {}
                    for d in o.deps:
                        if d.dma:
                            k = (d.eng, "d", d.ord[0])
                            if k not in best or best[k].idx < d.idx:
                                best[k] = d
                        else:
                            k = (d.eng, "c")
                            if k not in best or best[k].idx < d.idx:
                                best[k] = d
                    for k, d in best.items():
                        if d.dma:
                            sem, val = dsem[d.eng][d.ord[0]], d.ord[1]
                        else:
                            sem, val = csem[d.eng], d.ord
                        if waited.get(k, 0) < val:
                            E.wait_ge(sem, val)
                            waited[k] = val
                    if o.dma:
                        slot, val, i = o.ord
                        if i >= nring:
                            k = (eng, "d", slot)
                            if waited.get(k, 0) < val - 16:
                                E.wait_ge(dsem[eng][slot], val - 16)
                                waited[k] = val - 16
                    inst = o.fn(E)
                    if o.dma:
                        inst.then_inc(dsem[eng][o.ord[0]], 16)
                    elif o.awaited:
                        inst.then_inc(csem[eng], 1)
            return body

        for eng in ENGINES:
            getattr(block, eng)(mk(eng))


L = {}


def _lay(name, nbytes, align=64):
    off = _lay.cur
    off = (off + align - 1) // align * align
    L[name] = off
    _lay.cur = off + nbytes
    return off


_lay.cur = 0
_lay("ident_bf", 256)
_lay("ident_f", 512)
_lay("ones_f", 512)
_lay("ones_bf", 256)
_lay("bias_all", 128)
_lay("brow", 384)
_lay("negr_bf", 64)
_lay("maxsq", 64)
_lay("maxk", 64)
_lay("gs_sc", 256)
_lay("us_sc", 256)
_lay("ds_sc", 64)
_lay("wcst", 32)
_lay("scnat", 512)
_lay("mn_bf", 16)
_lay("qmask", 128)
_lay("YS", 4096)
_lay("S1", 4096)
_lay("S2", 4096)
_lay("S3", 4096)
_lay("S4", 4096)
_lay("rowbf", 2048)
_lay("QA", 2048)
_lay("QB", 2048)
_lay("P2", 32768)
_lay("P3", 16384 + 32768)
_lay("P4", 65536)
_lay("P5", 32768)
AR_BYTES = (_lay.cur + 63) // 64 * 64
assert AR_BYTES <= 212800, AR_BYTES

D_KEY_UNIT = 2048


def _keys(off, nbytes):
    return tuple(("A", u) for u in range(off // D_KEY_UNIT,
                                         (off + nbytes - 1) // D_KEY_UNIT + 1))


class TV:

    def __init__(self, ap, keys):
        self.ap = ap
        self.k = keys

    def __getitem__(self, sl):
        return self.ap[sl]


def build_nc(dump=None):
    nc = bass.Bass("TRN2", target_bir_lowering=False)

    XT = nc.dram_tensor("XT", [2048, 2048], F32, kind="ExternalInput").ap()
    WQC = nc.dram_tensor("WQC", [16, 128, 2048], F32, kind="ExternalInput").ap()
    WKC = nc.dram_tensor("WKC", [4, 128, 2048], F32, kind="ExternalInput").ap()
    WVC = nc.dram_tensor("WVC", [4, 128, 2048], F32, kind="ExternalInput").ap()
    MEM = nc.dram_tensor("MEM", [512, 512], F32, kind="ExternalInput").ap()
    MNORM = nc.dram_tensor("MNORM", [512, 1], F32, kind="ExternalInput").ap()
    MW = nc.dram_tensor("MW", [1, 1], F32, kind="ExternalInput").ap()
    GCB = nc.dram_tensor("GCB", [32, 128, 4096], U8, kind="ExternalInput").ap()
    UCB = nc.dram_tensor("UCB", [32, 128, 4096], U8, kind="ExternalInput").ap()
    DCB = nc.dram_tensor("DCB", [8, 8, 128, 2048], U8, kind="ExternalInput").ap()
    GS = nc.dram_tensor("GS", [64, 128], F32, kind="ExternalInput").ap()
    US = nc.dram_tensor("US", [64, 128], F32, kind="ExternalInput").ap()
    DS = nc.dram_tensor("DS", [16, 128], F32, kind="ExternalInput").ap()
    OUT = nc.dram_tensor("OUT", [2048, 1024], F32, kind="ExternalOutput").ap()
    DUMP = None
    if dump is not None:
        DUMP = nc.dram_tensor("DUMP", [128, AR_BYTES // 4], F32,
                              kind="ExternalOutput").ap()

    with contextlib.ExitStack() as ctx:
        AR = ctx.enter_context(nc.sbuf_tensor("AR", [128, AR_BYTES // 4], F32))
        ARB = AR.bitcast(BF16)
        AR_F = AR.ap()
        AR_B = ARB.ap()
        PS = [ctx.enter_context(nc.psum_tensor("PS%d" % i, [128, 1024], F32))
              for i in range(4)]
        PSB = [p.bitcast(BF16) for p in PS]
        PSK = [(("PS", i),) for i in range(4)]

        csem = {e: ctx.enter_context(nc.semaphore("c_" + e)) for e in ENGINES}
        dsem = {e: [ctx.enter_context(nc.semaphore("d_%s%d" % (e, i)))
                    for i in range(8)]
                for e in ("gpsimd", "sync", "scalar")}

        def vf(name, cols, off_extra=0):
            off = L[name] + off_extra
            return TV(AR_F[:, off // 4: off // 4 + cols], _keys(off, cols * 4))

        def vb(name, cols, off_extra=0):
            off = L[name] + off_extra
            return TV(AR_B[:, off // 2: off // 2 + cols], _keys(off, cols * 2))

        ident_bf = vb("ident_bf", 128)
        ident_f = vf("ident_f", 128)
        ones_f = vf("ones_f", 128)
        ones_bf = vb("ones_bf", 128)
        bias_all = vf("bias_all", 32)
        brow = vf("brow", 96)
        negr_bf = vb("negr_bf", 32)
        maxsq = vf("maxsq", 16)
        maxk = vf("maxk", 16)
        gs_sc = vf("gs_sc", 64)
        us_sc = vf("us_sc", 64)
        ds_sc = vf("ds_sc", 16)
        wcst = vf("wcst", 8)
        scnat = vf("scnat", 128)
        mn_bf = vb("mn_bf", 8)
        qmask = vb("qmask", 33)
        YS = vf("YS", 1024)
        y1b = vb("YS", 1024)
        sqb = vb("YS", 1024, off_extra=2048)
        S1 = vf("S1", 1024)
        S2 = vf("S2", 1024)
        S3 = vf("S3", 1024)
        S4 = vf("S4", 1024)
        KSQ = TV(AR_F[:, L["S1"] // 4: L["S1"] // 4 + 2048],
                 _keys(L["S1"], 8192))
        rowbf = vb("rowbf", 1024)

        QT = [vb("P2", 1024, off_extra=i * 2048) for i in range(16)]
        x1T = QT
        QAb = vb("QA", 1024)
        QBb = vb("QB", 1024)
        khat = [vb("P3", 2048, off_extra=kc * 4096) for kc in range(4)]
        vhat = [vb("P3", 2048, off_extra=16384 + h * 4096) for h in range(8)]
        xbT = [vb("P4", 2048, off_extra=hc * 4096) for hc in range(16)]
        y1 = [vf("P4", 1024, off_extra=hc * 4096) for hc in range(16)]
        zz = y1
        NWS = 3
        wslot_f = [vf("P5", 2048, off_extra=s * 8192) for s in range(NWS)]
        wslot = [vb("P5", 2048, off_extra=s * 8192) for s in range(NWS)]
        vt = [vb("S1", 2048), vb("S2", 2048)]
        mem_bf = vb("P5", 2048, off_extra=24576)
        expT = [vb("P5", 1024, off_extra=s * 2048) for s in range(8)]
        sig = [vb("P5", 1024, off_extra=16384 + s * 2048) for s in range(4)]
        hact = [vb("P5", 8192, off_extra=s * 16384) for s in range(2)]
        AR_I = AR.bitcast(I32).ap()

        def vi(name, cols, off_extra=0):
            off = L[name] + off_extra
            return TV(AR_I[:, off // 4: off // 4 + cols], _keys(off, cols * 4))

        AR_U = AR.bitcast(U8).ap()

        def vu(name, cols, off_extra=0):
            off = L[name] + off_extra
            return TV(AR_U[:, off: off + cols], _keys(off, cols))

        NGS = 2
        NDS = 3
        gstg_u = [vu("P3", 4096, off_extra=s * 8192 + 4096) for s in range(NGS)]
        gstg = [vb("P3", 4096, off_extra=s * 8192) for s in range(NGS)]
        ustg_u = [vu("P3", 4096, off_extra=16384 + s * 8192 + 4096) for s in range(NGS)]
        ustg = [vb("P3", 4096, off_extra=16384 + s * 8192) for s in range(NGS)]
        dstg_u = [vu("P3", 2048, off_extra=32768 + s * 4096 + 2048) for s in range(NDS)]
        dstg = [vb("P3", 2048, off_extra=32768 + s * 4096) for s in range(NDS)]
        g8 = [vu("S1", 4096), vu("S2", 4096)]
        u8_ = [vu("YS", 4096), vu("S4", 4096)]
        d8 = [vu("QA", 2048), vu("QB", 2048), vu("rowbf", 2048)]

        S = Sched()

        if dump is not None:
            S.op("gpsimd", lambda E: E.memset(AR_F[:, :], 0.0),
                 writes=[tuple(("A", u) for u in range(0, 120))])
        S.op("gpsimd", lambda E: E.memset(ones_f.ap, 1.0), writes=[ones_f.k])
        S.op("gpsimd", lambda E: E.memset(ones_bf.ap, 1.0), writes=[ones_bf.k])
        S.op("gpsimd", lambda E: E.memset(wcst[0:128, 4:5], -128.0),
             writes=[wcst.k])
        S.op("gpsimd", lambda E: E.memset(QAb[64:128, :], 0.0), writes=[QAb.k])
        S.op("gpsimd", lambda E: E.memset(QBb[0:64, :], 0.0), writes=[QBb.k])
        S.op("gpsimd", lambda E: E.memset(qmask.ap, 0.0), writes=[qmask.k])
        S.op("gpsimd", lambda E: E.memset(qmask[0:64, 0:1], 1.0),
             writes=[qmask.k])
        S.op("gpsimd", lambda E: E.memset(qmask[64:128, 32:33], 1.0),
             writes=[qmask.k])
        S.op("gpsimd", lambda E: E.affine_select(
            ident_bf.ap, ones_bf.ap, [[-1, 128]], ALU.is_equal, 0.0,
            base=0, channel_multiplier=1), reads=[ones_bf.k], writes=[ident_bf.k])
        S.op("gpsimd", lambda E: E.affine_select(
            ident_f.ap, ones_f.ap, [[-1, 128]], ALU.is_equal, 0.0,
            base=0, channel_multiplier=1), reads=[ones_f.k], writes=[ident_f.k])

        S.op("sync", lambda E: E.dma_start(out=wcst[0:1, 0:1], in_=MW[:, :]),
             writes=[wcst.k], dma=True)
        S.op("sync", lambda E: E.dma_start(out=wcst[64:65, 0:1], in_=MW[:, :]),
             writes=[wcst.k], dma=True)
        for b in (0, 64):
            S.op("scalar", lambda E, b=b: E.activation(
                wcst[b:b + 1, 1:2], wcst[b:b + 1, 0:1], AF.Sigmoid),
                reads=[wcst.k], writes=[wcst.k])
            S.op("vector", lambda E, b=b: E.tensor_scalar(
                wcst[b:b + 1, 2:3], wcst[b:b + 1, 1:2], -1.0, 1.0,
                ALU.mult, ALU.add), reads=[wcst.k], writes=[wcst.k])
            S.op("scalar", lambda E, b=b: E.activation(
                wcst[b:b + 1, 3:4], wcst[b:b + 1, 2:3], AF.Ln),
                reads=[wcst.k], writes=[wcst.k])

        for (nat_rows, DR, dst) in ((64, GS, gs_sc), (64, US, us_sc),
                                    (16, DS, ds_sc)):
            S.op("sync", lambda E, DR=DR, r=nat_rows: E.dma_start(
                out=scnat[0:r, 0:128], in_=DR[:, :]), writes=[scnat.k], dma=True)
            S.op("tensor", lambda E, r=nat_rows: E.transpose(
                PS[0].ap()[0:128, 0:r], scnat[0:r, 0:128], ident_f[0:r, 0:r]),
                reads=[scnat.k, ident_f.k], writes=[PSK[0]])
            S.op("vector", lambda E, dst=dst, r=nat_rows: E.tensor_copy(
                dst[0:128, 0:r], PS[0].ap()[0:128, 0:r]),
                reads=[PSK[0]], writes=[dst.k])

        S.op("gpsimd", lambda E: E.dma_start(
            out=mem_bf.ap, in_=MEM.rearrange("(c p) j -> p c j", p=128)),
            writes=[mem_bf.k], dma=True)
        S.op("sync", lambda E: E.dma_start(
            out=scnat[0:4, 0:128], in_=MNORM.rearrange("(c p) o -> c (p o)", p=128)),
            writes=[scnat.k], dma=True)
        S.op("tensor", lambda E: E.transpose(
            PS[0].ap()[0:128, 0:4], scnat[0:4, 0:128], ident_f[0:4, 0:4]),
            reads=[scnat.k, ident_f.k], writes=[PSK[0]])
        S.op("vector", lambda E: E.tensor_copy(mn_bf[0:128, 0:4],
                                               PS[0].ap()[0:128, 0:4]),
             reads=[PSK[0]], writes=[mn_bf.k])

        xstg = [TV(AR_F[:, L["S1"] // 4: L["S1"] // 4 + 2048],
                   _keys(L["S1"], 8192)),
                TV(AR_F[:, L["S3"] // 4: L["S3"] // 4 + 2048],
                   _keys(L["S3"], 8192)),
                TV(AR_F[:, L["P2"] // 4: L["P2"] // 4 + 2048],
                   _keys(L["P2"], 8192)),
                TV(AR_F[:, L["P2"] // 4 + 2048: L["P2"] // 4 + 4096],
                   _keys(L["P2"] + 8192, 8192))]
        _wl = [0]

        def wload_dma(DRC, idx):
            slot = _wl[0] % NWS
            _wl[0] += 1
            op = S.op("scalar", lambda E, slot=slot: E.dma_start(
                out=wslot_f[slot].ap, in_=DRC[idx]),
                writes=[wslot_f[slot].k], dma=True)
            return slot, op

        def wload_cast(slot):
            S.op("scalar", lambda E, slot=slot: E.activation(
                wslot[slot].ap, wslot_f[slot].ap, AF.Copy),
                 reads=[wslot_f[slot].k], writes=[wslot_f[slot].k])

        def wload(DRC, idx):
            slot, op = wload_dma(DRC, idx)
            wload_cast(slot)
            return slot, op

        wpref = [wload_dma(WKC, kc) for kc in range(2)]
        for hc in range(16):
            xsl = hc % 4
            ring = "sync" if xsl < 2 else "gpsimd"
            ceng = "scalar" if hc % 2 == 0 else "vector"
            S.op(ring, lambda E, hc=hc, xsl=xsl: E.dma_start(
                out=xstg[xsl].ap, in_=XT[hc * 128:(hc + 1) * 128, :]),
                writes=[xstg[xsl].k], dma=True)
            if ceng == "scalar":
                S.op("scalar", lambda E, hc=hc, xsl=xsl: E.activation(
                    xbT[hc].ap, xstg[xsl].ap, AF.Copy),
                     reads=[xstg[xsl].k], writes=[xbT[hc].k])
            else:
                S.op("vector", lambda E, hc=hc, xsl=xsl: E.tensor_copy(
                    xbT[hc].ap, xstg[xsl].ap),
                     reads=[xstg[xsl].k], writes=[xbT[hc].k])

        def proj_group(slot, ps_lo, ps_hi, tq_n, psk_writes):
            for hc in range(16):
                def fn(E, hc=hc):
                    inst = None
                    for tq in range(tq_n):
                        p = PS[ps_lo] if tq < 2 else PS[ps_hi]
                        col = (tq % 2) * 512
                        inst = E.matmul(
                            p.ap()[:, col:col + 512],
                            wslot[slot][:, hc * 128:(hc + 1) * 128],
                            xbT[hc][:, tq * 512:(tq + 1) * 512],
                            start=(hc == 0), stop=(hc == 15))
                    return inst
                S.op("tensor", fn, reads=[wslot[slot].k, xbT[hc].k],
                     writes=psk_writes)

        for kc in range(4):
            if kc < 2:
                sl, _ = wpref[kc]
                wload_cast(sl)
            else:
                sl, _ = wload(WKC, kc)
            pl = 2 * (kc % 2)
            proj_group(sl, pl, pl + 1, 4, [PSK[pl], PSK[pl + 1]])
            for half in range(2):
                S.op("vector", lambda E, kc=kc, half=half, pl=pl: E.tensor_copy(
                    khat[kc][:, half * 1024:(half + 1) * 1024],
                    PS[pl + half].ap()[:, :]),
                    reads=[PSK[pl + half]], writes=[khat[kc].k])
        for kc in range(4):
            sl, _ = wload(WVC, kc)
            vsl = kc % 2
            pl = 2 * (kc % 2)
            proj_group(sl, pl, pl + 1, 4, [PSK[pl], PSK[pl + 1]])
            for half in range(2):
                S.op("vector", lambda E, half=half, vsl=vsl, pl=pl: E.tensor_copy(
                    vt[vsl][:, half * 1024:(half + 1) * 1024],
                    PS[pl + half].ap()[:, :]),
                    reads=[PSK[pl + half]], writes=[vt[vsl].k])
            for hsub in range(2):
                h = 2 * kc + hsub
                b = hsub * 64
                pt = 2 + (kc % 2)
                def trf(E, b=b, sl=vsl, pt=pt):
                    inst = None
                    for tch in range(16):
                        inst = E.transpose(
                            PSB[pt].ap()[0:128, tch * 64:(tch + 1) * 64],
                            vt[sl][b:b + 64, tch * 128:(tch + 1) * 128],
                            ident_bf[b:b + 64, b:b + 64])
                    return inst
                S.op("tensor", trf, reads=[vt[vsl].k, ident_bf.k],
                     writes=[PSK[pt]])
                S.op("vector", lambda E, h=h, pt=pt: E.tensor_copy(
                    vhat[h].ap.rearrange("p (c w) -> p c w", w=128)[:, :, 0:64],
                    PSB[pt].ap()[0:128, 0:1024].rearrange("p (c w) -> p c w", w=64)),
                    reads=[PSK[pt]], writes=[vhat[h].k])
                S.op("gpsimd", lambda E, h=h: E.memset(
                    vhat[h].ap.rearrange("p (c w) -> p c w", w=128)[:, :, 64:65],
                    1.0), writes=[vhat[h].k])
                S.op("gpsimd", lambda E, h=h: E.memset(
                    vhat[h].ap.rearrange("p (c w) -> p c w", w=128)[:, :, 65:128],
                    0.0), writes=[vhat[h].k])
        sqA = vb("S3", 1024)
        sqB = vb("YS", 1024)
        KSQB = vb("S1", 2048)

        def qbias(oc):
            sq = sqA if oc % 2 == 0 else sqB
            pb = 1 + 2 * (oc % 2)
            S.op("vector", lambda E, oc=oc, sq=sq: E.tensor_tensor(
                sq.ap, QT[oc].ap, QT[oc].ap, ALU.mult),
                reads=[QT[oc].k], writes=[sq.k])
            def qrowfn(E, pb=pb, sq=sq):
                inst = None
                for th in range(2):
                    inst = E.matmul(
                        PS[pb].ap()[0:33, th * 512:(th + 1) * 512],
                        qmask[0:128, 0:33],
                        sq[0:128, th * 512:(th + 1) * 512],
                        start=True, stop=True)
                return inst
            S.op("tensor", qrowfn, reads=[qmask.k, sq.k], writes=[PSK[pb]])
            S.op("vector", lambda E, oc=oc, pb=pb: E.reduce_max(
                brow[0:33, oc:oc + 1], PS[pb].ap()[0:33, 0:1024], axis=AX),
                reads=[PSK[pb]], writes=[brow.k])

        def kbias(kc):
            S.op("vector", lambda E, kc=kc: E.tensor_tensor(
                KSQB.ap, khat[kc].ap, khat[kc].ap, ALU.mult),
                reads=[khat[kc].k], writes=[KSQB.k])
            for tha in range(2):
                pb = 1 + 2 * tha
                def krowfn(E, pb=pb, tha=tha):
                    inst = None
                    for tq in range(2):
                        inst = E.matmul(
                            PS[pb].ap()[0:33, tq * 512:(tq + 1) * 512],
                            qmask[0:128, 0:33],
                            KSQB.ap[0:128, (2 * tha + tq) * 512:
                                    (2 * tha + tq + 1) * 512],
                            start=True, stop=True)
                    return inst
                S.op("tensor", krowfn, reads=[qmask.k, KSQB.k],
                     writes=[PSK[pb]])
                S.op("vector", lambda E, kc=kc, pb=pb, tha=tha: E.reduce_max(
                    brow[0:33, 16 + 4 * tha + kc:17 + 4 * tha + kc],
                    PS[pb].ap()[0:33, 0:1024], axis=AX),
                    reads=[PSK[pb]], writes=[brow.k])
            S.op("vector", lambda E, kc=kc: E.tensor_tensor(
                brow[0:33, 24 + kc:25 + kc], brow[0:33, 16 + kc:17 + kc],
                brow[0:33, 20 + kc:21 + kc], ALU.max),
                reads=[brow.k], writes=[brow.k])

        for oc in range(16):
            sl, _ = wload(WQC, oc)
            pp = 2 * (oc % 2)
            proj_group(sl, pp, pp + 1, 2, [PSK[pp]])
            S.op("vector", lambda E, oc=oc, pp=pp: E.tensor_copy(
                QT[oc].ap, PS[pp].ap()[:, :]),
                reads=[PSK[pp]], writes=[QT[oc].k])
            qbias(oc)
            if oc < 4:
                kbias(oc)

        for hc in range(16):
            S.op("sync", lambda E, hc=hc: E.dma_start(
                out=y1[hc].ap, in_=XT[hc * 128:(hc + 1) * 128, 0:1024]),
                writes=[y1[hc].k], dma=True)
        def prodfn(E):
            inst = None
            for oc in range(16):
                kc = oc % 4
                inst = E.tensor_tensor(brow[0:33, 28 + oc:29 + oc],
                                       brow[0:33, oc:oc + 1],
                                       brow[0:33, 24 + kc:25 + kc], ALU.mult)
            return inst
        S.op("vector", prodfn, reads=[brow.k], writes=[brow.k])
        S.op("vector", lambda E: E.memset(negr_bf[0:33, 0:32], 0.0),
             writes=[negr_bf.k])
        for par, sub in ((0, 0), (64, 1)):
            pr = 32 if par else 0
            S.op("scalar", lambda E, pr=pr: E.activation(
                S4[pr:pr + 1, 0:16], brow[pr:pr + 1, 28:44], AF.Ln),
                 reads=[brow.k], writes=[S4.k])
            S.op("scalar", lambda E, pr=pr: E.activation(
                S4[pr:pr + 1, 16:32], S4[pr:pr + 1, 0:16], AF.Exp, scale=0.5),
                 reads=[S4.k], writes=[S4.k])
            S.op("vector", lambda E, pr=pr: E.tensor_scalar(
                S4[pr:pr + 1, 32:48], S4[pr:pr + 1, 16:32], -0.125, None,
                ALU.mult), reads=[S4.k], writes=[S4.k])
            S.op("vector", lambda E, pr=pr, sub=sub: E.tensor_copy(
                negr_bf.ap.rearrange("p (a b) -> p a b", b=2)[pr:pr + 1, :, sub:sub + 1],
                S4[pr:pr + 1, 32:48]),
                 reads=[S4.k], writes=[negr_bf.k])
        def bcastfn(E):
            E.matmul(PS[3].ap()[0:128, 0:32], ones_bf[0:1, 0:128],
                     negr_bf[0:1, 0:32], start=True, stop=False)
            return E.matmul(PS[3].ap()[0:128, 0:32], ones_bf[32:33, 0:128],
                            negr_bf[32:33, 0:32], start=False, stop=True)
        S.op("tensor", bcastfn, reads=[ones_bf.k, negr_bf.k], writes=[PSK[3]])
        S.op("vector", lambda E: E.tensor_copy(bias_all.ap,
                                               PS[3].ap()[0:128, 0:32]),
             reads=[PSK[3]], writes=[bias_all.k])

        if dump == "qkv":
            _emit_dump(S, AR_F, DUMP)

        for g in range(4):
            for c in range(4):
                oc = 4 * g + c
                sA, sB = (S1, S2) if c % 2 == 0 else (S3, YS)
                S.op("vector", lambda E, oc=oc, sA=sA: E.tensor_scalar_min(
                    sA.ap, QT[oc].ap, 0.0), reads=[QT[oc].k], writes=[sA.k])
                S.op("scalar", lambda E, sA=sA, sB=sB: E.activation(
                    sB.ap, sA.ap, AF.Exp), reads=[sA.k], writes=[sB.k])
                S.op("vector", lambda E, oc=oc, sA=sA: E.tensor_scalar_max(
                    sA.ap, QT[oc].ap, 0.0), reads=[QT[oc].k], writes=[sA.k])
                S.op("vector", lambda E, c=c, sA=sA, sB=sB: E.scalar_tensor_tensor(
                    sig[c].ap, sB.ap, -1.0, sA.ap, ALU.add, ALU.add),
                     reads=[sA.k, sB.k], writes=[sig[c].k])
            def nrmfn(E):
                inst = None
                for th in range(2):
                    for c in range(4):
                        inst = E.matmul(
                            PS[3].ap()[0:1, th * 512:(th + 1) * 512],
                            mn_bf[0:128, c:c + 1],
                            sig[c][:, th * 512:(th + 1) * 512],
                            start=(c == 0), stop=(c == 3))
                return inst
            S.op("tensor", nrmfn, reads=[mn_bf.k] + [t.k for t in sig],
                 writes=[PSK[3]])
            S.op("vector", lambda E: E.reciprocal(S4[0:1, 0:1024],
                                                  PS[3].ap()[0:1, :]),
                 reads=[PSK[3]], writes=[S4.k])
            S.op("vector", lambda E: E.tensor_scalar(
                rowbf[0:1, 0:1024], S4[0:1, 0:1024], wcst[0:1, 1:2], None,
                ALU.mult), reads=[S4.k, wcst.k], writes=[rowbf.k])
            def psi2fn(E):
                inst = None
                for th in range(2):
                    inst = E.matmul(PS[3].ap()[0:128, th * 512:(th + 1) * 512],
                                    ones_bf[0:1, 0:128],
                                    rowbf[0:1, th * 512:(th + 1) * 512],
                                    start=True, stop=True)
                return inst
            S.op("tensor", psi2fn, reads=[ones_bf.k, rowbf.k], writes=[PSK[3]])
            S.op("vector", lambda E: E.tensor_copy(S3.ap, PS[3].ap()[:, :]),
                 reads=[PSK[3]], writes=[S3.k])
            for jc in range(4):
                pj = jc % 2
                def amfn(E, pj=pj, jc=jc):
                    inst = None
                    for th in range(2):
                        for c in range(4):
                            inst = E.matmul(
                                PS[pj].ap()[:, th * 512:(th + 1) * 512],
                                mem_bf[:, c * 512 + jc * 128: c * 512 + (jc + 1) * 128],
                                sig[c][:, th * 512:(th + 1) * 512],
                                start=(c == 0), stop=(c == 3))
                    return inst
                S.op("tensor", amfn, reads=[mem_bf.k] + [t.k for t in sig],
                     writes=[PSK[pj]])
                tmp = S1 if jc % 2 == 0 else S2
                S.op("vector", lambda E, pj=pj, tmp=tmp: E.tensor_tensor(
                    tmp.ap, PS[pj].ap()[:, :], S3.ap, ALU.mult),
                     reads=[PSK[pj], S3.k], writes=[tmp.k])
                hc = 4 * g + jc
                S.op("vector", lambda E, hc=hc, tmp=tmp: E.tensor_tensor(
                    y1[hc].ap, y1[hc].ap, tmp.ap, ALU.add),
                     reads=[tmp.k, y1[hc].k], writes=[y1[hc].k])

        if dump == "amem":
            _emit_dump(S, AR_F, DUMP)

        def emit_qk(p, tch):
            q, h = p // 8, p % 8
            oc, b, kc = q * 4 + h // 2, (h % 2) * 64, h // 2
            G = p * 16 + tch
            pc = 2 * (G % 2)
            sl = G % 8
            qh = QAb if b == 0 else QBb
            def qkfn(E, kc=kc, tch=tch, pc=pc, qh=qh):
                inst = None
                for sh in range(2):
                    inst = E.matmul(
                        PS[pc].ap()[:, sh * 512:(sh + 1) * 512],
                        khat[kc][0:128, tch * 128:(tch + 1) * 128],
                        qh[0:128, sh * 512:(sh + 1) * 512],
                        start=True, stop=True)
                return inst
            S.op("tensor", qkfn, reads=[khat[kc].k, qh.k], writes=[PSK[pc]])
            S.op("scalar", lambda E, sl=sl, p=p, pc=pc: E.activation(
                expT[sl].ap, PS[pc].ap()[:, :], AF.Exp,
                scale=0.125, bias=bias_all[:, p:p + 1]),
                 reads=[PSK[pc], bias_all.k], writes=[expT[sl].k])

        def emit_pv(p, tch):
            h = p % 8
            pa = 1 + 2 * (p % 2)
            sl = (p * 16 + tch) % 8
            def pvfn(E, h=h, tch=tch, sl=sl, pa=pa):
                inst = None
                for sh in range(2):
                    inst = E.matmul(
                        PS[pa].ap()[:, sh * 512:(sh + 1) * 512],
                        vhat[h][:, tch * 128: tch * 128 + 128],
                        expT[sl][:, sh * 512:(sh + 1) * 512],
                        start=(tch == 0), stop=(tch == 15))
                return inst
            S.op("tensor", pvfn, reads=[vhat[h].k, expT[sl].k],
                 writes=[PSK[pa]])

        def emit_tail(p, step):
            q, h = p // 8, p % 8
            oc, b = q * 4 + h // 2, (h % 2) * 64
            pa = 1 + 2 * (p % 2)
            if step == 0:
                S.op("scalar", lambda E, pa=pa: E.activation(
                    S4[64:65, 0:1024], PS[pa].ap()[64:65, :], AF.Ln),
                     reads=[PSK[pa]], writes=[S4.k])
                S.op("scalar", lambda E: E.activation(
                    rowbf[64:65, 0:1024], S4[64:65, 0:1024], AF.Exp,
                    scale=-1.0, bias=wcst[64:65, 3:4]),
                     reads=[S4.k, wcst.k], writes=[rowbf.k])
            elif step == 1:
                def psifn(E, pa=pa):
                    inst = None
                    for sh in range(2):
                        inst = E.matmul(
                            PS[pa].ap()[64:128, sh * 512:(sh + 1) * 512],
                            ones_bf[64:65, 0:64],
                            rowbf[64:65, sh * 512:(sh + 1) * 512],
                            start=True, stop=True, tile_position=(64, 64))
                    return inst
                S.op("tensor", psifn, reads=[ones_bf.k, rowbf.k],
                     writes=[PSK[pa]])
            elif step == 2:
                S.op("vector", lambda E, pa=pa: E.tensor_copy(
                    S2[0:64, 0:1024], PS[pa].ap()[64:128, :]),
                     reads=[PSK[pa]], writes=[S2.k])
            elif step == 3:
                if b == 0:
                    S.op("vector", lambda E, pa=pa: E.tensor_tensor(
                        S1[0:64, 0:1024], PS[pa].ap()[0:64, :],
                        S2[0:64, 0:1024], ALU.mult),
                         reads=[PSK[pa], S2.k], writes=[S1.k])
                else:
                    S.op("vector", lambda E, pa=pa: E.tensor_tensor(
                        rowbf[0:64, 0:1024], PS[pa].ap()[0:64, :],
                        S2[0:64, 0:1024], ALU.mult),
                         reads=[PSK[pa], S2.k], writes=[rowbf.k])
                    def shfn(E, pa=pa):
                        inst = None
                        for sh in range(2):
                            inst = E.matmul(
                                PS[pa].ap()[64:128, sh * 512:(sh + 1) * 512],
                                ident_bf[0:64, 0:64],
                                rowbf[0:64, sh * 512:(sh + 1) * 512],
                                start=True, stop=True, tile_position=(0, 64))
                        return inst
                    S.op("tensor", shfn, reads=[ident_bf.k, rowbf.k],
                         writes=[PSK[pa]])
            elif step == 4:
                if b == 0:
                    S.op("vector", lambda E, oc=oc: E.tensor_tensor(
                        y1[oc][0:64, :], y1[oc][0:64, :], S1[0:64, 0:1024],
                        ALU.add), reads=[S1.k, y1[oc].k], writes=[y1[oc].k])
                else:
                    S.op("vector", lambda E, oc=oc, pa=pa: E.tensor_tensor(
                        y1[oc][64:128, :], y1[oc][64:128, :],
                        PS[pa].ap()[64:128, :], ALU.add),
                         reads=[PSK[pa], y1[oc].k], writes=[y1[oc].k])

        PVLAG = 3
        TAIL_SLOTS = {5: 0, 7: 1, 9: 2, 11: 3, 13: 4}
        for p in range(32):
            if p % 2 == 0:
                oc0 = (p // 8) * 4 + (p % 8) // 2
                S.op("vector", lambda E, oc0=oc0: E.tensor_copy(
                    QAb[0:64, :], QT[oc0][0:64, :]),
                     reads=[QT[oc0].k], writes=[QAb.k])
                S.op("vector", lambda E, oc0=oc0: E.tensor_copy(
                    QBb[64:128, :], QT[oc0][64:128, :]),
                     reads=[QT[oc0].k], writes=[QBb.k])
            for tch in range(16):
                emit_qk(p, tch)
                G = p * 16 + tch
                if G >= PVLAG:
                    emit_pv((G - PVLAG) // 16, (G - PVLAG) % 16)
                if p >= 1 and tch in TAIL_SLOTS:
                    emit_tail(p - 1, TAIL_SLOTS[tch])
        for G in range(512 - PVLAG, 512):
            emit_pv(G // 16, G % 16)
        for step in range(5):
            emit_tail(31, step)

        if dump == "attn":
            _emit_dump(S, AR_F, DUMP)

        y1b2 = vb("S3", 1024)
        sqb2 = vb("S3", 1024, off_extra=2048)

        def layernorm(src, dst_fn, tag):
            for hc in range(16):
                yb = y1b if hc % 2 == 0 else y1b2
                sb_ = sqb if hc % 2 == 0 else sqb2
                S.op("vector", lambda E, hc=hc, yb=yb: E.tensor_copy(
                    yb.ap, src[hc].ap), reads=[src[hc].k], writes=[yb.k])
                S.op("vector", lambda E, yb=yb, sb_=sb_: E.tensor_tensor(
                    sb_.ap, yb.ap, yb.ap, ALU.mult),
                     reads=[yb.k], writes=[sb_.k])
                def s12fn(E, hc=hc, yb=yb, sb_=sb_):
                    inst = None
                    for th in range(2):
                        inst = E.matmul(
                            PS[0].ap()[0:1, th * 512:(th + 1) * 512],
                            ones_bf[0:128, 0:1],
                            yb[0:128, th * 512:(th + 1) * 512],
                            start=(hc == 0), stop=(hc == 15))
                        inst = E.matmul(
                            PS[3].ap()[0:1, th * 512:(th + 1) * 512],
                            ones_bf[0:128, 0:1],
                            sb_[0:128, th * 512:(th + 1) * 512],
                            start=(hc == 0), stop=(hc == 15))
                    return inst
                S.op("tensor", s12fn, reads=[yb.k, sb_.k, ones_bf.k],
                     writes=[PSK[0], PSK[3]])
            S.op("vector", lambda E: E.tensor_scalar(
                S4[0:1, 0:1024], PS[0].ap()[0:1, :], 1.0 / 2048.0, None,
                ALU.mult), reads=[PSK[0]], writes=[S4.k])
            S.op("vector", lambda E: E.tensor_tensor(
                S2[0:1, 0:1024], S4[0:1, 0:1024], S4[0:1, 0:1024], ALU.mult),
                 reads=[S4.k], writes=[S2.k])
            S.op("vector", lambda E: E.scalar_tensor_tensor(
                S3[0:1, 0:1024], PS[3].ap()[0:1, :], 1.0 / 2048.0,
                S2[0:1, 0:1024], ALU.mult, ALU.subtract),
                 reads=[PSK[3], S2.k], writes=[S3.k])
            S.op("vector", lambda E: E.tensor_scalar(
                S3[0:1, 0:1024], S3[0:1, 0:1024], 1e-5, None, ALU.add),
                 reads=[S3.k], writes=[S3.k])
            S.op("scalar", lambda E: E.activation(
                S2[0:1, 0:1024], S3[0:1, 0:1024], AF.Ln),
                 reads=[S3.k], writes=[S2.k])
            S.op("scalar", lambda E: E.activation(
                S3[0:1, 0:1024], S2[0:1, 0:1024], AF.Exp, scale=-0.5),
                 reads=[S2.k], writes=[S3.k])
            S.op("vector", lambda E: E.scalar_tensor_tensor(
                S4[0:1, 0:1024], S4[0:1, 0:1024], -1.0, S3[0:1, 0:1024],
                ALU.mult, ALU.mult), reads=[S3.k, S4.k], writes=[S4.k])
            def bcast(row, psi):
                def fn(E):
                    inst = None
                    for th in range(2):
                        inst = E.matmul(
                            PS[psi].ap()[0:128, th * 512:(th + 1) * 512],
                            ones_f[0:1, 0:128],
                            row[0:1, th * 512:(th + 1) * 512],
                            start=True, stop=True)
                    return inst
                return fn
            S.op("tensor", bcast(S3.ap, 1), reads=[ones_f.k, S3.k],
                 writes=[PSK[1]])
            S.op("tensor", bcast(S4.ap, 2), reads=[ones_f.k, S4.k],
                 writes=[PSK[2]])
            S.op("vector", lambda E: E.tensor_copy(S1.ap, PS[1].ap()[:, :]),
                 reads=[PSK[1]], writes=[S1.k])
            S.op("vector", lambda E: E.tensor_copy(S2.ap, PS[2].ap()[:, :]),
                 reads=[PSK[2]], writes=[S2.k])
            for hc in range(16):
                dst_fn(hc)

        QAf = TV(AR_F[:, L["QA"] // 4: L["QA"] // 4 + 1024], _keys(L["QA"], 4096))

        def ln1_dst(hc):
            tmp = S3 if hc % 2 == 0 else QAf
            S.op("vector", lambda E, hc=hc, tmp=tmp: E.tensor_tensor(
                tmp.ap, y1[hc].ap, S1.ap, ALU.mult),
                 reads=[y1[hc].k, S1.k], writes=[tmp.k])
            S.op("vector", lambda E, hc=hc, tmp=tmp: E.tensor_tensor(
                x1T[hc].ap, tmp.ap, S2.ap, ALU.add),
                 reads=[tmp.k, S2.k], writes=[x1T[hc].k])
        layernorm(y1, ln1_dst, "ln1")

        if dump == "ln1":
            _emit_dump(S, AR_F, DUMP)

        def stage(DRC, idx, slots_u, slots_b, nsl, ring):
            sl = idx % nsl
            S.op(ring, lambda E, sl=sl, DRC=DRC: E.dma_start(
                out=slots_u[sl].ap, in_=DRC), writes=[slots_b[sl].k], dma=True)
            S.op("scalar", lambda E, sl=sl: E.activation(
                slots_b[sl].ap, slots_u[sl].ap, AF.Copy, bias=-128.0),
                 reads=[slots_b[sl].k], writes=[slots_b[sl].k])
            return sl

        def stage2(DRC, idx, lands, slots_b, nsl, ring):
            ld = lands[idx % len(lands)]
            sl = idx % nsl
            S.op(ring, lambda E, ld=ld, DRC=DRC: E.dma_start(
                out=ld.ap, in_=DRC), writes=[ld.k], dma=True)
            S.op("scalar", lambda E, sl=sl, ld=ld: E.activation(
                slots_b[sl].ap, ld.ap, AF.Copy, bias=-128.0),
                 reads=[ld.k], writes=[slots_b[sl].k])
            return sl

        def down_block(g, cb):
            hs = g % 2
            dsl = stage2(DCB[cb, g], 8 * g + cb, d8, dstg, NDS, "gpsimd")
            for sub in range(2):
                co = 2 * cb + sub
                pd = 2 + (co % 2)
                def dfn(E, dsl=dsl, hs=hs, pd=pd, sub=sub):
                    inst = None
                    for th in range(2):
                        for il in range(8):
                            inst = E.matmul(
                                PS[pd].ap()[:, th * 512:(th + 1) * 512],
                                dstg[dsl][:, il * 256 + sub * 128:
                                          il * 256 + sub * 128 + 128],
                                hact[hs][:, il * 1024 + th * 512:
                                         il * 1024 + (th + 1) * 512],
                                start=(il == 0), stop=(il == 7))
                    return inst
                S.op("tensor", dfn, reads=[dstg[dsl].k, hact[hs].k],
                     writes=[PSK[pd]])
                if g == 0:
                    S.op("vector", lambda E, co=co, pd=pd: E.tensor_scalar(
                        zz[co].ap, PS[pd].ap()[:, :], ds_sc[:, co:co + 1],
                        None, ALU.mult),
                         reads=[PSK[pd], ds_sc.k], writes=[zz[co].k])
                else:
                    S.op("vector", lambda E, co=co, pd=pd: E.scalar_tensor_tensor(
                        zz[co].ap, PS[pd].ap()[:, :], ds_sc[:, co:co + 1],
                        zz[co].ap, ALU.mult, ALU.add),
                         reads=[PSK[pd], ds_sc.k, zz[co].k], writes=[zz[co].k])

        for g in range(8):
            hs = g % 2
            for ib in range(4):
                idx = 4 * g + ib
                if idx < 2:
                    ssl = stage(GCB[idx], idx, gstg_u, gstg, NGS, "sync")
                    usl = stage(UCB[idx], idx, ustg_u, ustg, NGS, "scalar")
                else:
                    ssl = stage2(GCB[idx], idx, g8, gstg, NGS, "sync")
                    usl = stage2(UCB[idx], idx, u8_, ustg, NGS, "scalar")
                for sub in range(2):
                    il = 2 * ib + sub
                    ic = 8 * g + il
                    def gufn(E, ssl=ssl, usl=usl, sub=sub, w=0):
                        stg = gstg[ssl] if w == 0 else ustg[usl]
                        pi = 0 if w == 0 else 1
                        inst = None
                        for hc in range(16):
                            for th in range(2):
                                inst = E.matmul(
                                    PS[pi].ap()[:, th * 512:(th + 1) * 512],
                                    stg[:, hc * 256 + sub * 128:
                                        hc * 256 + sub * 128 + 128],
                                    x1T[hc][:, th * 512:(th + 1) * 512],
                                    start=(hc == 0), stop=(hc == 15))
                        return inst
                    S.op("tensor", lambda E, ssl=ssl, usl=usl, sub=sub: gufn(
                        E, ssl, usl, sub, 0),
                         reads=[gstg[ssl].k] + [x.k for x in x1T],
                         writes=[PSK[0]])
                    S.op("tensor", lambda E, ssl=ssl, usl=usl, sub=sub: gufn(
                        E, ssl, usl, sub, 1),
                         reads=[ustg[usl].k] + [x.k for x in x1T],
                         writes=[PSK[1]])
                    S.op("scalar", lambda E, ic=ic: E.activation(
                        S3.ap, PS[0].ap()[:, :], AF.Silu,
                        scale=gs_sc[:, ic:ic + 1]),
                         reads=[PSK[0], gs_sc.k], writes=[S3.k])
                    S.op("vector", lambda E, ic=ic, hs=hs, il=il: E.scalar_tensor_tensor(
                        hact[hs][:, il * 1024:(il + 1) * 1024],
                        PS[1].ap()[:, :], us_sc[:, ic:ic + 1], S3.ap,
                        ALU.mult, ALU.mult),
                         reads=[PSK[1], us_sc.k, S3.k], writes=[hact[hs].k])
                if g > 0:
                    down_block(g - 1, 2 * ib)
                    down_block(g - 1, 2 * ib + 1)
        for cb in range(8):
            down_block(7, cb)

        if dump == "ffn":
            _emit_dump(S, AR_F, DUMP)

        st_ops = []

        def ln2_dst(hc):
            outv = YS if hc % 2 == 0 else S4
            tmp = S3 if hc % 2 == 0 else QAf
            S.op("vector", lambda E, hc=hc, tmp=tmp: E.tensor_tensor(
                tmp.ap, zz[hc].ap, S1.ap, ALU.mult),
                 reads=[zz[hc].k, S1.k], writes=[tmp.k])
            S.op("vector", lambda E, hc=hc, outv=outv, tmp=tmp: E.tensor_tensor(
                outv.ap, tmp.ap, S2.ap, ALU.add),
                 reads=[tmp.k, S2.k], writes=[outv.k])
            st_ops.append(S.op("sync", lambda E, hc=hc, outv=outv: E.dma_start(
                out=OUT[hc * 128:(hc + 1) * 128, :], in_=outv.ap),
                reads=[outv.k], dma=True))
        layernorm(zz, ln2_dst, "ln2")

        S.op("sync", lambda E: E.nop(), deps=st_ops)

        block = ctx.enter_context(nc.Block())
        S.emit(block, csem, dsem)
    return nc


def _emit_dump(S, AR_F, DUMP):
    for hc in range(16):
        w = AR_F.shape[1] // 16 if hc < 15 else AR_F.shape[1] - 15 * (AR_F.shape[1] // 16)
        c0 = hc * (AR_F.shape[1] // 16)
        S.op("sync", lambda E, c0=c0, w=w: E.dma_start(
            out=DUMP[:, c0:c0 + w], in_=AR_F[:, c0:c0 + w]),
            reads=[tuple(("A", u) for u in range(0, 120))], dma=True)


def _prep(inp):
    x = np.asarray(inp["L0_attn_x"], np.float32)
    Wq = np.asarray(inp["L0_attn_Wq"], np.float32)
    Wk = np.asarray(inp["L0_attn_Wk"], np.float32)
    Wv = np.asarray(inp["L0_attn_Wv"], np.float32)
    mem = np.ascontiguousarray(np.asarray(inp["L0_attn_memory"], np.float32))
    mnorm = np.ascontiguousarray(
        np.asarray(inp["L0_attn_memory_norm"], np.float32).reshape(512, 1))
    mw = np.asarray(inp["L0_attn_memory_weight"], np.float32).reshape(1, 1)
    gw = np.asarray(inp["L0_ffn_gate_w"], np.int32)
    uw = np.asarray(inp["L0_ffn_up_w"], np.int32)
    dw = np.asarray(inp["L0_ffn_down_w"], np.int32)
    gw8 = np.ascontiguousarray(gw.view(np.uint8).reshape(8192, 2048, 4)[:, :, 0])
    uw8 = np.ascontiguousarray(uw.view(np.uint8).reshape(8192, 2048, 4)[:, :, 0])
    dw8 = np.ascontiguousarray(dw.view(np.uint8).reshape(2048, 8192, 4)[:, :, 0])
    gs = np.asarray(inp["L0_ffn_gate_scale"], np.float32)
    us = np.asarray(inp["L0_ffn_up_scale"], np.float32)
    ds = np.asarray(inp["L0_ffn_down_scale"], np.float32)

    common = dict(
        WQC=np.ascontiguousarray(
            Wq.reshape(16, 128, 16, 128).transpose(0, 3, 2, 1)
            .reshape(16, 128, 2048)),
        WKC=np.ascontiguousarray(
            Wk.reshape(4, 128, 16, 128).transpose(0, 3, 2, 1)
            .reshape(4, 128, 2048)),
        WVC=np.ascontiguousarray(
            Wv.reshape(4, 128, 16, 128).transpose(0, 3, 2, 1)
            .reshape(4, 128, 2048)),
        MEM=mem, MNORM=mnorm, MW=np.ascontiguousarray(mw),
        GCB=np.ascontiguousarray(
            gw8.reshape(32, 256, 16, 128).transpose(0, 3, 2, 1)
            .reshape(32, 128, 4096)),
        UCB=np.ascontiguousarray(
            uw8.reshape(32, 256, 16, 128).transpose(0, 3, 2, 1)
            .reshape(32, 128, 4096)),
        DCB=np.ascontiguousarray(
            dw8.reshape(8, 256, 8, 8, 128).transpose(0, 2, 4, 3, 1)
            .reshape(8, 8, 128, 2048)),
        GS=np.ascontiguousarray(gs.reshape(64, 128)),
        US=np.ascontiguousarray(us.reshape(64, 128)),
        DS=np.ascontiguousarray(ds.reshape(16, 128)),
    )
    in_maps = []
    xTs = {}
    for c in range(8):
        beta, half = c // 2, c % 2
        if beta not in xTs:
            xTs[beta] = np.ascontiguousarray(x[beta].T)
        xT = xTs[beta]
        if half == 0:
            XT = xT
        else:
            XT = np.ascontiguousarray(
                np.concatenate([xT[:, 1024:], xT[:, :1024]], axis=1))
        m = dict(common)
        m["XT"] = XT
        in_maps.append(m)
    return in_maps


def kernel(**inputs):
    in_maps = _prep(inputs)
    nc = build_nc()
    res = run_bass_kernel_spmd(nc, in_maps, core_ids=list(range(8)))
    out = np.empty((4, 2048, 2048), np.float32)
    for c in range(8):
        beta, half = c // 2, c % 2
        out[beta, half * 1024:(half + 1) * 1024, :] = res.results[c]["OUT"].T
    return out
